# version 03
# baseline (speedup 1.0000x reference)
import math
import contextlib
import numpy as np

import concourse.bass as bass
import concourse.mybir as mybir
import concourse.bass_utils as bass_utils

F32 = mybir.dt.float32
BF16 = mybir.dt.bfloat16
OP = mybir.AluOpType
AF = mybir.ActivationFunctionType
AX = mybir.AxisListType

D = 1024
DP = D // 128
NHEAD = 16
DH = 64
CHUNK = 128

CFG = {
    "stages": ("A", "B", "C", "D"),
    "fused": True,
    "n_cores": 8,
    "T": 2048,
    "B_per_core": 2,
    "collective": True,
    "gelu_lut": False,
}

EPS = 1e-5



ENGINES = ("tensor", "vector", "scalar", "gpsimd", "sync")


class Prog:

    def __init__(self):
        self.ops = []
        self.last_w = {}
        self.readers = {}

    def add(self, engine, emit, reads=(), writes=(), dma=False, tag=""):
        assert engine in ENGINES, engine
        idx = len(self.ops)
        deps = set()
        for r in reads:
            w = self.last_w.get(r)
            if w is not None:
                deps.add(w)
        for r in writes:
            w = self.last_w.get(r)
            if w is not None:
                deps.add(w)
            deps.update(self.readers.get(r, ()))
        deps.discard(idx)
        self.ops.append(dict(i=idx, engine=engine, emit=emit, dma=dma,
                             tag=tag, deps=deps, barrier=False))
        for r in reads:
            self.readers.setdefault(r, set()).add(idx)
        for r in writes:
            self.last_w[r] = idx
            self.readers[r] = {idx}
        return idx

    def barrier(self):
        self.ops.append(dict(i=len(self.ops), engine=None, emit=None, dma=False,
                             tag="BARRIER", deps=set(), barrier=True))

    NDMA_SLOTS = 6

    def finalize_into(self, nc, es):
        R_ = Prog.NDMA_SLOTS
        sems = {e: es.enter_context(nc.semaphore(f"s_{e}")) for e in ENGINES}
        dsems = {}
        for e in ENGINES:
            for sl in range(R_):
                dsems[(e, sl)] = es.enter_context(nc.semaphore(f"d_{e}{sl}"))
        dsems[("cc", 0)] = es.enter_context(nc.semaphore("d_cc"))

        pos = {}
        cnt = {e: 0 for e in ENGINES}
        dcnt = {e: 0 for e in ENGINES}
        duse = {(e, sl): 0 for e in ENGINES for sl in range(R_)}
        per_engine = {e: [] for e in ENGINES}
        pending_bar = None
        bar_seen = set()
        for o in self.ops:
            if o["engine"] is None:
                pending_bar = {"c": dict(cnt), "d": dict(duse)}
                bar_seen = set()
                continue
            e = o["engine"]
            if o["dma"] == "cc":
                duse.setdefault(("cc", 0), 0)
                duse[("cc", 0)] += 1
                o["slot"] = ("cc", 0)
                o["slot_use"] = duse[("cc", 0)]
                pos[o["i"]] = ("D", ("cc", 0), duse[("cc", 0)])
            elif o["dma"]:
                sl = dcnt[e] % R_
                duse[(e, sl)] += 1
                dcnt[e] += 1
                o["slot"] = (e, sl)
                o["slot_use"] = duse[(e, sl)]
                pos[o["i"]] = ("D", (e, sl), duse[(e, sl)] * 16)
            else:
                cnt[e] += 1
                pos[o["i"]] = ("C", e, cnt[e])
            o["bar"] = None
            if pending_bar is not None and e not in bar_seen:
                o["bar"] = pending_bar
                bar_seen.add(e)
                if bar_seen == set(ENGINES):
                    pending_bar = None
            per_engine[e].append(o)

        for e in ENGINES:
            hw_c = {}
            hw_d = {}
            for o in per_engine[e]:
                waits = {}
                if o.get("bar"):
                    for e2, c in o["bar"]["c"].items():
                        if c > 0:
                            waits[("C", e2)] = max(waits.get(("C", e2), 0), c)
                    for key, u in o["bar"]["d"].items():
                        if u > 0:
                            step = 1 if key[0] == "cc" else 16
                            waits[("D", key)] = max(waits.get(("D", key), 0), u * step)
                for dep in o["deps"]:
                    kind, who, dv = pos[dep]
                    if kind == "C" and who == e == "tensor" and not o["dma"]:
                        continue
                    waits[(kind, who)] = max(waits.get((kind, who), 0), dv)
                if o["dma"]:
                    key = ("D", o["slot"])
                    step = 1 if o["dma"] == "cc" else 16
                    need = (o["slot_use"] - 1) * step
                    if need > 0:
                        waits[key] = max(waits.get(key, 0), need)
                o["w_list"] = []
                for (kind, who), v in sorted(waits.items(), key=str):
                    hw = hw_d if kind == "D" else hw_c
                    if v > hw.get(who, 0):
                        hw[who] = v
                        o["w_list"].append((kind, who, v))

        def section(eng_obj, e):
            for o in per_engine[e]:
                for (kind, who, v) in o["w_list"]:
                    eng_obj.wait_ge(dsems[who] if kind == "D" else sems[who], v)
                inst = o["emit"](eng_obj)
                if inst is None:
                    raise RuntimeError(f"op {o['i']} ({o['tag']}) returned None")
                if o["dma"] == "cc":
                    inst.then_inc(dsems[o["slot"]])
                elif o["dma"]:
                    inst.then_inc(dsems[o["slot"]], 16)
                else:
                    inst.then_inc(sems[e], 1)

        with nc.Block() as block:
            block.tensor(lambda t: section(t, "tensor"))
            block.vector(lambda v: section(v, "vector"))
            block.scalar(lambda s: section(s, "scalar"))
            block.gpsimd(lambda g: section(g, "gpsimd"))
            block.sync(lambda sy: section(sy, "sync"))
        return cnt, dcnt


def R(name, i=None):
    return name if i is None else f"{name}:{i}"



def decay_gammas():
    return 1.0 - np.exp(np.linspace(math.log(1 / 32), math.log(1 / 512), NHEAD))


def host_constants():
    gam = decay_gammas()
    a = np.arange(CHUNK, dtype=np.float64)[:, None]
    Gq = np.concatenate([(g ** a) * np.ones((1, DH)) for g in gam], axis=1) / math.sqrt(DH)
    Gk = np.concatenate([(g ** (-a)) * np.ones((1, DH)) for g in gam], axis=1)
    causal = (np.arange(CHUNK)[:, None] <= np.arange(CHUNK)[None, :]).astype(np.float32)
    ltri = (np.arange(128)[:, None] <= np.arange(128)[None, :]).astype(np.float32).T
    ltri = (np.arange(128)[:, None][:, :] <= np.arange(128)[None, :]).astype(np.float32)
    return {
        "cGq": Gq.astype(np.float32),
        "cGk": Gk.astype(np.float32),
        "cCausal": causal,
        "cIdent": np.eye(128, dtype=np.float32),
        "cLtri": ltri.astype(np.float32),
        "cOnes": np.ones((128, 128), dtype=np.float32),
        "cGpc": np.concatenate([
            np.concatenate([np.full((64, 1), g2 ** CHUNK), np.full((64, 1), g21 ** CHUNK)])
            for g2, g21 in zip(gam[0::2], gam[1::2])], axis=1).astype(np.float32),
    }


def fm(vec):
    v = np.asarray(vec, dtype=np.float32)
    k = v.shape[0] // 128
    return np.ascontiguousarray(v.reshape(k, 128).T)


def rowv(vec):
    return np.ascontiguousarray(np.asarray(vec, dtype=np.float32)[None, :])


def tr(w):
    return np.ascontiguousarray(np.asarray(w, dtype=np.float32).T)


def stage_inputs(inputs, stages, T, B_per_core, n_cores):
    x = np.ascontiguousarray(np.asarray(inputs["L0_attn_x"], dtype=np.float32))
    B_total, T_in, D_in = x.shape
    assert D_in == D
    shared = dict(host_constants())
    if "A" in stages:
        shared.update({
            "aQwT": tr(inputs["L0_attn_qw"]), "aKwT": tr(inputs["L0_attn_kw"]),
            "aVwT": tr(inputs["L0_attn_vw"]), "aOwT": tr(inputs["L0_attn_ow"]),
            "aW1T": tr(inputs["L0_attn_w1"]), "aW2T": tr(inputs["L0_attn_w2"]),
            "aGn1g": fm(inputs["L0_attn_norm_g"]), "aGn1b": fm(inputs["L0_attn_norm_b"]),
            "aGn2g": fm(inputs["L0_attn_ffn_norm_g"]), "aGn2b": fm(inputs["L0_attn_ffn_norm_b"]),
            "aObR": rowv(inputs["L0_attn_ob"]), "aB2R": rowv(inputs["L0_attn_b2"]),
            "aB1F": fm(inputs["L0_attn_b1"]),
        })
    if "B" in stages:
        shared.update({
            "bGwT": tr(inputs["L0_ffn_gate_w"]),
            "bWgT": np.ascontiguousarray(np.asarray(inputs["L0_ffn_w_gate"], np.float32).transpose(0, 2, 1)),
            "bWuT": np.ascontiguousarray(np.asarray(inputs["L0_ffn_w_up"], np.float32).transpose(0, 2, 1)),
            "bWdT": np.ascontiguousarray(np.asarray(inputs["L0_ffn_w_down"], np.float32).transpose(0, 2, 1)),
        })
    if "C" in stages:
        shared.update({
            "cWqT": tr(inputs["L1_attn_Wq"]), "cWkT": tr(inputs["L1_attn_Wk"]),
            "cWvT": tr(inputs["L1_attn_Wv"]), "cWpT": tr(inputs["L1_attn_Wp"]),
            "cGq_g": fm(inputs["L1_attn_gq"]), "cGq_b": fm(inputs["L1_attn_bq"]),
            "cGk_g": fm(inputs["L1_attn_gk"]), "cGk_b": fm(inputs["L1_attn_bk"]),
            "cGv_g": fm(inputs["L1_attn_gv"]), "cGv_b": fm(inputs["L1_attn_bv"]),
            "cBpR": rowv(inputs["L1_attn_bp"]),
        })
    if "D" in stages:
        shared.update({
            "dU1": np.ascontiguousarray(np.asarray(inputs["L1_ffn_U1"], np.float32)),
            "dV1": np.ascontiguousarray(np.asarray(inputs["L1_ffn_V1"], np.float32)),
            "dU2": np.ascontiguousarray(np.asarray(inputs["L1_ffn_U2"], np.float32)),
            "dV2": np.ascontiguousarray(np.asarray(inputs["L1_ffn_V2"], np.float32)),
            "dB1F": fm(inputs["L1_ffn_b1"]), "dB2R": rowv(inputs["L1_ffn_b2"]),
        })
    in_maps = []
    for c in range(n_cores):
        m = dict(shared)
        xs = x[c * B_per_core:(c + 1) * B_per_core, :T, :]
        m["x"] = np.ascontiguousarray(xs.reshape(B_per_core * T, D))
        in_maps.append(m)
    return in_maps



class Ctx:
    def __init__(self, nc, prog, es, cfg):
        self.nc = nc
        self.prog = prog
        self.es = es
        self.cfg = cfg
        self.T = cfg["T"]
        self.B = cfg["B_per_core"]
        self.NT = self.B * self.T // 128
        self.TPB = self.T // 128
        self.sb = {}
        self.dram = {}

    def inp(self, name, shape):
        t = self.nc.dram_tensor(name, list(shape), F32, kind="ExternalInput")
        self.dram[name] = t
        return t

    def sbuf(self, name, shape, dtype):
        es = getattr(self, "cur_es", None) or self.es
        t = es.enter_context(self.nc.sbuf_tensor(name, list(shape), dtype))
        self.sb[name] = t
        return t

    def psum(self, name, shape, dtype=F32):
        es = getattr(self, "cur_es", None) or self.es
        t = es.enter_context(self.nc.psum_tensor(name, list(shape), dtype))
        self.sb[name] = t
        return t

    def load_weight_fm(self, dst_sb, src_dram, n_ptiles, cols, res, srcsl=None, tag=""):
        src = src_dram.ap() if srcsl is None else srcsl
        src = src.rearrange("(a p) n -> p a n", p=128)
        dst = dst_sb[:, 0:n_ptiles * cols].rearrange("p (a n) -> p a n", a=n_ptiles)
        self.prog.add("gpsimd", lambda g: g.dma_start(out=dst, in_=src),
                      reads=(), writes=[res], dma=True, tag=tag or f"load:{res}")

    def dma_in(self, engine, dst_ap, src_ap, writes, tag):
        self.prog.add(engine, lambda g: g.dma_start(out=dst_ap, in_=src_ap),
                      reads=(), writes=writes, dma=True, tag=tag)

    def dma_out(self, engine, dst_ap, src_ap, reads, writes, tag):
        self.prog.add(engine, lambda g: g.dma_start(out=dst_ap, in_=src_ap),
                      reads=reads, writes=writes, dma=True, tag=tag)


def transpose_tile(ctx, out_sb_ap, in_sb_ap, ptr_ap, ident_ap, reads, writes, pres, tag):
    prog = ctx.prog
    prog.add("tensor", lambda t: t.transpose(ptr_ap, in_sb_ap, ident_ap),
             reads=list(reads) + ["cIdentB"], writes=[pres], tag=tag + ".tr")
    prog.add("vector", lambda v: v.tensor_copy(out_sb_ap, ptr_ap),
             reads=[pres], writes=list(writes), tag=tag + ".ev")



def ln_from_z(ctx, xi, out_ap, out_res, tag):
    prog = ctx.prog
    z = ctx.sb["LNZ"][:, :]
    st = ctx.sb["LNST"]
    mu, ez2, rstd, ssq, nvar = st[:, 0:1], st[:, 1:2], st[:, 2:3], st[:, 3:4], st[:, 4:5]
    prog.add("vector", lambda v: v.reduce_sum(mu, z, axis=AX.X),
             reads=["LNZ"], writes=["LNST"], tag=tag + ".mu")
    prog.add("scalar", lambda s: s.activation(ctx.sb["OUTT"][:, :], z, AF.Square, accum_out=ssq),
             reads=["LNZ", "LNST"], writes=["OUTT", "LNST"], tag=tag + ".sq")
    prog.add("vector", lambda v: v.tensor_scalar(mu, mu, 1.0 / D, None, OP.mult),
             reads=["LNST"], writes=["LNST"], tag=tag + ".m2")
    prog.add("vector", lambda v: v.tensor_scalar(ez2, ssq, 1.0 / D, None, OP.mult),
             reads=["LNST"], writes=["LNST"], tag=tag + ".v1")
    prog.add("vector", lambda v: v.scalar_tensor_tensor(nvar, mu, mu[:, 0:1], ez2, OP.mult, OP.subtract),
             reads=["LNST"], writes=["LNST"], tag=tag + ".v2")
    prog.add("scalar", lambda s: s.activation(rstd, nvar, AF.Sqrt, bias=ctx.sb["CONSTV"][:, 0:1], scale=-1.0),
             reads=["LNST", "CONSTV"], writes=["LNST"], tag=tag + ".rt")
    prog.add("vector", lambda v: v.reciprocal(rstd, rstd),
             reads=["LNST"], writes=["LNST"], tag=tag + ".rc")
    prog.add("vector", lambda v: v.tensor_scalar(out_ap, z, mu, rstd, OP.subtract, OP.mult),
             reads=["LNZ", "LNST"], writes=[out_res], tag=tag + ".o")



def build_stage_D(ctx, is_last):
    nc, prog, cfg = ctx.nc, ctx.prog, ctx.cfg
    es = contextlib.ExitStack()
    ctx.cur_es = es
    with es:
        U1 = ctx.sbuf("dU1s", [128, DP * 256], BF16)
        V1 = ctx.sbuf("dV1s", [128, 2 * 4096], BF16)
        U2 = ctx.sbuf("dU2s", [128, 32 * 256], BF16)
        V2 = ctx.sbuf("dV2s", [128, 2 * 1024], BF16)
        B1F = ctx.sbuf("dB1Fs", [128, 32], F32)
        B2R = ctx.sbuf("dB2Rs", [1, 1024], F32)
        XF = ctx.sbuf("dXF", [128, DP * 512], BF16)
        PT = ctx.sbuf("dPT", [128, 2 * 512], BF16)
        HT = ctx.sbuf("dHT", [128, 32 * 512], BF16)
        QT = ctx.sbuf("dQT", [128, 2 * 512], BF16)
        GS = ctx.sbuf("dGS", [128, 512], F32)
        pTR = ctx.psum("dpTR", [128, 128], BF16)
        pP = [ctx.psum(f"dpP{i}", [128, 512]) for i in range(2)]
        pH = [ctx.psum(f"dpH{i}", [128, 512]) for i in range(2)]
        pO = ctx.psum("dpO", [128, 1024])

        ctx.load_weight_fm(U1, ctx.dram["dU1"], DP, 256, "dU1s")
        ctx.load_weight_fm(V1, ctx.dram["dV1"], 2, 4096, "dV1s")
        ctx.load_weight_fm(U2, ctx.dram["dU2"], 32, 256, "dU2s")
        ctx.load_weight_fm(V2, ctx.dram["dV2"], 2, 1024, "dV2s")
        ctx.dma_in("sync", B1F[:, :], ctx.dram["dB1F"].ap(), ["dB1Fs"], "dB1F")
        ctx.dma_in("sync", B2R[:, :], ctx.dram["dB2R"].ap(), ["dB2Rs"], "dB2R")

        XB = ctx.sb["XB"]
        NS = ctx.TPB // 4
        for b in range(ctx.B):
          load_xb(ctx, b, f"D.{b}")
          for s in range(NS):
            for j in range(4):
                i_loc = s * 4 + j
                xi = b * ctx.TPB + i_loc
                for a in range(DP):
                    transpose_tile(
                        ctx, XF[:, a * 512 + j * 128: a * 512 + (j + 1) * 128],
                        XB[:, i_loc * D + a * 128: i_loc * D + (a + 1) * 128],
                        pTR[:, :], ctx.sb["cIdentB"][:, :],
                        reads=[R("XB", i_loc)], writes=[R("dXF", a)], pres="dpTR",
                        tag=f"D.xf{xi}.{a}")
            for rc in range(2):
                def em(t, rc=rc):
                    for a in range(DP):
                        last = t.matmul(pP[rc][:, :],
                                        U1[:, a * 256 + rc * 128: a * 256 + (rc + 1) * 128],
                                        XF[:, a * 512:(a + 1) * 512],
                                        start=(a == 0), stop=(a == DP - 1))
                    return last
                prog.add("tensor", em,
                         reads=["dU1s"] + [R("dXF", a) for a in range(DP)],
                         writes=[f"dpP{rc}"], tag=f"D.P{s}.{rc}")
                prog.add("scalar", lambda sc, rc=rc: sc.copy(PT[:, rc * 512:(rc + 1) * 512], pP[rc][:, :]),
                         reads=[f"dpP{rc}"], writes=[R("dPT", rc)], tag=f"D.Pe{s}.{rc}")
            for f in range(32):
                pb = pH[f % 2]
                def em(t, f=f, pb=pb):
                    for a in range(2):
                        last = t.matmul(pb[:, :],
                                        V1[:, a * 4096 + f * 128:a * 4096 + (f + 1) * 128],
                                        PT[:, a * 512:(a + 1) * 512],
                                        start=(a == 0), stop=(a == 1))
                    return last
                prog.add("tensor", em, reads=["dV1s", R("dPT", 0), R("dPT", 1)],
                         writes=[f"dpH{f % 2}"], tag=f"D.H{s}.{f}")
                hslice = HT[:, f * 512:(f + 1) * 512]
                bcol = B1F[:, f:f + 1]
                if cfg["gelu_lut"]:
                    prog.add("scalar", lambda sc, pb=pb, hs=hslice, bc=bcol:
                             sc.activation(hs, pb[:, :], AF.Gelu_apprx_tanh, bias=bc, scale=1.0),
                             reads=[f"dpH{f % 2}", "dB1Fs"], writes=[R("dHT", f)],
                             tag=f"D.g{s}.{f}")
                else:
                    prog.add("vector", lambda v, pb=pb, bc=bcol:
                             v.tensor_scalar(GS[:, :], pb[:, :], bc, None, OP.add),
                             reads=[f"dpH{f % 2}", "dB1Fs"], writes=["dGS"], tag=f"D.g0.{s}.{f}")
                    prog.add("scalar", lambda sc, pb=pb:
                             sc.activation(pb[:, :], GS[:, :], AF.Square),
                             reads=["dGS"], writes=[f"dpH{f % 2}"], tag=f"D.g1.{s}.{f}")
                    t2 = ctx.sb["OUTT"][:, 0:512]
                    prog.add("vector", lambda v, pb=pb:
                             v.tensor_scalar(t2, pb[:, :], 0.07135481627159174, 1.5957691216057308, OP.mult, OP.add),
                             reads=[f"dpH{f % 2}"], writes=["OUTT"], tag=f"D.g2.{s}.{f}")
                    prog.add("vector", lambda v:
                             v.tensor_mul(t2, t2, GS[:, :]),
                             reads=["OUTT", "dGS"], writes=["OUTT"], tag=f"D.g3.{s}.{f}")
                    prog.add("scalar", lambda sc:
                             sc.activation(t2, t2, AF.Sigmoid),
                             reads=["OUTT"], writes=["OUTT"], tag=f"D.g4.{s}.{f}")
                    prog.add("vector", lambda v, hs=hslice:
                             v.tensor_mul(hs, t2, GS[:, :]),
                             reads=["OUTT", "dGS"], writes=[R("dHT", f)], tag=f"D.g5.{s}.{f}")
            for rc in range(2):
                def em(t, rc=rc):
                    for a in range(32):
                        last = t.matmul(pP[rc][:, :],
                                        U2[:, a * 256 + rc * 128:a * 256 + (rc + 1) * 128],
                                        HT[:, a * 512:(a + 1) * 512],
                                        start=(a == 0), stop=(a == 31))
                    return last
                prog.add("tensor", em,
                         reads=["dU2s"] + [R("dHT", a) for a in range(32)],
                         writes=[f"dpP{rc}"], tag=f"D.Q{s}.{rc}")
                prog.add("scalar", lambda sc, rc=rc: sc.copy(QT[:, rc * 512:(rc + 1) * 512], pP[rc][:, :]),
                         reads=[f"dpP{rc}"], writes=[R("dQT", rc)], tag=f"D.Qe{s}.{rc}")
            for j in range(4):
                i_loc = s * 4 + j
                xi = b * ctx.TPB + i_loc
                def em(t, j=j):
                    for half in range(2):
                        for rc in range(2):
                            t.matmul(pO[:, half * 512:(half + 1) * 512],
                                     QT[:, rc * 512 + j * 128: rc * 512 + (j + 1) * 128],
                                     V2[:, rc * 1024 + half * 512: rc * 1024 + (half + 1) * 512],
                                     start=(rc == 0), stop=False)
                        last = t.matmul(pO[:, half * 512:(half + 1) * 512],
                                        ctx.sb["cOnesF"][0:1, 0:128],
                                        B2R[0:1, half * 512:(half + 1) * 512],
                                        start=False, stop=True)
                    return last
                prog.add("tensor", em,
                         reads=[R("dQT", 0), R("dQT", 1), "dV2s", "dB2Rs", "cOnesF"],
                         writes=["dpO"], tag=f"D.o{xi}")
                prog.add("vector", lambda v, xi=xi, i_loc=i_loc: v.tensor_add(
                    ctx.sb["LNZ"][:, :], pO[:, :], XB[:, i_loc * D:(i_loc + 1) * D]),
                    reads=["dpO", R("XB", i_loc)], writes=["LNZ"], tag=f"D.z{xi}")
                emit_ln_out(ctx, xi, is_last, f"D.ln{xi}")


    ctx.cur_es = None

def emit_ln_out(ctx, xi, is_last, tag):
    i = xi % ctx.TPB
    if is_last:
        o = ctx.sb["OUTT"][:, :]
        ln_from_z(ctx, xi, o, "OUTT", tag)
        dst = ctx.dram["xout"].ap()[xi * 128:(xi + 1) * 128, :]
        ctx.dma_out("sync", dst, o, reads=["OUTT"], writes=[R("xout", xi)], tag=tag + ".st")
    else:
        XB = ctx.sb["XB"]
        ln_from_z(ctx, xi, XB[:, i * D:(i + 1) * D], R("XB", i), tag)
        dst = ctx.dram["xres"].ap()[xi * 128:(xi + 1) * 128, :]
        ctx.dma_out("sync", dst, XB[:, i * D:(i + 1) * D], reads=[R("XB", i)],
                    writes=[R("xres", xi)], tag=tag + ".st")
        ctx.xsrc[xi] = "xres"


def load_xb(ctx, b, tag):
    XB = ctx.sb["XB"]
    for i in range(ctx.TPB):
        xi = b * ctx.TPB + i
        if ctx.xsrc[xi] == "x":
            srcap = ctx.dram["x"].ap()[xi * 128:(xi + 1) * 128, :]
            rr = []
        else:
            srcap = ctx.dram["xres"].ap()[xi * 128:(xi + 1) * 128, :]
            rr = [R("xres", xi)]
        ctx.prog.add("gpsimd", lambda g, d=XB[:, i * D:(i + 1) * D], s=srcap: g.dma_start(out=d, in_=s),
                     reads=rr, writes=[R("XB", i)], dma=True, tag=f"{tag}.lx{i}")



def build_program(cfg):
    nc = bass.Bass("TRN2", target_bir_lowering=False)
    es = contextlib.ExitStack()
    prog = Prog()
    ctx = Ctx(nc, prog, es, cfg)
    stages = cfg["stages"]
    NTOK = cfg["B_per_core"] * cfg["T"]

    with es:
        xin = ctx.inp("x", [NTOK, D])
        ctx.dram["xout"] = nc.dram_tensor("xout", [NTOK, D], F32, kind="ExternalOutput")
        for nm, shape in [("cGq", [128, D]), ("cGk", [128, D]), ("cCausal", [128, 128]),
                          ("cIdent", [128, 128]), ("cLtri", [128, 128]), ("cOnes", [128, 128]),
                          ("cGpc", [128, 8])]:
            ctx.inp(nm, shape)
        wshapes = {
            "A": [("aQwT", [D, D]), ("aKwT", [D, D]), ("aVwT", [D, D]), ("aOwT", [D, D]),
                  ("aW1T", [D, 2 * D]), ("aW2T", [2 * D, D]),
                  ("aGn1g", [128, DP]), ("aGn1b", [128, DP]), ("aGn2g", [128, DP]), ("aGn2b", [128, DP]),
                  ("aObR", [1, D]), ("aB2R", [1, D]), ("aB1F", [128, 16])],
            "B": [("bGwT", [D, 8]), ("bWgT", [8, D, 4096]), ("bWuT", [8, D, 4096]),
                  ("bWdT", [8, 4096, D])],
            "C": [("cWqT", [D, D]), ("cWkT", [D, D]), ("cWvT", [D, D]), ("cWpT", [D, D]),
                  ("cGq_g", [128, DP]), ("cGq_b", [128, DP]), ("cGk_g", [128, DP]),
                  ("cGk_b", [128, DP]), ("cGv_g", [128, DP]), ("cGv_b", [128, DP]),
                  ("cBpR", [1, D])],
            "D": [("dU1", [D, 256]), ("dV1", [256, 4096]), ("dU2", [4096, 256]),
                  ("dV2", [256, D]), ("dB1F", [128, 32]), ("dB2R", [1, D])],
        }
        for st in stages:
            for nm, shape in wshapes[st]:
                ctx.inp(nm, shape)

        XB = ctx.sbuf("XB", [128, ctx.TPB * D], BF16)
        ctx.dram["xres"] = nc.dram_tensor("xres", [NTOK, D], BF16)
        ctx.xsrc = ["x"] * ctx.NT
        ctx.sbuf("LNZ", [128, D], F32)
        ctx.sbuf("LNST", [128, 8], F32)
        ctx.sbuf("OUTT", [128, D], F32)
        identb = ctx.sbuf("cIdentB", [128, 128], BF16)
        constv = ctx.sbuf("CONSTV", [128, 4], F32)
        prog.add("vector", lambda v: v.memset(constv[:, 0:1], float(EPS)), writes=["CONSTV"], tag="c_eps")
        onesf = ctx.sbuf("cOnesF", [128, 128], F32)
        onescb = ctx.sbuf("cOnesCB", [128, 1], BF16)

        ctx.dma_in("gpsimd", identb[:, :], ctx.dram["cIdent"].ap(), ["cIdentB"], "ldid")
        ctx.dma_in("sync", onesf[:, :], ctx.dram["cOnes"].ap(), ["cOnesF"], "ldon")
        prog.add("vector", lambda v: v.memset(onescb[:, :], 1.0), writes=["cOnesCB"], tag="onescb")

        for si, st in enumerate(stages):
            is_last = (si == len(stages) - 1)
            if st == "A":
                build_stage_A(ctx, is_last)
            elif st == "B":
                build_stage_B(ctx, is_last)
            elif st == "C":
                build_stage_C(ctx, is_last)
            elif st == "D":
                build_stage_D(ctx, is_last)
            prog.barrier()

        prog.add("sync", lambda sy: sy.nop(), reads=[R("xout", i) for i in range(ctx.NT)],
                 tag="final")
        cnt, dcnt = prog.finalize_into(nc, es)
    return nc


def groupnorm_coefs(ctx, src_name, nt0, g_sb, b_sb, af_ap, bf_ap, af_res, bf_res,
                    psU, psQ, psG, psU_res, psQ_res, sus, sqs, grs, gsm, sqscr, onesc_bf, tag):
    prog, T = ctx.prog, ctx.T
    src = ctx.sb[src_name]
    ntiles = ctx.TPB
    for i in range(ntiles):
        xi = nt0 + i
        def emS(t, i=i, xi=xi):
            for half in range(2):
                last = t.matmul(psU[0:1, half * 512:(half + 1) * 512], onesc_bf[:, 0:1],
                                src[:, xi * D + half * 512: xi * D + (half + 1) * 512],
                                start=(i == 0), stop=(i == ntiles - 1))
            return last
        prog.add("tensor", emS, reads=[R(src_name, xi), "cOnesCB"], writes=[psU_res],
                 tag=f"{tag}.s{i}")
        prog.add("scalar", lambda s, xi=xi: s.activation(
            sqscr[:, :], src[:, xi * D:(xi + 1) * D], AF.Square),
            reads=[R(src_name, xi)], writes=["GSQs"], tag=f"{tag}.q{i}")
        def emQ(t, i=i):
            for half in range(2):
                last = t.matmul(psQ[0:1, half * 512:(half + 1) * 512], ctx.sb["cOnesF"][:, 0:1],
                                sqscr[:, half * 512:(half + 1) * 512],
                                start=(i == 0), stop=(i == ntiles - 1))
            return last
        prog.add("tensor", emQ, reads=["GSQs", "cOnesF"], writes=[psQ_res],
                 tag=f"{tag}.qs{i}")
    grrow = ctx.sb["GRROW"]
    prog.add("vector", lambda v: v.tensor_copy(grrow[0:1, :], psU[0:1, :]),
             reads=[psU_res], writes=["GRROW"], tag=f"{tag}.cU")
    prog.add("vector", lambda v: v.reduce_sum(
        grs[0:1, 0:16], grrow[0:1, :].rearrange("p (a hi c) -> p hi a c", hi=2, c=64),
        axis=AX.X), reads=["GRROW"], writes=["GRS"], tag=f"{tag}.rU")
    prog.add("vector", lambda v: v.tensor_copy(grrow[0:1, :], psQ[0:1, :]),
             reads=[psQ_res], writes=["GRROW"], tag=f"{tag}.cQ")
    prog.add("vector", lambda v: v.reduce_sum(
        grs[0:1, 16:32], grrow[0:1, :].rearrange("p (a hi c) -> p hi a c", hi=2, c=64),
        axis=AX.X), reads=["GRROW"], writes=["GRS"], tag=f"{tag}.rQ")
    ONES = ctx.sb["cOnesF"]
    kk = 1.0 / (64.0 * T)
    prog.add("vector", lambda v: v.tensor_scalar(gsm[0:1, 0:16], grs[0:1, 0:16], kk, None, OP.mult),
             reads=["GRS"], writes=["GSM"], tag=f"{tag}.k1")
    prog.add("vector", lambda v: v.tensor_scalar(gsm[0:1, 16:32], grs[0:1, 16:32], kk, None, OP.mult),
             reads=["GRS"], writes=["GSM"], tag=f"{tag}.k2")
    prog.add("vector", lambda v: v.tensor_mul(gsm[0:1, 32:48], gsm[0:1, 0:16], gsm[0:1, 0:16]),
             reads=["GSM"], writes=["GSM"], tag=f"{tag}.m2")
    prog.add("vector", lambda v: v.tensor_sub(gsm[0:1, 48:64], gsm[0:1, 16:32], gsm[0:1, 32:48]),
             reads=["GSM"], writes=["GSM"], tag=f"{tag}.var")
    prog.add("scalar", lambda s: s.activation(gsm[0:1, 32:48], gsm[0:1, 48:64], AF.Sqrt,
                                              bias=ctx.sb["CONSTV"][0:1, 0:1]),
             reads=["GSM", "CONSTV"], writes=["GSM"], tag=f"{tag}.sq")
    prog.add("vector", lambda v: v.reciprocal(gsm[0:1, 32:48], gsm[0:1, 32:48]),
             reads=["GSM"], writes=["GSM"], tag=f"{tag}.rc")
    def emB(t):
        t.matmul(psU[0:64, 0:8], ONES[0:1, 0:64], gsm[0:1, 32:40], start=True, stop=True)
        t.matmul(psU[64:128, 0:8], ONES[0:1, 0:64], gsm[0:1, 40:48], start=True, stop=True)
        t.matmul(psQ[0:64, 0:8], ONES[0:1, 0:64], gsm[0:1, 0:8], start=True, stop=True)
        return t.matmul(psQ[64:128, 0:8], ONES[0:1, 0:64], gsm[0:1, 8:16], start=True, stop=True)
    prog.add("tensor", emB, reads=["GSM", "cOnesF"], writes=[psU_res, psQ_res], tag=f"{tag}.bc")
    prog.add("vector", lambda v: v.tensor_mul(af_ap, psU[:, 0:8], g_sb[:, :]),
             reads=[psU_res, f"{tag}.gsb"], writes=[af_res], tag=f"{tag}.af")
    prog.add("vector", lambda v: v.tensor_mul(ctx.sb["GTMP"][:, 0:8], psQ[:, 0:8], af_ap),
             reads=[psQ_res, af_res], writes=["GTMP"], tag=f"{tag}.mb")
    prog.add("vector", lambda v: v.tensor_sub(bf_ap, b_sb[:, :], ctx.sb["GTMP"][:, 0:8]),
             reads=["GTMP", f"{tag}.bsb"], writes=[bf_res], tag=f"{tag}.bf")


def norm_transpose_tile(ctx, src_name, xi, dst_ap_fn, dst_res, af_ap, af_res, bf_ap, bf_res,
                        ptr, ptr_res, tag):
    prog = ctx.prog
    src = ctx.sb[src_name]
    for a in range(DP):
        prog.add("tensor", lambda t, a=a, xi=xi: t.transpose(
            ptr[:, :], src[:, xi * D + a * 128: xi * D + (a + 1) * 128], ctx.sb["cIdentB"][:, :]),
            reads=[R(src_name, xi), "cIdentB"], writes=[ptr_res], tag=f"{tag}.t{a}")
        prog.add("vector", lambda v, a=a: v.tensor_scalar(
            dst_ap_fn(a), ptr[:, :], af_ap[:, a:a + 1], bf_ap[:, a:a + 1], OP.mult, OP.add),
            reads=[ptr_res, af_res, bf_res], writes=[dst_res], tag=f"{tag}.n{a}")


def build_stage_A(ctx, is_last):
    nc, prog, cfg = ctx.nc, ctx.prog, ctx.cfg
    es = contextlib.ExitStack()
    TPB = ctx.TPB
    ctx.cur_es = es
    with es:
        WB = ctx.sbuf("aWB", [128, 4 * 8192], BF16)
        XMB = ctx.sbuf("aXMB", [128, TPB * D], BF16)
        GQ = ctx.sbuf("aGQ", [128, D], BF16)
        GK = ctx.sbuf("aGK", [128, D], BF16)
        GPC = ctx.sbuf("aGPC", [128, 8], F32)
        CAUS = ctx.sbuf("aCAUS", [128, 128], F32)
        G1 = ctx.sbuf("aG1", [128, DP], F32); B1 = ctx.sbuf("aB1", [128, DP], F32)
        G2 = ctx.sbuf("aG2", [128, DP], F32); B2 = ctx.sbuf("aB2", [128, DP], F32)
        AF1 = ctx.sbuf("aAF1", [128, DP], F32); BF1 = ctx.sbuf("aBF1", [128, DP], F32)
        AF2 = ctx.sbuf("aAF2", [128, DP], F32); BF2 = ctx.sbuf("aBF2", [128, DP], F32)
        GTMP = ctx.sbuf("GTMP", [128, 8], F32)
        OBR = ctx.sbuf("aOBR", [1, D], F32)
        B2R = ctx.sbuf("aB2R2", [1, D], F32)
        B1F = ctx.sbuf("aB1F2", [128, 16], F32)
        XNF = ctx.sbuf("aXNF", [128, D], BF16)
        QTK = ctx.sbuf("aQTK", [128, D], BF16)
        KTK = ctx.sbuf("aKTK", [128, D], BF16)
        VTK = ctx.sbuf("aVTK", [128, D], BF16)
        QF = ctx.sbuf("aQF", [128, D], BF16)
        KF = ctx.sbuf("aKF", [128, D], BF16)
        YF = ctx.sbuf("aYF", [128, D], BF16)
        SIMB = [ctx.sbuf(f"aSIMB{i}", [128, 128], BF16) for i in range(2)]
        SST = ctx.sbuf("aSST", [128, 8 * DH], BF16)
        RL = ctx.sbuf("aRL", [128, 8], F32)
        GSQs = ctx.sbuf("GSQs", [128, D], F32)
        GSUS = ctx.sbuf("GSUS", [128, 8], F32)
        GSQS = ctx.sbuf("GSQS", [128, 8], F32)
        GRS = ctx.sbuf("GRS", [1, 64], F32)
        ctx.sbuf("GRROW", [1, 1024], F32)
        GSM = ctx.sbuf("GSM", [1, 64], F32)
        XN2F = ctx.sbuf("aXN2F", [128, DP * 256], BF16)
        H1F = ctx.sbuf("aH1F", [128, 16 * 256], BF16)
        pTR = ctx.psum("apTR", [128, 128], BF16)
        pQ = ctx.psum("apQ", [128, D])
        pK = ctx.psum("apK", [128, D])
        pV = ctx.psum("apV", [128, D])
        PSM = ctx.psum("aPSM", [128, 512])
        pSIM = [pQ[:, 0:128], pK[:, 0:128]]
        pYH = pV[:, 0:128]
        pM = PSM[:, 0:64]
        pG = PSM[0:1, 64:96]
        pH1 = [pQ[:, 0:256], pK[:, 0:256]]

        def load_qkvo():
            for r, nm in enumerate(["aQwT", "aKwT", "aVwT", "aOwT"]):
                src = ctx.dram[nm].ap().rearrange("(a p) n -> p a n", p=128)
                dst = WB[:, r * 8192:(r + 1) * 8192].rearrange("p (a n) -> p a n", a=DP)
                prog.add("gpsimd", lambda g, dst=dst, src=src: g.dma_start(out=dst, in_=src),
                         writes=[f"WBr{r}"], dma=True, tag=f"A.ld{nm}")

        def load_w12():
            src = ctx.dram["aW1T"].ap().rearrange("(a p) n -> p a n", p=128)
            dst = WB[:, 0:16384].rearrange("p (a n) -> p a n", a=DP)
            prog.add("gpsimd", lambda g: g.dma_start(out=dst, in_=src),
                     writes=["WBr0", "WBr1"], dma=True, tag="A.ldw1")
            src2 = ctx.dram["aW2T"].ap().rearrange("(a p) n -> p a n", p=128)
            dst2 = WB[:, 16384:32768].rearrange("p (a n) -> p a n", a=16)
            prog.add("gpsimd", lambda g: g.dma_start(out=dst2, in_=src2),
                     writes=["WBr2", "WBr3"], dma=True, tag="A.ldw2")

        QWT = lambda: WB[:, 0:8192]
        KWT = lambda: WB[:, 8192:16384]
        VWT = lambda: WB[:, 16384:24576]
        OWT = lambda: WB[:, 24576:32768]
        W1T = lambda: WB[:, 0:16384]
        W2T = lambda: WB[:, 16384:32768]

        for nm, dst, res in [("cGq", GQ, "aGQ"), ("cGk", GK, "aGK")]:
            ctx.dma_in("gpsimd", dst[:, :], ctx.dram[nm].ap(), [res], f"A.{nm}")
        ctx.dma_in("sync", CAUS[:, :], ctx.dram["cCausal"].ap(), ["aCAUS"], "A.caus")
        ctx.dma_in("sync", GPC[:, :], ctx.dram["cGpc"].ap(), ["aGPC"], "A.gpc")
        for nm, dst, res in [("aGn1g", G1, "A.gn1.gsb"), ("aGn1b", B1, "A.gn1.bsb"),
                             ("aGn2g", G2, "A.gn2.gsb"), ("aGn2b", B2, "A.gn2.bsb"),
                             ("aObR", OBR, "aOBR"), ("aB2R", B2R, "aB2R2"), ("aB1F", B1F, "aB1F2")]:
            ctx.dma_in("sync", dst[:, :], ctx.dram[nm].ap(), [res], f"A.{nm}")

        XB = ctx.sb["XB"]

        for b in range(ctx.B):
            nt0 = b * TPB
            load_xb(ctx, b, f"A.{b}")
            load_qkvo()
            groupnorm_coefs(ctx, "XB", 0, G1, B1, AF1[:, :], BF1[:, :], "aAF1", "aBF1",
                            pQ, pK, pG, "apQ", "apK", GSUS, GSQS, GRS, GSM, GSQs,
                            ctx.sb["cOnesCB"], "A.gn1")
            prog.add("vector", lambda v: v.memset(SST[:, :], 0.0), writes=["aSST"], tag="A.sst0")
            for i in range(TPB):
                xi = nt0 + i
                norm_transpose_tile(ctx, "XB", i, lambda a: XNF[:, a * 128:(a + 1) * 128],
                                    "aXNF", AF1, "aAF1", BF1, "aBF1", pTR, "apTR", f"A.xn{xi}")
                for nm, ps, psres, wfn, wres in [("q", pQ, "apQ", QWT, "WBr0"),
                                                 ("k", pK, "apK", KWT, "WBr1"),
                                                 ("v", pV, "apV", VWT, "WBr2")]:
                    def emP(t, ps=ps, wfn=wfn):
                        for half in range(2):
                            for a in range(DP):
                                last = t.matmul(ps[:, half * 512:(half + 1) * 512],
                                                XNF[:, a * 128:(a + 1) * 128],
                                                wfn()[:, a * 1024 + half * 512: a * 1024 + (half + 1) * 512],
                                                start=(a == 0), stop=(a == DP - 1))
                        return last
                    prog.add("tensor", emP, reads=["aXNF", wres], writes=[psres],
                             tag=f"A.{nm}p{xi}")
                for nm, ps, psres, col, gg, ggres, dst, dres in [
                        ("q", pQ, "apQ", 0, GQ, "aGQ", QTK, "aQTK"),
                        ("k", pK, "apK", 3, GK, "aGK", KTK, "aKTK")]:
                    prog.add("scalar", lambda s, ps=ps, col=col: s.activation(
                        GSQs[:, :], ps[:, :], AF.Square, accum_out=RL[:, col:col + 1]),
                        reads=[psres], writes=["GSQs", "aRL"], tag=f"A.n2{nm}{xi}")
                    prog.add("vector", lambda v, col=col: v.tensor_scalar_max(
                        RL[:, col + 1:col + 2], RL[:, col:col + 1], 1e-24),
                        reads=["aRL"], writes=["aRL"], tag=f"A.mx{nm}{xi}")
                    prog.add("scalar", lambda s, col=col: s.activation(
                        RL[:, col + 1:col + 2], RL[:, col + 1:col + 2], AF.Sqrt),
                        reads=["aRL"], writes=["aRL"], tag=f"A.sq{nm}{xi}")
                    prog.add("vector", lambda v, col=col: v.reciprocal(
                        RL[:, col + 2:col + 3], RL[:, col + 1:col + 2]),
                        reads=["aRL"], writes=["aRL"], tag=f"A.rc{nm}{xi}")
                    prog.add("vector", lambda v, ps=ps, col=col, gg=gg, dst=dst: v.scalar_tensor_tensor(
                        dst[:, :], ps[:, :], RL[:, col + 2:col + 3], gg[:, :], OP.mult, OP.mult),
                        reads=[psres, "aRL", ggres], writes=[dres], tag=f"A.sc{nm}{xi}")
                prog.add("scalar", lambda s: s.copy(VTK[:, :], pV[:, :]),
                         reads=["apV"], writes=["aVTK"], tag=f"A.ve{xi}")
                for srcn, sres, dstn, dres in [("aQTK", "aQTK", QF, "aQF"), ("aKTK", "aKTK", KF, "aKF")]:
                    srcb = ctx.sb[srcn]
                    for a in range(DP):
                        prog.add("tensor", lambda t, a=a, srcb=srcb: t.transpose(
                            pTR[:, :], srcb[:, a * 128:(a + 1) * 128], ctx.sb["cIdentB"][:, :]),
                            reads=[sres, "cIdentB"], writes=["apTR"], tag=f"A.tf{srcn}{xi}{a}")
                        prog.add("vector", lambda v, a=a, dstn=dstn: v.tensor_copy(
                            dstn[:, a * 128:(a + 1) * 128], pTR[:, :]),
                            reads=["apTR"], writes=[dres], tag=f"A.te{srcn}{xi}{a}")
                for h in range(NHEAD):
                    j = h // 2
                    ro = (h % 2) * 64
                    sim = pSIM[h % 2]
                    simres = "apQ" if h % 2 == 0 else "apK"
                    prog.add("tensor", lambda t, sim=sim, h=h, j=j, ro=ro: t.matmul(
                        sim[:, :], KF[ro:ro + 64, j * 128:(j + 1) * 128],
                        QF[ro:ro + 64, j * 128:(j + 1) * 128], start=True, stop=True),
                        reads=["aQF", "aKF"], writes=[simres], tag=f"A.sim{xi}.{h}")
                    sb_ = SIMB[h % 2]
                    prog.add("vector", lambda v, sim=sim, sb_=sb_: v.tensor_mul(
                        sb_[:, :], sim[:, :], CAUS[:, :]),
                        reads=[simres, "aCAUS"], writes=[f"aSIMB{h % 2}"], tag=f"A.sm{xi}.{h}")
                    def emY(t, h=h, j=j, ro=ro, sb_=sb_):
                        t.matmul(pYH[ro:ro + 64, :], VTK[:, h * 64:(h + 1) * 64], sb_[:, :],
                                 start=True, stop=False)
                        return t.matmul(pYH[ro:ro + 64, :],
                                        SST[ro:ro + 64, j * 64:(j + 1) * 64],
                                        QF[ro:ro + 64, j * 128:(j + 1) * 128],
                                        start=False, stop=True)
                    prog.add("tensor", emY, reads=["aVTK", f"aSIMB{h % 2}", "aSST", "aQF"],
                             writes=["apV"], tag=f"A.yh{xi}.{h}")
                    prog.add("tensor", lambda t, h=h, ro=ro: t.matmul(
                        pM[ro:ro + 64, :], KTK[:, h * 64:(h + 1) * 64],
                        VTK[:, h * 64:(h + 1) * 64], start=True, stop=True),
                        reads=["aKTK", "aVTK"], writes=["aPSM"], tag=f"A.m{xi}.{h}")
                    if h % 2 == 1:
                        prog.add("scalar", lambda s, j=j: s.copy(
                            YF[:, j * 128:(j + 1) * 128], pYH[:, :]),
                            reads=["apV"], writes=["aYF"], tag=f"A.ye{xi}.{j}")
                        prog.add("vector", lambda v, j=j: v.tensor_add(
                            SST[:, j * 64:(j + 1) * 64], SST[:, j * 64:(j + 1) * 64], pM[:, :]),
                            reads=["aPSM", "aSST"], writes=["aSST"], tag=f"A.su{xi}.{j}")
                        prog.add("vector", lambda v, j=j: v.tensor_scalar(
                            SST[:, j * 64:(j + 1) * 64], SST[:, j * 64:(j + 1) * 64],
                            GPC[:, j:j + 1], None, OP.mult),
                            reads=["aSST", "aGPC"], writes=["aSST"], tag=f"A.sg{xi}.{j}")
                def emO(t):
                    for half in range(2):
                        for a in range(DP):
                            t.matmul(pQ[:, half * 512:(half + 1) * 512],
                                     YF[:, a * 128:(a + 1) * 128],
                                     OWT()[:, a * 1024 + half * 512: a * 1024 + (half + 1) * 512],
                                     start=(a == 0), stop=False)
                        last = t.matmul(pQ[:, half * 512:(half + 1) * 512],
                                        ctx.sb["cOnesF"][0:1, 0:128],
                                        OBR[0:1, half * 512:(half + 1) * 512],
                                        start=False, stop=True)
                    return last
                prog.add("tensor", emO, reads=["aYF", "WBr3", "aOBR", "cOnesF"],
                         writes=["apQ"], tag=f"A.op{xi}")
                prog.add("vector", lambda v, i=i, xi=xi: v.tensor_add(
                    XMB[:, i * D:(i + 1) * D], pQ[:, :], XB[:, i * D:(i + 1) * D]),
                    reads=["apQ", R("XB", i)], writes=[R("aXMB", i)], tag=f"A.xm{xi}")
            load_w12()
            groupnorm_coefs(ctx, "aXMB", 0, G2, B2, AF2[:, :], BF2[:, :], "aAF2", "aBF2",
                            pQ, pK, pG, "apQ", "apK", GSUS, GSQS, GRS, GSM, GSQs,
                            ctx.sb["cOnesCB"], "A.gn2")
            for sl in range(TPB // 2):
                for jj in range(2):
                    i = sl * 2 + jj
                    norm_transpose_tile(ctx, "aXMB", i,
                                        lambda a, jj=jj: XN2F[:, a * 256 + jj * 128: a * 256 + (jj + 1) * 128],
                                        "aXN2F", AF2, "aAF2", BF2, "aBF2", pTR, "apTR",
                                        f"A.x2{b}.{i}")
                for oc in range(16):
                    ph = pH1[oc % 2]
                    def emH(t, oc=oc, ph=ph):
                        for a in range(DP):
                            last = t.matmul(ph[:, :],
                                            W1T()[:, a * 2048 + oc * 128: a * 2048 + (oc + 1) * 128],
                                            XN2F[:, a * 256:(a + 1) * 256],
                                            start=(a == 0), stop=(a == DP - 1))
                        return last
                    prog.add("tensor", emH, reads=["aXN2F", "WBr0", "WBr1"],
                             writes=["apQ" if oc % 2 == 0 else "apK"],
                             tag=f"A.h1{b}.{sl}.{oc}")
                    prog.add("scalar", lambda s, oc=oc, ph=ph: s.activation(
                        H1F[:, oc * 256:(oc + 1) * 256], ph[:, :], AF.Silu,
                        bias=B1F[:, oc:oc + 1], scale=1.0),
                        reads=["apQ" if oc % 2 == 0 else "apK", "aB1F2"],
                        writes=[R("aH1F", oc)], tag=f"A.si{b}.{sl}.{oc}")
                for jj in range(2):
                    i = sl * 2 + jj
                    xi = nt0 + i
                    def emF(t, jj=jj):
                        for half in range(2):
                            for oc in range(16):
                                t.matmul(pV[:, half * 512:(half + 1) * 512],
                                         H1F[:, oc * 256 + jj * 128: oc * 256 + (jj + 1) * 128],
                                         W2T()[:, oc * 1024 + half * 512: oc * 1024 + (half + 1) * 512],
                                         start=(oc == 0), stop=False)
                            last = t.matmul(pV[:, half * 512:(half + 1) * 512],
                                            ctx.sb["cOnesF"][0:1, 0:128],
                                            B2R[0:1, half * 512:(half + 1) * 512],
                                            start=False, stop=True)
                        return last
                    prog.add("tensor", emF,
                             reads=[R("aH1F", oc) for oc in range(16)] + ["WBr2", "WBr3", "aB2R2", "cOnesF"],
                             writes=["apV"], tag=f"A.w2{b}.{sl}.{jj}")
                    prog.add("vector", lambda v, i=i, xi=xi: v.tensor_add(
                        ctx.sb["LNZ"][:, :], pV[:, :], XB[:, i * D:(i + 1) * D]),
                        reads=["apV", R("XB", i)], writes=["LNZ"], tag=f"A.z1{xi}")
                    prog.add("vector", lambda v, i=i: v.tensor_add(
                        ctx.sb["LNZ"][:, :], ctx.sb["LNZ"][:, :], XMB[:, i * D:(i + 1) * D]),
                        reads=["LNZ", R("aXMB", i)], writes=["LNZ"], tag=f"A.z2{xi}")
                    emit_ln_out(ctx, xi, is_last, f"A.ln{xi}")


    ctx.cur_es = None

def build_stage_B(ctx, is_last):
    nc, prog, cfg = ctx.nc, ctx.prog, ctx.cfg
    es = contextlib.ExitStack()
    TPB = ctx.TPB
    T = ctx.T
    K = T // 4
    NRS = K // 128
    RG = max(1, K // 256)
    RPG = K // RG
    NSG = RPG // 128
    NBIS = 36
    ctx.cur_es = es
    with es:
        GWTs = ctx.sbuf("bGWTs", [128, DP * 8], BF16)
        XF1 = ctx.sbuf("bXF1", [128, DP * 128], BF16)
        EXPT = ctx.sbuf("bEXPT", [128, 8], F32)
        RLB = ctx.sbuf("bRLB", [128, 4], F32)
        AFFT = ctx.sbuf("bAFFT", [128, ctx.B * TPB * 8], F32)
        AFFB = ctx.sbuf("bAFFB", [128, ctx.B * TPB * 8], BF16)
        AFFE = ctx.sbuf("bAFFE", [64, T], F32)
        THL = ctx.sbuf("bTHL", [64, 8], F32)
        THR = ctx.sbuf("bTHR", [1, 16], F32)
        THBS = ctx.sbuf("bTHBS", [128, 8], F32)
        MSKS = ctx.sbuf("bMSKS", [128, 8], F32)
        RSUM = ctx.sbuf("bRSUM", [1, 8], F32)
        RKT = ctx.sbuf("bRKT", [128, ctx.B * TPB * 8], F32)
        IDENTF = ctx.sbuf("bIDF", [128, 128], F32)
        LTRIS = ctx.sbuf("bLTRI", [128, 128], F32)
        IOTA512 = ctx.sbuf("bIOTA", [128, K], F32)
        PTS = ctx.sbuf("bPTS", [128, TPB * 256], BF16)
        PTB = ctx.sbuf("bPTB", [128, K], BF16)
        PSL = ctx.sbuf("bPSL", [128, NRS * 128], BF16)
        XST = ctx.sbuf("bXST", [128, DP * 512], BF16)
        GCOL = ctx.sbuf("bGCOL", [128, max(NRS, 1)], F32)
        SIL = ctx.sbuf("bSIL", [128, 256], BF16)
        HT = ctx.sbuf("bHT", [128, 32 * 512], BF16)
        YB = ctx.sbuf("bYB", [128, 8 * NRS * 1024], BF16)
        WG = ctx.sbuf("bWG", [128, 2 * DP * 512], BF16)
        WU = ctx.sbuf("bWU", [128, 2 * DP * 512], BF16)
        WD = ctx.sbuf("bWD", [128, 2 * 512], BF16)
        pTR = ctx.psum("bpTR", [128, 128], BF16)
        bpA = ctx.psum("bpA", [128, 512])
        bpB = ctx.psum("bpB", [128, 512])
        Y4 = [ctx.psum(f"bY{c}", [128, 512]) for c in "abcd"]
        Y4r = [f"bY{c}" for c in "abcd"]

        XB = ctx.sb["XB"]
        gw_src = ctx.dram["bGwT"].ap().rearrange("(a p) n -> p a n", p=128)
        gw_dst = GWTs[:, :].rearrange("p (a n) -> p a n", a=DP)
        prog.add("gpsimd", lambda g, d=gw_dst, s=gw_src: g.dma_start(out=d, in_=s),
            writes=["bGWTs"], dma=True, tag="B.ldgw")
        ctx.dma_in("sync", IDENTF[:, :], ctx.dram["cIdent"].ap(), ["bIDF"], "B.idf")
        ctx.dma_in("sync", LTRIS[:, :], ctx.dram["cLtri"].ap(), ["bLTRI"], "B.ltri")
        prog.add("gpsimd", lambda g: g.iota(IOTA512[:, :], [[1, K]], base=1,
                                            channel_multiplier=0,
                                            allow_small_or_imprecise_dtypes=True),
                 writes=["bIOTA"], tag="B.iota")
        prog.add("vector", lambda v: v.memset(AFFE[:, :], 0.5), writes=["bAFFE"], tag="B.ae0")

        for b in range(ctx.B):
            load_xb(ctx, b, f"Br.{b}")
            for i in range(TPB):
                xi = b * TPB + i
                ci = (b * TPB + i) * 8
                for a in range(DP):
                    transpose_tile(ctx, XF1[:, a * 128:(a + 1) * 128],
                                   XB[:, i * D + a * 128: i * D + (a + 1) * 128],
                                   pTR[:, :], ctx.sb["cIdentB"][:, :],
                                   reads=[R("XB", i)], writes=["bXF1"], pres="bpTR",
                                   tag=f"B.xf{xi}.{a}")
                def emL(t):
                    for a in range(DP):
                        last = t.matmul(bpA[:, 0:8], XF1[:, a * 128:(a + 1) * 128],
                                        GWTs[:, a * 8:(a + 1) * 8],
                                        start=(a == 0), stop=(a == DP - 1))
                    return last
                prog.add("tensor", emL, reads=["bXF1", "bGWTs"], writes=["bbpA"],
                         tag=f"B.lg{xi}")
                prog.add("vector", lambda v: v.tensor_copy(EXPT[:, :], bpA[:, 0:8]),
                         reads=["bbpA"], writes=["bEXPT"], tag=f"B.sm0{xi}")
                prog.add("vector", lambda v: v.reduce_max(RLB[:, 0:1], EXPT[:, :],
                                                          axis=AX.X, negate=True),
                         reads=["bEXPT"], writes=["bRLB"], tag=f"B.sm1{xi}")
                prog.add("scalar", lambda s: s.activation(EXPT[:, :], EXPT[:, :], AF.Exp,
                                                          bias=RLB[:, 0:1]),
                         reads=["bEXPT", "bRLB"], writes=["bEXPT"], tag=f"B.sm2{xi}")
                prog.add("vector", lambda v: v.reduce_sum(RLB[:, 1:2], EXPT[:, :], axis=AX.X),
                         reads=["bEXPT"], writes=["bRLB"], tag=f"B.sm3{xi}")
                prog.add("vector", lambda v: v.reciprocal(RLB[:, 2:3], RLB[:, 1:2]),
                         reads=["bRLB"], writes=["bRLB"], tag=f"B.sm4{xi}")
                prog.add("vector", lambda v, ci=ci: v.tensor_scalar(
                    AFFT[:, ci:ci + 8], EXPT[:, :], RLB[:, 2:3], None, OP.mult),
                    reads=["bEXPT", "bRLB"], writes=["bAFFT"], tag=f"B.sm5{xi}")
                prog.add("vector", lambda v, ci=ci: v.tensor_copy(
                    AFFB[:, ci:ci + 8], AFFT[:, ci:ci + 8]),
                    reads=["bAFFT"], writes=["bAFFB"], tag=f"B.sm6{xi}")
                prog.add("tensor", lambda t, ci=ci: t.transpose(
                    bpB[0:8, 0:128], AFFT[:, ci:ci + 8], IDENTF[:, :]),
                    reads=["bAFFT", "bIDF"], writes=["bbpB"], tag=f"B.tr{xi}")
                prog.add("vector", lambda v, b=b, i=i: v.tensor_copy(
                    AFFE[b * 32:b * 32 + 8, i * 128:(i + 1) * 128], bpB[0:8, 0:128]),
                    reads=["bbpB"], writes=["bAFFE"], tag=f"B.ae{xi}")
        prog.add("vector", lambda v: v.memset(THL[:, 0:1], 0.0), writes=["bTHL"], tag="B.th0")
        prog.add("vector", lambda v: v.memset(THL[:, 1:2], 1.0),
                 reads=["bTHL"], writes=["bTHL"], tag="B.th1")
        BSS = PTS[0:64, 0:T]
        for it in range(NBIS):
            prog.add("vector", lambda v: v.tensor_add(THL[:, 2:3], THL[:, 0:1], THL[:, 1:2]),
                     reads=["bTHL"], writes=["bTHL"], tag=f"B.b0.{it}")
            prog.add("vector", lambda v: v.tensor_scalar(THL[:, 2:3], THL[:, 2:3], 0.5, None, OP.mult),
                     reads=["bTHL"], writes=["bTHL"], tag=f"B.b1.{it}")
            prog.add("vector", lambda v: v.tensor_scalar(
                BSS, AFFE[:, :], THL[:, 2:3], None, OP.is_ge, OP.add, accum_out=THL[:, 3:4]),
                reads=["bAFFE", "bTHL"], writes=["bPTS", "bTHL"], tag=f"B.b2.{it}")
            prog.add("vector", lambda v: v.tensor_single_scalar(
                THL[:, 4:5], THL[:, 3:4], float(K), OP.is_ge),
                reads=["bTHL"], writes=["bTHL"], tag=f"B.b3.{it}")
            prog.add("vector", lambda v: v.tensor_sub(THL[:, 5:6], THL[:, 2:3], THL[:, 0:1]),
                     reads=["bTHL"], writes=["bTHL"], tag=f"B.b4.{it}")
            prog.add("vector", lambda v: v.scalar_tensor_tensor(
                THL[:, 0:1], THL[:, 5:6], THL[:, 4:5], THL[:, 0:1], OP.mult, OP.add),
                reads=["bTHL"], writes=["bTHL"], tag=f"B.b5.{it}")
            prog.add("vector", lambda v: v.tensor_sub(THL[:, 6:7], THL[:, 1:2], THL[:, 2:3]),
                     reads=["bTHL"], writes=["bTHL"], tag=f"B.b6.{it}")
            prog.add("vector", lambda v: v.scalar_tensor_tensor(
                THL[:, 1:2], THL[:, 6:7], THL[:, 4:5], THL[:, 2:3], OP.mult, OP.add),
                reads=["bTHL"], writes=["bTHL"], tag=f"B.b7.{it}")
        for b in range(ctx.B):
            prog.add("tensor", lambda t, b=b: t.transpose(
                bpB[0:1, 0:8], THL[b * 32:b * 32 + 8, 0:1], IDENTF[b * 32:b * 32 + 8, b * 32:b * 32 + 8]),
                reads=["bTHL", "bIDF"], writes=["bbpB"], tag=f"B.tt{b}")
            prog.add("vector", lambda v, b=b: v.tensor_copy(THR[0:1, b * 8:(b + 1) * 8],
                                                            bpB[0:1, 0:8]),
                     reads=["bbpB"], writes=["bTHR"], tag=f"B.te{b}")
        for b in range(ctx.B):
            prog.add("tensor", lambda t, b=b: t.matmul(
                bpA[:, 0:8], ctx.sb["cOnesF"][0:1, 0:128], THR[0:1, b * 8:(b + 1) * 8],
                start=True, stop=True),
                reads=["bTHR", "cOnesF"], writes=["bbpA"], tag=f"B.tb{b}")
            prog.add("vector", lambda v: v.tensor_copy(THBS[:, :], bpA[:, 0:8]),
                     reads=["bbpA"], writes=["bTHBS"], tag=f"B.tb2{b}")
            prog.add("vector", lambda v: v.memset(RSUM[:, :], 0.0),
                     reads=["bRSUM"], writes=["bRSUM"], tag=f"B.rs0{b}")
            for i in range(TPB):
                ci = (b * TPB + i) * 8
                prog.add("vector", lambda v, ci=ci: v.tensor_tensor(
                    MSKS[:, :], AFFT[:, ci:ci + 8], THBS[:, :], OP.is_ge),
                    reads=["bAFFT", "bTHBS"], writes=["bMSKS"], tag=f"B.mk{b}.{i}")
                def emRk(t):
                    t.matmul(bpA[:, 0:8], LTRIS[:, :], MSKS[:, :], start=True, stop=False)
                    return t.matmul(bpA[:, 0:8], ctx.sb["cOnesF"][0:1, 0:128],
                                    RSUM[0:1, 0:8], start=False, stop=True)
                prog.add("tensor", emRk, reads=["bMSKS", "bRSUM", "bLTRI", "cOnesF"],
                         writes=["bbpA"], tag=f"B.rk{b}.{i}")
                prog.add("vector", lambda v, ci=ci: v.tensor_mul(
                    RKT[:, ci:ci + 8], bpA[:, 0:8], MSKS[:, :]),
                    reads=["bbpA", "bMSKS"], writes=["bRKT"], tag=f"B.rm{b}.{i}")
                prog.add("tensor", lambda t: t.matmul(
                    bpB[0:1, 0:8], ctx.sb["cOnesF"][:, 0:1], MSKS[:, :], start=True, stop=True),
                    reads=["bMSKS", "cOnesF"], writes=["bbpB"], tag=f"B.cs{b}.{i}")
                prog.add("vector", lambda v: v.tensor_add(RSUM[0:1, 0:8], RSUM[0:1, 0:8],
                                                          bpB[0:1, 0:8]),
                         reads=["bbpB", "bRSUM"], writes=["bRSUM"], tag=f"B.ru{b}.{i}")
        for b in range(ctx.B):
            load_xb(ctx, b, f"Be.{b}")
            for e in range(8):
                for rg in range(RG):
                    for i in range(TPB):
                        ci = (b * TPB + i) * 8 + e
                        prog.add("vector", lambda v, i=i, ci=ci, rg=rg: v.tensor_scalar(
                            PTS[:, i * RPG:(i + 1) * RPG], IOTA512[:, rg * RPG:(rg + 1) * RPG],
                            RKT[:, ci:ci + 1], None, OP.is_equal),
                            reads=["bIOTA", "bRKT"], writes=[R("bPTS", i), "bPTS"],
                            tag=f"B.pt{b}.{e}.{rg}.{i}")
                    g4 = [bpA[:, 0:RPG], bpB[:, 0:RPG], Y4[0][:, 0:RPG], Y4[1][:, 0:RPG]]
                    g4res = ["bbpA", "bbpB", Y4r[0], Y4r[1]]
                    for pss in range(2):
                        for i in range(TPB):
                            def emG(t, i=i, pss=pss):
                                for dc in range(4):
                                    last = t.matmul(g4[dc],
                                                    XB[:, i * D + (pss * 4 + dc) * 128: i * D + (pss * 4 + dc + 1) * 128],
                                                    PTS[:, i * RPG:(i + 1) * RPG],
                                                    start=(i == 0), stop=(i == TPB - 1))
                                return last
                            prog.add("tensor", emG,
                                     reads=[R("XB", i), R("bPTS", i)],
                                     writes=list(set(g4res)), tag=f"B.g{pss}{b}.{e}.{rg}.{i}")
                            if pss == 0:
                                def emGC(t, i=i, b=b, e=e):
                                    for rc in range(NSG):
                                        last = t.matmul(Y4[2 + rc][:, 0:1],
                                                        PTS[:, i * RPG + rc * 128: i * RPG + (rc + 1) * 128],
                                                        AFFB[:, (b * TPB + i) * 8 + e:(b * TPB + i) * 8 + e + 1],
                                                        start=(i == 0), stop=(i == TPB - 1))
                                    return last
                                prog.add("tensor", emGC, reads=[R("bPTS", i), "bAFFB"],
                                         writes=[Y4r[2 + rc] for rc in range(NSG)],
                                         tag=f"B.gc{b}.{e}.{rg}.{i}")
                        for dc in range(4):
                            prog.add("scalar", lambda s, dc=dc, pss=pss, rg=rg: s.copy(
                                XST[:, (pss * 4 + dc) * 512 + rg * 256: (pss * 4 + dc) * 512 + rg * 256 + RPG],
                                g4[dc]),
                                reads=[g4res[dc]], writes=["bXST"], tag=f"B.ge{pss}{b}.{e}.{rg}.{dc}")
                    for rc in range(NSG):
                        rsub = rg * NSG + rc
                        prog.add("vector", lambda v, rc=rc, rsub=rsub: v.tensor_copy(
                            GCOL[:, rsub: rsub + 1], Y4[2 + rc][:, 0:1]),
                            reads=[Y4r[2 + rc]], writes=["bGCOL"], tag=f"B.gce{b}.{e}.{rg}.{rc}")
                for ic in range(8):
                    slot = ic % 2
                    for wnm, wsb, wres in [("bWgT", WG, "bWG"), ("bWuT", WU, "bWU")]:
                        srcw = ctx.dram[wnm].ap()[e, :, ic * 512:(ic + 1) * 512].rearrange(
                            "(a p) n -> p a n", p=128)
                        dstw = wsb[:, slot * 4096:(slot + 1) * 4096].rearrange(
                            "p (a n) -> p a n", a=DP)
                        prog.add("gpsimd", lambda g, dst=dstw, src=srcw: g.dma_start(out=dst, in_=src),
                                 writes=[R(wres, slot)], dma=True, tag=f"B.lw{wnm}{b}.{e}.{ic}")
                    for rg in range(RG):
                        for isub in range(4):
                            def emGm(t, ic=ic, isub=isub, slot=slot, rg=rg):
                                for a in range(DP):
                                    last = t.matmul(bpA[:, 0:RPG],
                                                    WG[:, slot * 4096 + a * 512 + isub * 128: slot * 4096 + a * 512 + (isub + 1) * 128],
                                                    XST[:, a * 512 + rg * 256: a * 512 + rg * 256 + RPG],
                                                    start=(a == 0), stop=(a == DP - 1))
                                return last
                            prog.add("tensor", emGm, reads=[R("bWG", slot), "bXST"],
                                     writes=["bbpA"], tag=f"B.hg{b}.{e}.{rg}.{ic}.{isub}")
                            prog.add("scalar", lambda s: s.activation(SIL[:, 0:RPG], bpA[:, 0:RPG], AF.Silu),
                                     reads=["bbpA"], writes=["bSIL"], tag=f"B.hs{b}.{e}.{rg}.{ic}.{isub}")
                            def emUm(t, ic=ic, isub=isub, slot=slot, rg=rg):
                                for a in range(DP):
                                    last = t.matmul(bpB[:, 0:RPG],
                                                    WU[:, slot * 4096 + a * 512 + isub * 128: slot * 4096 + a * 512 + (isub + 1) * 128],
                                                    XST[:, a * 512 + rg * 256: a * 512 + rg * 256 + RPG],
                                                    start=(a == 0), stop=(a == DP - 1))
                                return last
                            prog.add("tensor", emUm, reads=[R("bWU", slot), "bXST"],
                                     writes=["bbpB"], tag=f"B.hu{b}.{e}.{rg}.{ic}.{isub}")
                            hidx = ic * 4 + isub
                            prog.add("vector", lambda v, hidx=hidx, rg=rg: v.tensor_mul(
                                HT[:, hidx * 512 + rg * 256: hidx * 512 + rg * 256 + RPG],
                                SIL[:, 0:RPG], bpB[:, 0:RPG]),
                                reads=["bSIL", "bbpB"], writes=[R("bHT", hidx)],
                                tag=f"B.hm{b}.{e}.{rg}.{ic}.{isub}")
                for dh in range(2):
                    for it in range(32):
                        slot = it % 2
                        srcw = ctx.dram["bWdT"].ap()[e, it * 128:(it + 1) * 128, dh * 512:(dh + 1) * 512]
                        prog.add("gpsimd", lambda g, slot=slot, src=srcw: g.dma_start(
                            out=WD[:, slot * 512:(slot + 1) * 512], in_=src),
                            writes=[R("bWD", slot)], dma=True, tag=f"B.lwd{b}.{e}.{dh}.{it}")
                        def emD(t, it=it, slot=slot):
                            for rsub in range(NRS):
                                last = t.matmul(Y4[rsub][:, 0:512],
                                                HT[:, it * 512 + rsub * 128: it * 512 + rsub * 128 + 128],
                                                WD[:, slot * 512:(slot + 1) * 512],
                                                start=(it == 0), stop=(it == 31))
                            return last
                        prog.add("tensor", emD,
                                 reads=[R("bHT", it), R("bWD", slot)],
                                 writes=[Y4r[rsub] for rsub in range(NRS)],
                                 tag=f"B.dn{b}.{e}.{dh}.{it}")
                    for rsub in range(NRS):
                        prog.add("vector", lambda v, rsub=rsub, dh=dh, e=e: v.tensor_scalar(
                            YB[:, (e * NRS + rsub) * 1024 + dh * 512:(e * NRS + rsub) * 1024 + dh * 512 + 512],
                            Y4[rsub][:, 0:512], GCOL[:, rsub: rsub + 1], None, OP.mult),
                            reads=[Y4r[rsub], "bGCOL"],
                            writes=[R("bYB", e * NRS + rsub)], tag=f"B.ye{b}.{e}.{dh}.{rsub}")
            for i in range(TPB):
                xi = b * TPB + i
                for e in range(8):
                    ci = (b * TPB + i) * 8 + e
                    prog.add("vector", lambda v, ci=ci: v.tensor_scalar(
                        PTB[:, :], IOTA512[:, :], RKT[:, ci:ci + 1], None, OP.is_equal),
                        reads=["bIOTA", "bRKT"], writes=["bPTB"], tag=f"B.sp{b}.{i}.{e}")
                    for rsub in range(NRS):
                        prog.add("tensor", lambda t, rsub=rsub: t.transpose(
                            pTR[:, :], PTB[:, rsub * 128:(rsub + 1) * 128], ctx.sb["cIdentB"][:, :]),
                            reads=["bPTB", "cIdentB"], writes=["bpTR"], tag=f"B.st{b}.{i}.{e}.{rsub}")
                        prog.add("vector", lambda v, rsub=rsub: v.tensor_copy(
                            PSL[:, rsub * 128:(rsub + 1) * 128], pTR[:, :]),
                            reads=["bpTR"], writes=["bPSL"], tag=f"B.se{b}.{i}.{e}.{rsub}")
                    def emSc(t, e=e):
                        for rsub in range(NRS):
                            for dh in range(2):
                                last = t.matmul(Y4[dh][:, 0:512],
                                                PSL[:, rsub * 128:(rsub + 1) * 128],
                                                YB[:, (e * NRS + rsub) * 1024 + dh * 512: (e * NRS + rsub) * 1024 + dh * 512 + 512],
                                                start=(e == 0 and rsub == 0), stop=(e == 7 and rsub == NRS - 1))
                        return last
                    prog.add("tensor", emSc,
                             reads=["bPSL"] + [R("bYB", e * NRS + rsub) for rsub in range(NRS)],
                             writes=[Y4r[0], Y4r[1]], tag=f"B.sc{b}.{i}.{e}")
                for dh in range(2):
                    prog.add("vector", lambda v, i=i, dh=dh: v.tensor_add(
                        ctx.sb["LNZ"][:, dh * 512:(dh + 1) * 512], Y4[dh][:, 0:512],
                        XB[:, i * D + dh * 512: i * D + dh * 512 + 512]),
                        reads=[Y4r[dh], R("XB", i)], writes=["LNZ"], tag=f"B.z{xi}.{dh}")
                emit_ln_out(ctx, xi, is_last, f"B.ln{xi}")

    ctx.cur_es = None

def build_stage_C(ctx, is_last):
    nc, prog, cfg = ctx.nc, ctx.prog, ctx.cfg
    es = contextlib.ExitStack()
    TPB = ctx.TPB
    NTOK = ctx.B * ctx.T
    NTOT = cfg["n_cores"] * NTOK if cfg["collective"] else NTOK
    NSL = TPB // 4
    ctx.cur_es = es
    with es:
        WC = ctx.sbuf("cWC", [128, 4 * 8192], BF16)
        XF = ctx.sbuf("cXF", [128, DP * 512], BF16)
        HB = ctx.sbuf("cHB", [128, 512], BF16)
        CSQ = ctx.sbuf("cCSQ", [128, 512], F32)
        SSTAT = ctx.sbuf("cSSTAT", [128, 48], F32)
        GSTAT = ctx.sbuf("cGSTAT", [128, 48], F32)
        RT = ctx.sbuf("cRT", [128, 4], F32)
        CF = ctx.sbuf("cCF", [128, 48], F32)
        A3 = ctx.sbuf("cA3", [128, 24], F32)
        B3 = ctx.sbuf("cB3", [128, 24], F32)
        HSB = ctx.sbuf("cHSB", [128, DP * 512], BF16)
        VF = ctx.sbuf("cVF", [128, DP * 512], BF16)
        QBF = ctx.sbuf("cQBF", [128, DP * 512], BF16)
        KBF = ctx.sbuf("cKBF", [128, DP * 512], BF16)
        QKF = ctx.sbuf("cQKF", [128, DP * 512], BF16)
        OUTF = ctx.sbuf("cOUTF", [128, DP * 512], BF16)
        VTOK = ctx.sbuf("cVTOK", [128, TPB * D], BF16)
        VVS = ctx.sbuf("cVVS", [128, 8 * DH], BF16)
        BPR = ctx.sbuf("cBPR", [1, D], F32)
        pTR = ctx.psum("cpTR", [128, 128], BF16)
        pB0 = ctx.psum("cpB0", [128, 512])
        pB1 = ctx.psum("cpB1", [128, 512])
        pO = ctx.psum("cpO", [128, D])
        hq = nc.dram_tensor("c_hq", [D, NTOK], BF16)
        hk = nc.dram_tensor("c_hk", [D, NTOK], BF16)
        hv = nc.dram_tensor("c_hv", [D, NTOK], BF16)
        ccin = nc.dram_tensor("c_ccin", [128, 48], F32)
        ccout = nc.dram_tensor("c_ccout", [128, 48], F32)

        for r, nm in enumerate(["cWqT", "cWkT", "cWvT", "cWpT"]):
            src = ctx.dram[nm].ap().rearrange("(a p) n -> p a n", p=128)
            dst = WC[:, r * 8192:(r + 1) * 8192].rearrange("p (a n) -> p a n", a=DP)
            prog.add("gpsimd", lambda g, dst=dst, src=src: g.dma_start(out=dst, in_=src),
                     writes=[f"cWC{r}"], dma=True, tag=f"C.ld{nm}")
        PAR = {}
        for nm in ["cGq_g", "cGq_b", "cGk_g", "cGk_b", "cGv_g", "cGv_b"]:
            t = ctx.sbuf(nm + "s", [128, DP], F32)
            ctx.dma_in("sync", t[:, :], ctx.dram[nm].ap(), [nm + "s"], f"C.{nm}")
            PAR[nm] = t
        ctx.dma_in("sync", BPR[:, :], ctx.dram["cBpR"].ap(), ["cBPR"], "C.bp")
        prog.add("vector", lambda v: v.memset(SSTAT[:, :], 0.0), writes=["cSSTAT"], tag="C.st0")

        XB = ctx.sb["XB"]
        branches = [("q", hq, 0, 0), ("k", hk, 1, 1), ("v", hv, 2, 2)]
        for b in range(ctx.B):
            load_xb(ctx, b, f"C1.{b}")
            for s in range(NSL):
                for j in range(4):
                    i_loc = s * 4 + j
                    xi = b * TPB + i_loc
                    for a in range(DP):
                        transpose_tile(
                            ctx, XF[:, a * 512 + j * 128: a * 512 + (j + 1) * 128],
                            XB[:, i_loc * D + a * 128: i_loc * D + (a + 1) * 128],
                            pTR[:, :], ctx.sb["cIdentB"][:, :],
                            reads=[R("XB", i_loc)], writes=[R("cXF", a)], pres="cpTR",
                            tag=f"C.xf{xi}.{a}")
                for bi, (bn, hd, wr, sc0) in enumerate(branches):
                    for oc in range(DP):
                        pb = pB0 if oc % 2 == 0 else pB1
                        pres = "cpB0" if oc % 2 == 0 else "cpB1"
                        def emH(t, wr=wr, oc=oc, pb=pb):
                            for a in range(DP):
                                last = t.matmul(pb[:, :],
                                                WC[:, wr * 8192 + a * 1024 + oc * 128: wr * 8192 + a * 1024 + (oc + 1) * 128],
                                                XF[:, a * 512:(a + 1) * 512],
                                                start=(a == 0), stop=(a == DP - 1))
                            return last
                        prog.add("tensor", emH,
                                 reads=[R("cXF", a) for a in range(DP)] + [f"cWC{wr}"],
                                 writes=[pres], tag=f"C.h{bn}{b}.{s}.{oc}")
                        prog.add("scalar", lambda sc, pb=pb: sc.copy(HB[:, :], pb[:, :]),
                                 reads=[pres], writes=["cHB"], tag=f"C.he{bn}{b}.{s}.{oc}")
                        dst = hd.ap()[oc * 128:(oc + 1) * 128, b * ctx.T + s * 512: b * ctx.T + (s + 1) * 512]
                        ctx.dma_out("sync", dst, HB[:, :], reads=["cHB"],
                                    writes=[R(f"c_h{bn}", (b, s, oc))], tag=f"C.hs{bn}{b}.{s}.{oc}")
                        sumc = SSTAT[:, bi * 16 + oc: bi * 16 + oc + 1]
                        sqc = SSTAT[:, bi * 16 + 8 + oc: bi * 16 + 8 + oc + 1]
                        prog.add("scalar", lambda sc, pb=pb: sc.activation(CSQ[:, :], pb[:, :], AF.Identity,
                                                                           accum_out=RT[:, 0:1]),
                                 reads=[pres, "cRT"], writes=["cCSQ", "cRT"], tag=f"C.r1{bn}{b}.{s}.{oc}")
                        prog.add("vector", lambda v, sumc=sumc: v.tensor_add(sumc, sumc, RT[:, 0:1]),
                                 reads=["cRT", "cSSTAT"], writes=["cSSTAT"], tag=f"C.r2{bn}{b}.{s}.{oc}")
                        prog.add("scalar", lambda sc, pb=pb: sc.activation(CSQ[:, :], pb[:, :], AF.Square,
                                                                           accum_out=RT[:, 1:2]),
                                 reads=[pres, "cRT", "cCSQ"], writes=["cCSQ", "cRT"], tag=f"C.r3{bn}{b}.{s}.{oc}")
                        prog.add("vector", lambda v, sqc=sqc: v.tensor_add(sqc, sqc, RT[:, 1:2]),
                                 reads=["cRT", "cSSTAT"], writes=["cSSTAT"], tag=f"C.r4{bn}{b}.{s}.{oc}")
        if cfg["collective"]:
            ctx.dma_out("gpsimd", ccin.ap(), SSTAT[:, :], reads=["cSSTAT"], writes=["c_ccin"], tag="C.cci")
            prog.add("gpsimd", lambda g: g.collective_compute(
                "AllReduce", OP.add, replica_groups=[list(range(cfg["n_cores"]))],
                ins=[ccin.ap()], outs=[ccout.ap()]),
                reads=["c_ccin"], writes=["c_ccout"], dma="cc", tag="C.cc")
            ctx.dma_out("gpsimd", GSTAT[:, :], ccout.ap(), reads=["c_ccout"],
                        writes=["cGSTAT"], tag="C.cco")
        else:
            prog.add("vector", lambda v: v.tensor_copy(GSTAT[:, :], SSTAT[:, :]),
                     reads=["cSSTAT"], writes=["cGSTAT"], tag="C.cc")
        for bi, (bn, hd, wr, sc0) in enumerate(branches):
            g_sb = PAR[f"cG{bn}_g"]
            b_sb = PAR[f"cG{bn}_b"]
            mu = CF[:, bi * 16:bi * 16 + 8]
            e2 = CF[:, bi * 16 + 8:bi * 16 + 16]
            a3 = A3[:, bi * 8:(bi + 1) * 8]
            b3 = B3[:, bi * 8:(bi + 1) * 8]
            prog.add("vector", lambda v, bi=bi, mu=mu: v.tensor_scalar(
                mu, GSTAT[:, bi * 16:bi * 16 + 8], 1.0 / NTOT, None, OP.mult),
                reads=["cGSTAT"], writes=["cCF"], tag=f"C.c1{bn}")
            prog.add("vector", lambda v, bi=bi, e2=e2: v.tensor_scalar(
                e2, GSTAT[:, bi * 16 + 8:bi * 16 + 16], 1.0 / NTOT, None, OP.mult),
                reads=["cGSTAT"], writes=["cCF"], tag=f"C.c2{bn}")
            prog.add("vector", lambda v, mu=mu, e2=e2, a3=a3: v.tensor_mul(a3, mu, mu),
                     reads=["cCF"], writes=["cA3"], tag=f"C.c3{bn}")
            prog.add("vector", lambda v, e2=e2, a3=a3: v.tensor_sub(a3, e2, a3),
                     reads=["cCF", "cA3"], writes=["cA3"], tag=f"C.c4{bn}")
            prog.add("scalar", lambda s, a3=a3: s.activation(a3, a3, AF.Sqrt,
                                                             bias=ctx.sb["CONSTV"][:, 0:1]),
                     reads=["cA3", "CONSTV"], writes=["cA3"], tag=f"C.c5{bn}")
            prog.add("vector", lambda v, a3=a3: v.reciprocal(a3, a3),
                     reads=["cA3"], writes=["cA3"], tag=f"C.c6{bn}")
            prog.add("vector", lambda v, a3=a3, g_sb=g_sb: v.tensor_mul(a3, a3, g_sb[:, :]),
                     reads=["cA3", f"cG{bn}_gs"], writes=["cA3"], tag=f"C.c7{bn}")
            prog.add("vector", lambda v, b3=b3, mu=mu, a3=a3: v.tensor_mul(b3, mu, a3),
                     reads=["cCF", "cA3"], writes=["cB3"], tag=f"C.c8{bn}")
            prog.add("vector", lambda v, b3=b3, b_sb=b_sb: v.tensor_sub(b3, b_sb[:, :], b3),
                     reads=["cB3", f"cG{bn}_bs"], writes=["cB3"], tag=f"C.c9{bn}")
            prog.add("vector", lambda v, b3=b3: v.tensor_scalar(b3, b3, 4.0, -4.0, OP.mult, OP.add),
                     reads=["cB3"], writes=["cB3"], tag=f"C.cA{bn}")
            prog.add("vector", lambda v, a3=a3: v.tensor_scalar(a3, a3, 4.0, None, OP.mult),
                     reads=["cA3"], writes=["cA3"], tag=f"C.cB{bn}")

        def sig_stream(bn, bi, hd, b, s, dstF, dres):
            src = hd.ap()[:, b * ctx.T + s * 512: b * ctx.T + (s + 1) * 512].rearrange(
                "(a p) n -> p a n", p=128)
            dstH = HSB[:, :].rearrange("p (a n) -> p a n", a=DP)
            prog.add("sync", lambda g: g.dma_start(out=dstH, in_=src),
                     reads=[R(f"c_h{bn}", (b, s, oc)) for oc in range(DP)],
                     writes=["cHSB"], dma=True, tag=f"C.l{bn}{b}.{s}")
            for oc in range(DP):
                prog.add("scalar", lambda sc, oc=oc, bi=bi, dstF=dstF: sc.activation(
                    dstF[:, oc * 512:(oc + 1) * 512], HSB[:, oc * 512:(oc + 1) * 512],
                    AF.Sigmoid, bias=B3[:, bi * 8 + oc:bi * 8 + oc + 1],
                    scale=A3[:, bi * 8 + oc:bi * 8 + oc + 1]),
                    reads=["cHSB", "cA3", "cB3"], writes=[dres], tag=f"C.sg{bn}{b}.{s}.{oc}")

        for b in range(ctx.B):
            load_xb(ctx, b, f"C2.{b}")
            for s in range(NSL):
                sig_stream("v", 2, hv, b, s, VF, "cVF")
                for j in range(4):
                    for a in range(DP):
                        transpose_tile(
                            ctx, VTOK[:, (s * 4 + j) * D + a * 128: (s * 4 + j) * D + (a + 1) * 128],
                            VF[:, a * 512 + j * 128: a * 512 + (j + 1) * 128],
                            pTR[:, :], ctx.sb["cIdentB"][:, :],
                            reads=["cVF"], writes=[R("cVTOK", s * 4 + j)], pres="cpTR",
                            tag=f"C.vt{b}.{s * 4 + j}.{a}")
            for j in range(8):
                def emVV(t, j=j):
                    for hh in range(2):
                        h = 2 * j + hh
                        ro = hh * 64
                        for it in range(TPB):
                            last = t.matmul(pB0[ro:ro + 64, 0:64],
                                            VTOK[:, it * D + h * 64: it * D + h * 64 + 64],
                                            VTOK[:, it * D + h * 64: it * D + h * 64 + 64],
                                            start=(it == 0), stop=(it == TPB - 1))
                    return last
                prog.add("tensor", emVV, reads=[R("cVTOK", it) for it in range(TPB)],
                         writes=["cpB0"], tag=f"C.vv{b}.{j}")
                prog.add("vector", lambda v, j=j: v.tensor_scalar(
                    VVS[:, j * 64:(j + 1) * 64], pB0[:, 0:64], 0.125, None, OP.mult),
                    reads=["cpB0"], writes=["cVVS"], tag=f"C.ve{b}.{j}")
            for s in range(NSL):
                sig_stream("q", 0, hq, b, s, QBF, "cQBF")
                sig_stream("k", 1, hk, b, s, KBF, "cKBF")
                for a in range(DP):
                    prog.add("vector", lambda v, a=a: v.tensor_mul(
                        QKF[:, a * 512:(a + 1) * 512], QBF[:, a * 512:(a + 1) * 512],
                        KBF[:, a * 512:(a + 1) * 512]),
                        reads=["cQBF", "cKBF"], writes=["cQKF"], tag=f"C.qk{b}.{s}.{a}")
                for j in range(8):
                    def emO2(t, j=j):
                        for hh in range(2):
                            ro = hh * 64
                            last = t.matmul(pB1[ro:ro + 64, :],
                                            VVS[ro:ro + 64, j * 64:(j + 1) * 64],
                                            QKF[ro:ro + 64, j * 512:(j + 1) * 512],
                                            start=True, stop=True)
                        return last
                    prog.add("tensor", emO2, reads=["cVVS", "cQKF"], writes=["cpB1"],
                             tag=f"C.o{b}.{s}.{j}")
                    prog.add("scalar", lambda sc, j=j: sc.copy(OUTF[:, j * 512:(j + 1) * 512], pB1[:, :]),
                             reads=["cpB1"], writes=["cOUTF"], tag=f"C.oe{b}.{s}.{j}")
                for j in range(4):
                    i_loc = s * 4 + j
                    xi = b * TPB + i_loc
                    def emP(t, j=j):
                        for half in range(2):
                            for a in range(DP):
                                t.matmul(pO[:, half * 512:(half + 1) * 512],
                                         OUTF[:, a * 512 + j * 128: a * 512 + (j + 1) * 128],
                                         WC[:, 3 * 8192 + a * 1024 + half * 512: 3 * 8192 + a * 1024 + (half + 1) * 512],
                                         start=(a == 0), stop=False)
                            last = t.matmul(pO[:, half * 512:(half + 1) * 512],
                                            ctx.sb["cOnesF"][0:1, 0:128],
                                            BPR[0:1, half * 512:(half + 1) * 512],
                                            start=False, stop=True)
                        return last
                    prog.add("tensor", emP, reads=["cOUTF", "cWC3", "cBPR", "cOnesF"],
                             writes=["cpO"], tag=f"C.p{xi}")
                    prog.add("vector", lambda v, xi=xi, i_loc=i_loc: v.tensor_add(
                        ctx.sb["LNZ"][:, :], pO[:, :], XB[:, i_loc * D:(i_loc + 1) * D]),
                        reads=["cpO", R("XB", i_loc)], writes=["LNZ"], tag=f"C.z{xi}")
                    emit_ln_out(ctx, xi, is_last, f"C.ln{xi}")


    ctx.cur_es = None


def kernel(**inputs):
    cfg = dict(CFG)
    n_cores = cfg["n_cores"]
    T, Bpc = cfg["T"], cfg["B_per_core"]
    stage_groups = [cfg["stages"]] if cfg["fused"] else [(s,) for s in cfg["stages"]]

    x = np.asarray(inputs["L0_attn_x"], dtype=np.float32)
    B_total = x.shape[0]
    cur = inputs
    out_parts = None
    for gi, grp in enumerate(stage_groups):
        gcfg = dict(cfg)
        gcfg["stages"] = grp
        in_maps = stage_inputs(cur, grp, T, Bpc, n_cores)
        if gi > 0:
            for c in range(n_cores):
                in_maps[c]["x"] = out_parts[c]
        nc = build_program(gcfg)
        res = bass_utils.run_bass_kernel_spmd(nc, in_maps, core_ids=list(range(n_cores)))
        out_parts = [np.asarray(r["xout"]) for r in res.results]
    out = np.concatenate([p.reshape(Bpc, T, D) for p in out_parts], axis=0)
    return out.astype(np.float32)


# version 04
# speedup vs baseline: 1.0148x; 1.0148x over previous
import math
import contextlib
import numpy as np

import concourse.bass as bass
import concourse.mybir as mybir
import concourse.bass_utils as bass_utils

F32 = mybir.dt.float32
BF16 = mybir.dt.bfloat16
OP = mybir.AluOpType
AF = mybir.ActivationFunctionType
AX = mybir.AxisListType

D = 1024
DP = D // 128
NHEAD = 16
DH = 64
CHUNK = 128

CFG = {
    "stages": ("A", "B", "C", "D"),
    "fused": True,
    "n_cores": 8,
    "T": 2048,
    "B_per_core": 2,
    "collective": True,
    "gelu_lut": False,
}

EPS = 1e-5



ENGINES = ("tensor", "vector", "scalar", "gpsimd", "sync")


class Prog:

    def __init__(self):
        self.ops = []
        self.last_w = {}
        self.readers = {}

    def add(self, engine, emit, reads=(), writes=(), dma=False, tag=""):
        assert engine in ENGINES, engine
        idx = len(self.ops)
        deps = set()
        for r in reads:
            w = self.last_w.get(r)
            if w is not None:
                deps.add(w)
        for r in writes:
            w = self.last_w.get(r)
            if w is not None:
                deps.add(w)
            deps.update(self.readers.get(r, ()))
        deps.discard(idx)
        self.ops.append(dict(i=idx, engine=engine, emit=emit, dma=dma,
                             tag=tag, deps=deps, barrier=False))
        for r in reads:
            self.readers.setdefault(r, set()).add(idx)
        for r in writes:
            self.last_w[r] = idx
            self.readers[r] = {idx}
        return idx

    def barrier(self):
        self.ops.append(dict(i=len(self.ops), engine=None, emit=None, dma=False,
                             tag="BARRIER", deps=set(), barrier=True))

    NDMA_SLOTS = 6

    def finalize_into(self, nc, es):
        R_ = Prog.NDMA_SLOTS
        sems = {e: es.enter_context(nc.semaphore(f"s_{e}")) for e in ENGINES}
        dsems = {}
        for e in ENGINES:
            for sl in range(R_):
                dsems[(e, sl)] = es.enter_context(nc.semaphore(f"d_{e}{sl}"))
        dsems[("cc", 0)] = es.enter_context(nc.semaphore("d_cc"))

        pos = {}
        cnt = {e: 0 for e in ENGINES}
        dcnt = {e: 0 for e in ENGINES}
        duse = {(e, sl): 0 for e in ENGINES for sl in range(R_)}
        per_engine = {e: [] for e in ENGINES}
        pending_bar = None
        bar_seen = set()
        for o in self.ops:
            if o["engine"] is None:
                pending_bar = {"c": dict(cnt), "d": dict(duse)}
                bar_seen = set()
                continue
            e = o["engine"]
            if o["dma"] == "cc":
                duse.setdefault(("cc", 0), 0)
                duse[("cc", 0)] += 1
                o["slot"] = ("cc", 0)
                o["slot_use"] = duse[("cc", 0)]
                pos[o["i"]] = ("D", ("cc", 0), duse[("cc", 0)])
            elif o["dma"]:
                sl = dcnt[e] % R_
                duse[(e, sl)] += 1
                dcnt[e] += 1
                o["slot"] = (e, sl)
                o["slot_use"] = duse[(e, sl)]
                pos[o["i"]] = ("D", (e, sl), duse[(e, sl)] * 16)
            else:
                cnt[e] += 1
                pos[o["i"]] = ("C", e, cnt[e])
            o["bar"] = None
            if pending_bar is not None and e not in bar_seen:
                o["bar"] = pending_bar
                bar_seen.add(e)
                if bar_seen == set(ENGINES):
                    pending_bar = None
            per_engine[e].append(o)

        for e in ENGINES:
            hw_c = {}
            hw_d = {}
            for o in per_engine[e]:
                waits = {}
                if o.get("bar"):
                    for e2, c in o["bar"]["c"].items():
                        if c > 0:
                            waits[("C", e2)] = max(waits.get(("C", e2), 0), c)
                    for key, u in o["bar"]["d"].items():
                        if u > 0:
                            step = 1 if key[0] == "cc" else 16
                            waits[("D", key)] = max(waits.get(("D", key), 0), u * step)
                for dep in o["deps"]:
                    kind, who, dv = pos[dep]
                    if kind == "C" and who == e == "tensor" and not o["dma"]:
                        continue
                    waits[(kind, who)] = max(waits.get((kind, who), 0), dv)
                if o["dma"]:
                    key = ("D", o["slot"])
                    step = 1 if o["dma"] == "cc" else 16
                    need = (o["slot_use"] - 1) * step
                    if need > 0:
                        waits[key] = max(waits.get(key, 0), need)
                o["w_list"] = []
                for (kind, who), v in sorted(waits.items(), key=str):
                    hw = hw_d if kind == "D" else hw_c
                    if v > hw.get(who, 0):
                        hw[who] = v
                        o["w_list"].append((kind, who, v))

        def section(eng_obj, e):
            for o in per_engine[e]:
                for (kind, who, v) in o["w_list"]:
                    eng_obj.wait_ge(dsems[who] if kind == "D" else sems[who], v)
                inst = o["emit"](eng_obj)
                if inst is None:
                    raise RuntimeError(f"op {o['i']} ({o['tag']}) returned None")
                if o["dma"] == "cc":
                    inst.then_inc(dsems[o["slot"]])
                elif o["dma"]:
                    inst.then_inc(dsems[o["slot"]], 16)
                else:
                    inst.then_inc(sems[e], 1)

        with nc.Block() as block:
            block.tensor(lambda t: section(t, "tensor"))
            block.vector(lambda v: section(v, "vector"))
            block.scalar(lambda s: section(s, "scalar"))
            block.gpsimd(lambda g: section(g, "gpsimd"))
            block.sync(lambda sy: section(sy, "sync"))
        return cnt, dcnt


def R(name, i=None):
    return name if i is None else f"{name}:{i}"



def decay_gammas():
    return 1.0 - np.exp(np.linspace(math.log(1 / 32), math.log(1 / 512), NHEAD))


def host_constants():
    gam = decay_gammas()
    a = np.arange(CHUNK, dtype=np.float64)[:, None]
    Gq = np.concatenate([(g ** a) * np.ones((1, DH)) for g in gam], axis=1) / math.sqrt(DH)
    Gk = np.concatenate([(g ** (-a)) * np.ones((1, DH)) for g in gam], axis=1)
    causal = (np.arange(CHUNK)[:, None] <= np.arange(CHUNK)[None, :]).astype(np.float32)
    ltri = (np.arange(128)[:, None] <= np.arange(128)[None, :]).astype(np.float32).T
    ltri = (np.arange(128)[:, None][:, :] <= np.arange(128)[None, :]).astype(np.float32)
    return {
        "cGq": Gq.astype(np.float32),
        "cGk": Gk.astype(np.float32),
        "cCausal": causal,
        "cIdent": np.eye(128, dtype=np.float32),
        "cLtri": ltri.astype(np.float32),
        "cOnes": np.ones((128, 128), dtype=np.float32),
        "cGpc": np.concatenate([
            np.concatenate([np.full((64, 1), g2 ** CHUNK), np.full((64, 1), g21 ** CHUNK)])
            for g2, g21 in zip(gam[0::2], gam[1::2])], axis=1).astype(np.float32),
    }


def fm(vec):
    v = np.asarray(vec, dtype=np.float32)
    k = v.shape[0] // 128
    return np.ascontiguousarray(v.reshape(k, 128).T)


def rowv(vec):
    return np.ascontiguousarray(np.asarray(vec, dtype=np.float32)[None, :])


def tr(w):
    return np.ascontiguousarray(np.asarray(w, dtype=np.float32).T)


def stage_inputs(inputs, stages, T, B_per_core, n_cores):
    x = np.ascontiguousarray(np.asarray(inputs["L0_attn_x"], dtype=np.float32))
    B_total, T_in, D_in = x.shape
    assert D_in == D
    shared = dict(host_constants())
    if "A" in stages:
        shared.update({
            "aQwT": tr(inputs["L0_attn_qw"]), "aKwT": tr(inputs["L0_attn_kw"]),
            "aVwT": tr(inputs["L0_attn_vw"]), "aOwT": tr(inputs["L0_attn_ow"]),
            "aW1T": tr(inputs["L0_attn_w1"]), "aW2T": tr(inputs["L0_attn_w2"]),
            "aGn1g": fm(inputs["L0_attn_norm_g"]), "aGn1b": fm(inputs["L0_attn_norm_b"]),
            "aGn2g": fm(inputs["L0_attn_ffn_norm_g"]), "aGn2b": fm(inputs["L0_attn_ffn_norm_b"]),
            "aObR": rowv(inputs["L0_attn_ob"]), "aB2R": rowv(inputs["L0_attn_b2"]),
            "aB1F": fm(inputs["L0_attn_b1"]),
        })
    if "B" in stages:
        shared.update({
            "bGwT": tr(inputs["L0_ffn_gate_w"]),
            "bWgT": np.ascontiguousarray(np.asarray(inputs["L0_ffn_w_gate"], np.float32).transpose(0, 2, 1)),
            "bWuT": np.ascontiguousarray(np.asarray(inputs["L0_ffn_w_up"], np.float32).transpose(0, 2, 1)),
            "bWdT": np.ascontiguousarray(np.asarray(inputs["L0_ffn_w_down"], np.float32).transpose(0, 2, 1)),
        })
    if "C" in stages:
        shared.update({
            "cWqT": tr(inputs["L1_attn_Wq"]), "cWkT": tr(inputs["L1_attn_Wk"]),
            "cWvT": tr(inputs["L1_attn_Wv"]), "cWpT": tr(inputs["L1_attn_Wp"]),
            "cGq_g": fm(inputs["L1_attn_gq"]), "cGq_b": fm(inputs["L1_attn_bq"]),
            "cGk_g": fm(inputs["L1_attn_gk"]), "cGk_b": fm(inputs["L1_attn_bk"]),
            "cGv_g": fm(inputs["L1_attn_gv"]), "cGv_b": fm(inputs["L1_attn_bv"]),
            "cBpR": rowv(inputs["L1_attn_bp"]),
        })
    if "D" in stages:
        shared.update({
            "dU1": np.ascontiguousarray(np.asarray(inputs["L1_ffn_U1"], np.float32)),
            "dV1": np.ascontiguousarray(np.asarray(inputs["L1_ffn_V1"], np.float32)),
            "dU2": np.ascontiguousarray(np.asarray(inputs["L1_ffn_U2"], np.float32)),
            "dV2": np.ascontiguousarray(np.asarray(inputs["L1_ffn_V2"], np.float32)),
            "dB1F": fm(inputs["L1_ffn_b1"]), "dB2R": rowv(inputs["L1_ffn_b2"]),
        })
    in_maps = []
    for c in range(n_cores):
        m = dict(shared)
        xs = x[c * B_per_core:(c + 1) * B_per_core, :T, :]
        m["x"] = np.ascontiguousarray(xs.reshape(B_per_core * T, D))
        in_maps.append(m)
    return in_maps



class Ctx:
    def __init__(self, nc, prog, es, cfg):
        self.nc = nc
        self.prog = prog
        self.es = es
        self.cfg = cfg
        self.T = cfg["T"]
        self.B = cfg["B_per_core"]
        self.NT = self.B * self.T // 128
        self.TPB = self.T // 128
        self.sb = {}
        self.dram = {}

    def inp(self, name, shape):
        t = self.nc.dram_tensor(name, list(shape), F32, kind="ExternalInput")
        self.dram[name] = t
        return t

    def sbuf(self, name, shape, dtype):
        es = getattr(self, "cur_es", None) or self.es
        t = es.enter_context(self.nc.sbuf_tensor(name, list(shape), dtype))
        self.sb[name] = t
        return t

    def psum(self, name, shape, dtype=F32):
        es = getattr(self, "cur_es", None) or self.es
        t = es.enter_context(self.nc.psum_tensor(name, list(shape), dtype))
        self.sb[name] = t
        return t

    def load_weight_fm(self, dst_sb, src_dram, n_ptiles, cols, res, srcsl=None, tag=""):
        src = src_dram.ap() if srcsl is None else srcsl
        src = src.rearrange("(a p) n -> p a n", p=128)
        dst = dst_sb[:, 0:n_ptiles * cols].rearrange("p (a n) -> p a n", a=n_ptiles)
        self.prog.add("gpsimd", lambda g: g.dma_start(out=dst, in_=src),
                      reads=(), writes=[res], dma=True, tag=tag or f"load:{res}")

    def dma_in(self, engine, dst_ap, src_ap, writes, tag):
        self.prog.add(engine, lambda g: g.dma_start(out=dst_ap, in_=src_ap),
                      reads=(), writes=writes, dma=True, tag=tag)

    def dma_out(self, engine, dst_ap, src_ap, reads, writes, tag):
        self.prog.add(engine, lambda g: g.dma_start(out=dst_ap, in_=src_ap),
                      reads=reads, writes=writes, dma=True, tag=tag)


def transpose_tile(ctx, out_sb_ap, in_sb_ap, ptr_ap, ident_ap, reads, writes, pres, tag):
    prog = ctx.prog
    prog.add("tensor", lambda t: t.transpose(ptr_ap, in_sb_ap, ident_ap),
             reads=list(reads) + ["cIdentB"], writes=[pres], tag=tag + ".tr")
    prog.add("vector", lambda v: v.tensor_copy(out_sb_ap, ptr_ap),
             reads=[pres], writes=list(writes), tag=tag + ".ev")



def ln_from_z(ctx, xi, out_ap, out_res, tag):
    prog = ctx.prog
    z = ctx.sb["LNZ"][:, :]
    st = ctx.sb["LNST"]
    mu, ez2, rstd, ssq, nvar = st[:, 0:1], st[:, 1:2], st[:, 2:3], st[:, 3:4], st[:, 4:5]
    prog.add("vector", lambda v: v.reduce_sum(mu, z, axis=AX.X),
             reads=["LNZ"], writes=["LNST"], tag=tag + ".mu")
    prog.add("scalar", lambda s: s.activation(ctx.sb["OUTT"][:, :], z, AF.Square, accum_out=ssq),
             reads=["LNZ", "LNST"], writes=["OUTT", "LNST"], tag=tag + ".sq")
    prog.add("vector", lambda v: v.tensor_scalar(mu, mu, 1.0 / D, None, OP.mult),
             reads=["LNST"], writes=["LNST"], tag=tag + ".m2")
    prog.add("vector", lambda v: v.tensor_scalar(ez2, ssq, 1.0 / D, None, OP.mult),
             reads=["LNST"], writes=["LNST"], tag=tag + ".v1")
    prog.add("vector", lambda v: v.scalar_tensor_tensor(nvar, mu, mu[:, 0:1], ez2, OP.mult, OP.subtract),
             reads=["LNST"], writes=["LNST"], tag=tag + ".v2")
    prog.add("scalar", lambda s: s.activation(rstd, nvar, AF.Sqrt, bias=ctx.sb["CONSTV"][:, 0:1], scale=-1.0),
             reads=["LNST", "CONSTV"], writes=["LNST"], tag=tag + ".rt")
    prog.add("vector", lambda v: v.reciprocal(rstd, rstd),
             reads=["LNST"], writes=["LNST"], tag=tag + ".rc")
    prog.add("vector", lambda v: v.tensor_scalar(out_ap, z, mu, rstd, OP.subtract, OP.mult),
             reads=["LNZ", "LNST"], writes=[out_res], tag=tag + ".o")



def build_stage_D(ctx, is_last):
    nc, prog, cfg = ctx.nc, ctx.prog, ctx.cfg
    es = contextlib.ExitStack()
    ctx.cur_es = es
    with es:
        U1 = ctx.sbuf("dU1s", [128, DP * 256], BF16)
        V1 = ctx.sbuf("dV1s", [128, 2 * 4096], BF16)
        U2 = ctx.sbuf("dU2s", [128, 32 * 256], BF16)
        V2 = ctx.sbuf("dV2s", [128, 2 * 1024], BF16)
        B1F = ctx.sbuf("dB1Fs", [128, 32], F32)
        B2R = ctx.sbuf("dB2Rs", [1, 1024], F32)
        XF = ctx.sbuf("dXF", [128, DP * 512], BF16)
        PT = ctx.sbuf("dPT", [128, 2 * 512], BF16)
        HT = ctx.sbuf("dHT", [128, 32 * 512], BF16)
        QT = ctx.sbuf("dQT", [128, 2 * 512], BF16)
        GS = ctx.sbuf("dGS", [128, 2 * 512], F32)
        T2 = ctx.sbuf("dT2", [128, 2 * 512], F32)
        pTR = ctx.psum("dpTR", [128, 128], BF16)
        pP = [ctx.psum(f"dpP{i}", [128, 512]) for i in range(2)]
        pH = [ctx.psum(f"dpH{i}", [128, 512]) for i in range(2)]
        pO = ctx.psum("dpO", [128, 1024])

        ctx.load_weight_fm(U1, ctx.dram["dU1"], DP, 256, "dU1s")
        ctx.load_weight_fm(V1, ctx.dram["dV1"], 2, 4096, "dV1s")
        ctx.load_weight_fm(U2, ctx.dram["dU2"], 32, 256, "dU2s")
        ctx.load_weight_fm(V2, ctx.dram["dV2"], 2, 1024, "dV2s")
        ctx.dma_in("sync", B1F[:, :], ctx.dram["dB1F"].ap(), ["dB1Fs"], "dB1F")
        ctx.dma_in("sync", B2R[:, :], ctx.dram["dB2R"].ap(), ["dB2Rs"], "dB2R")

        XB = ctx.sb["XB"]
        NS = ctx.TPB // 4
        for b in range(ctx.B):
          load_xb(ctx, b, f"D.{b}")
          for s in range(NS):
            for j in range(4):
                i_loc = s * 4 + j
                xi = b * ctx.TPB + i_loc
                for a in range(DP):
                    transpose_tile(
                        ctx, XF[:, a * 512 + j * 128: a * 512 + (j + 1) * 128],
                        XB[:, i_loc * D + a * 128: i_loc * D + (a + 1) * 128],
                        pTR[:, :], ctx.sb["cIdentB"][:, :],
                        reads=[R("XB", i_loc)], writes=[R("dXF", a)], pres="dpTR",
                        tag=f"D.xf{xi}.{a}")
            for rc in range(2):
                def em(t, rc=rc):
                    for a in range(DP):
                        last = t.matmul(pP[rc][:, :],
                                        U1[:, a * 256 + rc * 128: a * 256 + (rc + 1) * 128],
                                        XF[:, a * 512:(a + 1) * 512],
                                        start=(a == 0), stop=(a == DP - 1))
                    return last
                prog.add("tensor", em,
                         reads=["dU1s"] + [R("dXF", a) for a in range(DP)],
                         writes=[f"dpP{rc}"], tag=f"D.P{s}.{rc}")
                prog.add("scalar", lambda sc, rc=rc: sc.copy(PT[:, rc * 512:(rc + 1) * 512], pP[rc][:, :]),
                         reads=[f"dpP{rc}"], writes=[R("dPT", rc)], tag=f"D.Pe{s}.{rc}")
            for f in range(32):
                pb = pH[f % 2]
                def em(t, f=f, pb=pb):
                    for a in range(2):
                        last = t.matmul(pb[:, :],
                                        V1[:, a * 4096 + f * 128:a * 4096 + (f + 1) * 128],
                                        PT[:, a * 512:(a + 1) * 512],
                                        start=(a == 0), stop=(a == 1))
                    return last
                prog.add("tensor", em, reads=["dV1s", R("dPT", 0), R("dPT", 1)],
                         writes=[f"dpH{f % 2}"], tag=f"D.H{s}.{f}")
                hslice = HT[:, f * 512:(f + 1) * 512]
                bcol = B1F[:, f:f + 1]
                if cfg["gelu_lut"]:
                    prog.add("scalar", lambda sc, pb=pb, hs=hslice, bc=bcol:
                             sc.activation(hs, pb[:, :], AF.Gelu_apprx_tanh, bias=bc, scale=1.0),
                             reads=[f"dpH{f % 2}", "dB1Fs"], writes=[R("dHT", f)],
                             tag=f"D.g{s}.{f}")
                else:
                    par = f % 2
                    gsh = GS[:, par * 512:(par + 1) * 512]
                    t2 = T2[:, par * 512:(par + 1) * 512]
                    prog.add("scalar", lambda sc, pb=pb, bc=bcol, gsh=gsh:
                             sc.activation(gsh, pb[:, :], AF.Identity, bias=bc, scale=1.0),
                             reads=[f"dpH{f % 2}", "dB1Fs"], writes=[R("dGS", par)], tag=f"D.g0.{s}.{f}")
                    prog.add("scalar", lambda sc, pb=pb, gsh=gsh:
                             sc.activation(pb[:, :], gsh, AF.Square),
                             reads=[R("dGS", par)], writes=[f"dpH{f % 2}"], tag=f"D.g1.{s}.{f}")
                    prog.add("vector", lambda v, pb=pb, t2=t2:
                             v.tensor_scalar(t2, pb[:, :], 0.07135481627159174, 1.5957691216057308, OP.mult, OP.add),
                             reads=[f"dpH{f % 2}"], writes=[R("dT2", par)], tag=f"D.g2.{s}.{f}")
                    prog.add("vector", lambda v, t2=t2, gsh=gsh:
                             v.tensor_mul(t2, t2, gsh),
                             reads=[R("dT2", par), R("dGS", par)], writes=[R("dT2", par)], tag=f"D.g3.{s}.{f}")
                    prog.add("scalar", lambda sc, t2=t2:
                             sc.activation(t2, t2, AF.Sigmoid),
                             reads=[R("dT2", par)], writes=[R("dT2", par)], tag=f"D.g4.{s}.{f}")
                    prog.add("vector", lambda v, hs=hslice, t2=t2, gsh=gsh:
                             v.tensor_mul(hs, t2, gsh),
                             reads=[R("dT2", par), R("dGS", par)], writes=[R("dHT", f)], tag=f"D.g5.{s}.{f}")
            for rc in range(2):
                def em(t, rc=rc):
                    for a in range(32):
                        last = t.matmul(pP[rc][:, :],
                                        U2[:, a * 256 + rc * 128:a * 256 + (rc + 1) * 128],
                                        HT[:, a * 512:(a + 1) * 512],
                                        start=(a == 0), stop=(a == 31))
                    return last
                prog.add("tensor", em,
                         reads=["dU2s"] + [R("dHT", a) for a in range(32)],
                         writes=[f"dpP{rc}"], tag=f"D.Q{s}.{rc}")
                prog.add("scalar", lambda sc, rc=rc: sc.copy(QT[:, rc * 512:(rc + 1) * 512], pP[rc][:, :]),
                         reads=[f"dpP{rc}"], writes=[R("dQT", rc)], tag=f"D.Qe{s}.{rc}")
            for j in range(4):
                i_loc = s * 4 + j
                xi = b * ctx.TPB + i_loc
                def em(t, j=j):
                    for half in range(2):
                        for rc in range(2):
                            t.matmul(pO[:, half * 512:(half + 1) * 512],
                                     QT[:, rc * 512 + j * 128: rc * 512 + (j + 1) * 128],
                                     V2[:, rc * 1024 + half * 512: rc * 1024 + (half + 1) * 512],
                                     start=(rc == 0), stop=False)
                        last = t.matmul(pO[:, half * 512:(half + 1) * 512],
                                        ctx.sb["cOnesF"][0:1, 0:128],
                                        B2R[0:1, half * 512:(half + 1) * 512],
                                        start=False, stop=True)
                    return last
                prog.add("tensor", em,
                         reads=[R("dQT", 0), R("dQT", 1), "dV2s", "dB2Rs", "cOnesF"],
                         writes=["dpO"], tag=f"D.o{xi}")
                prog.add("vector", lambda v, xi=xi, i_loc=i_loc: v.tensor_add(
                    ctx.sb["LNZ"][:, :], pO[:, :], XB[:, i_loc * D:(i_loc + 1) * D]),
                    reads=["dpO", R("XB", i_loc)], writes=["LNZ"], tag=f"D.z{xi}")
                emit_ln_out(ctx, xi, is_last, f"D.ln{xi}")


    ctx.cur_es = None

def emit_ln_out(ctx, xi, is_last, tag):
    i = xi % ctx.TPB
    if is_last:
        o = ctx.sb["OUTT"][:, :]
        ln_from_z(ctx, xi, o, "OUTT", tag)
        dst = ctx.dram["xout"].ap()[xi * 128:(xi + 1) * 128, :]
        ctx.dma_out("sync", dst, o, reads=["OUTT"], writes=[R("xout", xi)], tag=tag + ".st")
    else:
        XB = ctx.sb["XB"]
        ln_from_z(ctx, xi, XB[:, i * D:(i + 1) * D], R("XB", i), tag)
        dst = ctx.dram["xres"].ap()[xi * 128:(xi + 1) * 128, :]
        ctx.dma_out("sync", dst, XB[:, i * D:(i + 1) * D], reads=[R("XB", i)],
                    writes=[R("xres", xi)], tag=tag + ".st")
        ctx.xsrc[xi] = "xres"


def load_xb(ctx, b, tag):
    XB = ctx.sb["XB"]
    for i in range(ctx.TPB):
        xi = b * ctx.TPB + i
        if ctx.xsrc[xi] == "x":
            srcap = ctx.dram["x"].ap()[xi * 128:(xi + 1) * 128, :]
            rr = []
        else:
            srcap = ctx.dram["xres"].ap()[xi * 128:(xi + 1) * 128, :]
            rr = [R("xres", xi)]
        ctx.prog.add("gpsimd", lambda g, d=XB[:, i * D:(i + 1) * D], s=srcap: g.dma_start(out=d, in_=s),
                     reads=rr, writes=[R("XB", i)], dma=True, tag=f"{tag}.lx{i}")



def build_program(cfg):
    nc = bass.Bass("TRN2", target_bir_lowering=False)
    es = contextlib.ExitStack()
    prog = Prog()
    ctx = Ctx(nc, prog, es, cfg)
    stages = cfg["stages"]
    NTOK = cfg["B_per_core"] * cfg["T"]

    with es:
        xin = ctx.inp("x", [NTOK, D])
        ctx.dram["xout"] = nc.dram_tensor("xout", [NTOK, D], F32, kind="ExternalOutput")
        for nm, shape in [("cGq", [128, D]), ("cGk", [128, D]), ("cCausal", [128, 128]),
                          ("cIdent", [128, 128]), ("cLtri", [128, 128]), ("cOnes", [128, 128]),
                          ("cGpc", [128, 8])]:
            ctx.inp(nm, shape)
        wshapes = {
            "A": [("aQwT", [D, D]), ("aKwT", [D, D]), ("aVwT", [D, D]), ("aOwT", [D, D]),
                  ("aW1T", [D, 2 * D]), ("aW2T", [2 * D, D]),
                  ("aGn1g", [128, DP]), ("aGn1b", [128, DP]), ("aGn2g", [128, DP]), ("aGn2b", [128, DP]),
                  ("aObR", [1, D]), ("aB2R", [1, D]), ("aB1F", [128, 16])],
            "B": [("bGwT", [D, 8]), ("bWgT", [8, D, 4096]), ("bWuT", [8, D, 4096]),
                  ("bWdT", [8, 4096, D])],
            "C": [("cWqT", [D, D]), ("cWkT", [D, D]), ("cWvT", [D, D]), ("cWpT", [D, D]),
                  ("cGq_g", [128, DP]), ("cGq_b", [128, DP]), ("cGk_g", [128, DP]),
                  ("cGk_b", [128, DP]), ("cGv_g", [128, DP]), ("cGv_b", [128, DP]),
                  ("cBpR", [1, D])],
            "D": [("dU1", [D, 256]), ("dV1", [256, 4096]), ("dU2", [4096, 256]),
                  ("dV2", [256, D]), ("dB1F", [128, 32]), ("dB2R", [1, D])],
        }
        for st in stages:
            for nm, shape in wshapes[st]:
                ctx.inp(nm, shape)

        XB = ctx.sbuf("XB", [128, ctx.TPB * D], BF16)
        ctx.dram["xres"] = nc.dram_tensor("xres", [NTOK, D], BF16)
        ctx.xsrc = ["x"] * ctx.NT
        ctx.sbuf("LNZ", [128, D], F32)
        ctx.sbuf("LNST", [128, 8], F32)
        ctx.sbuf("OUTT", [128, D], F32)
        identb = ctx.sbuf("cIdentB", [128, 128], BF16)
        constv = ctx.sbuf("CONSTV", [128, 4], F32)
        prog.add("vector", lambda v: v.memset(constv[:, 0:1], float(EPS)), writes=["CONSTV"], tag="c_eps")
        onesf = ctx.sbuf("cOnesF", [128, 128], F32)
        onescb = ctx.sbuf("cOnesCB", [128, 1], BF16)

        ctx.dma_in("gpsimd", identb[:, :], ctx.dram["cIdent"].ap(), ["cIdentB"], "ldid")
        ctx.dma_in("sync", onesf[:, :], ctx.dram["cOnes"].ap(), ["cOnesF"], "ldon")
        prog.add("vector", lambda v: v.memset(onescb[:, :], 1.0), writes=["cOnesCB"], tag="onescb")

        for si, st in enumerate(stages):
            is_last = (si == len(stages) - 1)
            if st == "A":
                build_stage_A(ctx, is_last)
            elif st == "B":
                build_stage_B(ctx, is_last)
            elif st == "C":
                build_stage_C(ctx, is_last)
            elif st == "D":
                build_stage_D(ctx, is_last)
            prog.barrier()

        prog.add("sync", lambda sy: sy.nop(), reads=[R("xout", i) for i in range(ctx.NT)],
                 tag="final")
        cnt, dcnt = prog.finalize_into(nc, es)
    return nc


def groupnorm_coefs(ctx, src_name, nt0, g_sb, b_sb, af_ap, bf_ap, af_res, bf_res,
                    psU, psQ, psG, psU_res, psQ_res, sus, sqs, grs, gsm, sqscr, onesc_bf, tag):
    prog, T = ctx.prog, ctx.T
    src = ctx.sb[src_name]
    ntiles = ctx.TPB
    for i in range(ntiles):
        xi = nt0 + i
        def emS(t, i=i, xi=xi):
            for half in range(2):
                last = t.matmul(psU[0:1, half * 512:(half + 1) * 512], onesc_bf[:, 0:1],
                                src[:, xi * D + half * 512: xi * D + (half + 1) * 512],
                                start=(i == 0), stop=(i == ntiles - 1))
            return last
        prog.add("tensor", emS, reads=[R(src_name, xi), "cOnesCB"], writes=[psU_res],
                 tag=f"{tag}.s{i}")
        prog.add("scalar", lambda s, xi=xi: s.activation(
            sqscr[:, :], src[:, xi * D:(xi + 1) * D], AF.Square),
            reads=[R(src_name, xi)], writes=["GSQs"], tag=f"{tag}.q{i}")
        def emQ(t, i=i):
            for half in range(2):
                last = t.matmul(psQ[0:1, half * 512:(half + 1) * 512], ctx.sb["cOnesF"][:, 0:1],
                                sqscr[:, half * 512:(half + 1) * 512],
                                start=(i == 0), stop=(i == ntiles - 1))
            return last
        prog.add("tensor", emQ, reads=["GSQs", "cOnesF"], writes=[psQ_res],
                 tag=f"{tag}.qs{i}")
    grrow = ctx.sb["GRROW"]
    prog.add("vector", lambda v: v.tensor_copy(grrow[0:1, :], psU[0:1, :]),
             reads=[psU_res], writes=["GRROW"], tag=f"{tag}.cU")
    prog.add("vector", lambda v: v.reduce_sum(
        grs[0:1, 0:16], grrow[0:1, :].rearrange("p (a hi c) -> p hi a c", hi=2, c=64),
        axis=AX.X), reads=["GRROW"], writes=["GRS"], tag=f"{tag}.rU")
    prog.add("vector", lambda v: v.tensor_copy(grrow[0:1, :], psQ[0:1, :]),
             reads=[psQ_res], writes=["GRROW"], tag=f"{tag}.cQ")
    prog.add("vector", lambda v: v.reduce_sum(
        grs[0:1, 16:32], grrow[0:1, :].rearrange("p (a hi c) -> p hi a c", hi=2, c=64),
        axis=AX.X), reads=["GRROW"], writes=["GRS"], tag=f"{tag}.rQ")
    ONES = ctx.sb["cOnesF"]
    kk = 1.0 / (64.0 * T)
    prog.add("vector", lambda v: v.tensor_scalar(gsm[0:1, 0:16], grs[0:1, 0:16], kk, None, OP.mult),
             reads=["GRS"], writes=["GSM"], tag=f"{tag}.k1")
    prog.add("vector", lambda v: v.tensor_scalar(gsm[0:1, 16:32], grs[0:1, 16:32], kk, None, OP.mult),
             reads=["GRS"], writes=["GSM"], tag=f"{tag}.k2")
    prog.add("vector", lambda v: v.tensor_mul(gsm[0:1, 32:48], gsm[0:1, 0:16], gsm[0:1, 0:16]),
             reads=["GSM"], writes=["GSM"], tag=f"{tag}.m2")
    prog.add("vector", lambda v: v.tensor_sub(gsm[0:1, 48:64], gsm[0:1, 16:32], gsm[0:1, 32:48]),
             reads=["GSM"], writes=["GSM"], tag=f"{tag}.var")
    prog.add("scalar", lambda s: s.activation(gsm[0:1, 32:48], gsm[0:1, 48:64], AF.Sqrt,
                                              bias=ctx.sb["CONSTV"][0:1, 0:1]),
             reads=["GSM", "CONSTV"], writes=["GSM"], tag=f"{tag}.sq")
    prog.add("vector", lambda v: v.reciprocal(gsm[0:1, 32:48], gsm[0:1, 32:48]),
             reads=["GSM"], writes=["GSM"], tag=f"{tag}.rc")
    def emB(t):
        t.matmul(psU[0:64, 0:8], ONES[0:1, 0:64], gsm[0:1, 32:40], start=True, stop=True)
        t.matmul(psU[64:128, 0:8], ONES[0:1, 0:64], gsm[0:1, 40:48], start=True, stop=True)
        t.matmul(psQ[0:64, 0:8], ONES[0:1, 0:64], gsm[0:1, 0:8], start=True, stop=True)
        return t.matmul(psQ[64:128, 0:8], ONES[0:1, 0:64], gsm[0:1, 8:16], start=True, stop=True)
    prog.add("tensor", emB, reads=["GSM", "cOnesF"], writes=[psU_res, psQ_res], tag=f"{tag}.bc")
    prog.add("vector", lambda v: v.tensor_mul(af_ap, psU[:, 0:8], g_sb[:, :]),
             reads=[psU_res, f"{tag}.gsb"], writes=[af_res], tag=f"{tag}.af")
    prog.add("vector", lambda v: v.tensor_mul(ctx.sb["GTMP"][:, 0:8], psQ[:, 0:8], af_ap),
             reads=[psQ_res, af_res], writes=["GTMP"], tag=f"{tag}.mb")
    prog.add("vector", lambda v: v.tensor_sub(bf_ap, b_sb[:, :], ctx.sb["GTMP"][:, 0:8]),
             reads=["GTMP", f"{tag}.bsb"], writes=[bf_res], tag=f"{tag}.bf")


def norm_transpose_tile(ctx, src_name, xi, dst_ap_fn, dst_res, af_ap, af_res, bf_ap, bf_res,
                        ptr, ptr_res, tag):
    prog = ctx.prog
    src = ctx.sb[src_name]
    for a in range(DP):
        prog.add("tensor", lambda t, a=a, xi=xi: t.transpose(
            ptr[:, :], src[:, xi * D + a * 128: xi * D + (a + 1) * 128], ctx.sb["cIdentB"][:, :]),
            reads=[R(src_name, xi), "cIdentB"], writes=[ptr_res], tag=f"{tag}.t{a}")
        prog.add("vector", lambda v, a=a: v.tensor_scalar(
            dst_ap_fn(a), ptr[:, :], af_ap[:, a:a + 1], bf_ap[:, a:a + 1], OP.mult, OP.add),
            reads=[ptr_res, af_res, bf_res], writes=[dst_res], tag=f"{tag}.n{a}")


def build_stage_A(ctx, is_last):
    nc, prog, cfg = ctx.nc, ctx.prog, ctx.cfg
    es = contextlib.ExitStack()
    TPB = ctx.TPB
    ctx.cur_es = es
    with es:
        WB = ctx.sbuf("aWB", [128, 4 * 8192], BF16)
        XMB = ctx.sbuf("aXMB", [128, TPB * D], BF16)
        GQ = ctx.sbuf("aGQ", [128, D], BF16)
        GK = ctx.sbuf("aGK", [128, D], BF16)
        GPC = ctx.sbuf("aGPC", [128, 8], F32)
        CAUS = ctx.sbuf("aCAUS", [128, 128], F32)
        G1 = ctx.sbuf("aG1", [128, DP], F32); B1 = ctx.sbuf("aB1", [128, DP], F32)
        G2 = ctx.sbuf("aG2", [128, DP], F32); B2 = ctx.sbuf("aB2", [128, DP], F32)
        AF1 = ctx.sbuf("aAF1", [128, DP], F32); BF1 = ctx.sbuf("aBF1", [128, DP], F32)
        AF2 = ctx.sbuf("aAF2", [128, DP], F32); BF2 = ctx.sbuf("aBF2", [128, DP], F32)
        GTMP = ctx.sbuf("GTMP", [128, 8], F32)
        OBR = ctx.sbuf("aOBR", [1, D], F32)
        B2R = ctx.sbuf("aB2R2", [1, D], F32)
        B1F = ctx.sbuf("aB1F2", [128, 16], F32)
        XNF = ctx.sbuf("aXNF", [128, D], BF16)
        QTK = ctx.sbuf("aQTK", [128, D], BF16)
        KTK = ctx.sbuf("aKTK", [128, D], BF16)
        VTK = ctx.sbuf("aVTK", [128, D], BF16)
        QF = ctx.sbuf("aQF", [128, D], BF16)
        KF = ctx.sbuf("aKF", [128, D], BF16)
        YF = ctx.sbuf("aYF", [128, D], BF16)
        SIMB = [ctx.sbuf(f"aSIMB{i}", [128, 128], BF16) for i in range(2)]
        SST = ctx.sbuf("aSST", [128, 8 * DH], BF16)
        RL = ctx.sbuf("aRL", [128, 8], F32)
        GSQs = ctx.sbuf("GSQs", [128, D], F32)
        GSUS = ctx.sbuf("GSUS", [128, 8], F32)
        GSQS = ctx.sbuf("GSQS", [128, 8], F32)
        GRS = ctx.sbuf("GRS", [1, 64], F32)
        ctx.sbuf("GRROW", [1, 1024], F32)
        GSM = ctx.sbuf("GSM", [1, 64], F32)
        XN2F = ctx.sbuf("aXN2F", [128, DP * 512], BF16)
        H1F = ctx.sbuf("aH1F", [128, 16 * 512], BF16)
        pTR = ctx.psum("apTR", [128, 128], BF16)
        pQ = ctx.psum("apQ", [128, D])
        pK = ctx.psum("apK", [128, D])
        pV = ctx.psum("apV", [128, D])
        PSM = ctx.psum("aPSM", [128, 512])
        pSIM = [pQ[:, 0:128], pK[:, 0:128]]
        pYH = pV[:, 0:128]
        pM = PSM[:, 0:64]
        pG = PSM[0:1, 64:96]
        pH1 = [pQ[:, 0:512], pK[:, 0:512]]

        def load_qkvo():
            for r, nm in enumerate(["aQwT", "aKwT", "aVwT", "aOwT"]):
                src = ctx.dram[nm].ap().rearrange("(a p) n -> p a n", p=128)
                dst = WB[:, r * 8192:(r + 1) * 8192].rearrange("p (a n) -> p a n", a=DP)
                prog.add("gpsimd", lambda g, dst=dst, src=src: g.dma_start(out=dst, in_=src),
                         writes=[f"WBr{r}"], dma=True, tag=f"A.ld{nm}")

        def load_w12():
            src = ctx.dram["aW1T"].ap().rearrange("(a p) n -> p a n", p=128)
            dst = WB[:, 0:16384].rearrange("p (a n) -> p a n", a=DP)
            prog.add("gpsimd", lambda g: g.dma_start(out=dst, in_=src),
                     writes=["WBr0", "WBr1"], dma=True, tag="A.ldw1")
            src2 = ctx.dram["aW2T"].ap().rearrange("(a p) n -> p a n", p=128)
            dst2 = WB[:, 16384:32768].rearrange("p (a n) -> p a n", a=16)
            prog.add("gpsimd", lambda g: g.dma_start(out=dst2, in_=src2),
                     writes=["WBr2", "WBr3"], dma=True, tag="A.ldw2")

        QWT = lambda: WB[:, 0:8192]
        KWT = lambda: WB[:, 8192:16384]
        VWT = lambda: WB[:, 16384:24576]
        OWT = lambda: WB[:, 24576:32768]
        W1T = lambda: WB[:, 0:16384]
        W2T = lambda: WB[:, 16384:32768]

        for nm, dst, res in [("cGq", GQ, "aGQ"), ("cGk", GK, "aGK")]:
            ctx.dma_in("gpsimd", dst[:, :], ctx.dram[nm].ap(), [res], f"A.{nm}")
        ctx.dma_in("sync", CAUS[:, :], ctx.dram["cCausal"].ap(), ["aCAUS"], "A.caus")
        ctx.dma_in("sync", GPC[:, :], ctx.dram["cGpc"].ap(), ["aGPC"], "A.gpc")
        for nm, dst, res in [("aGn1g", G1, "A.gn1.gsb"), ("aGn1b", B1, "A.gn1.bsb"),
                             ("aGn2g", G2, "A.gn2.gsb"), ("aGn2b", B2, "A.gn2.bsb"),
                             ("aObR", OBR, "aOBR"), ("aB2R", B2R, "aB2R2"), ("aB1F", B1F, "aB1F2")]:
            ctx.dma_in("sync", dst[:, :], ctx.dram[nm].ap(), [res], f"A.{nm}")

        XB = ctx.sb["XB"]

        for b in range(ctx.B):
            nt0 = b * TPB
            load_xb(ctx, b, f"A.{b}")
            load_qkvo()
            groupnorm_coefs(ctx, "XB", 0, G1, B1, AF1[:, :], BF1[:, :], "aAF1", "aBF1",
                            pQ, pK, pG, "apQ", "apK", GSUS, GSQS, GRS, GSM, GSQs,
                            ctx.sb["cOnesCB"], "A.gn1")
            prog.add("vector", lambda v: v.memset(SST[:, :], 0.0), writes=["aSST"], tag="A.sst0")
            for i in range(TPB):
                xi = nt0 + i
                norm_transpose_tile(ctx, "XB", i, lambda a: XNF[:, a * 128:(a + 1) * 128],
                                    "aXNF", AF1, "aAF1", BF1, "aBF1", pTR, "apTR", f"A.xn{xi}")
                for nm, ps, psres, wfn, wres in [("q", pQ, "apQ", QWT, "WBr0"),
                                                 ("k", pK, "apK", KWT, "WBr1"),
                                                 ("v", pV, "apV", VWT, "WBr2")]:
                    def emP(t, ps=ps, wfn=wfn):
                        for half in range(2):
                            for a in range(DP):
                                last = t.matmul(ps[:, half * 512:(half + 1) * 512],
                                                XNF[:, a * 128:(a + 1) * 128],
                                                wfn()[:, a * 1024 + half * 512: a * 1024 + (half + 1) * 512],
                                                start=(a == 0), stop=(a == DP - 1))
                        return last
                    prog.add("tensor", emP, reads=["aXNF", wres], writes=[psres],
                             tag=f"A.{nm}p{xi}")
                for nm, ps, psres, col, gg, ggres, dst, dres in [
                        ("q", pQ, "apQ", 0, GQ, "aGQ", QTK, "aQTK"),
                        ("k", pK, "apK", 3, GK, "aGK", KTK, "aKTK")]:
                    prog.add("scalar", lambda s, ps=ps, col=col: s.activation(
                        GSQs[:, :], ps[:, :], AF.Square, accum_out=RL[:, col:col + 1]),
                        reads=[psres], writes=["GSQs", "aRL"], tag=f"A.n2{nm}{xi}")
                    prog.add("vector", lambda v, col=col: v.tensor_scalar_max(
                        RL[:, col + 1:col + 2], RL[:, col:col + 1], 1e-24),
                        reads=["aRL"], writes=["aRL"], tag=f"A.mx{nm}{xi}")
                    prog.add("scalar", lambda s, col=col: s.activation(
                        RL[:, col + 1:col + 2], RL[:, col + 1:col + 2], AF.Sqrt),
                        reads=["aRL"], writes=["aRL"], tag=f"A.sq{nm}{xi}")
                    prog.add("vector", lambda v, col=col: v.reciprocal(
                        RL[:, col + 2:col + 3], RL[:, col + 1:col + 2]),
                        reads=["aRL"], writes=["aRL"], tag=f"A.rc{nm}{xi}")
                    prog.add("vector", lambda v, ps=ps, col=col, gg=gg, dst=dst: v.scalar_tensor_tensor(
                        dst[:, :], ps[:, :], RL[:, col + 2:col + 3], gg[:, :], OP.mult, OP.mult),
                        reads=[psres, "aRL", ggres], writes=[dres], tag=f"A.sc{nm}{xi}")
                prog.add("scalar", lambda s: s.copy(VTK[:, :], pV[:, :]),
                         reads=["apV"], writes=["aVTK"], tag=f"A.ve{xi}")
                for srcn, sres, dstn, dres in [("aQTK", "aQTK", QF, "aQF"), ("aKTK", "aKTK", KF, "aKF")]:
                    srcb = ctx.sb[srcn]
                    for a in range(DP):
                        prog.add("tensor", lambda t, a=a, srcb=srcb: t.transpose(
                            pTR[:, :], srcb[:, a * 128:(a + 1) * 128], ctx.sb["cIdentB"][:, :]),
                            reads=[sres, "cIdentB"], writes=["apTR"], tag=f"A.tf{srcn}{xi}{a}")
                        prog.add("vector", lambda v, a=a, dstn=dstn: v.tensor_copy(
                            dstn[:, a * 128:(a + 1) * 128], pTR[:, :]),
                            reads=["apTR"], writes=[dres], tag=f"A.te{srcn}{xi}{a}")
                for h in range(NHEAD):
                    j = h // 2
                    ro = (h % 2) * 64
                    sim = pSIM[h % 2]
                    simres = "apQ" if h % 2 == 0 else "apK"
                    prog.add("tensor", lambda t, sim=sim, h=h, j=j, ro=ro: t.matmul(
                        sim[:, :], KF[ro:ro + 64, j * 128:(j + 1) * 128],
                        QF[ro:ro + 64, j * 128:(j + 1) * 128], start=True, stop=True),
                        reads=["aQF", "aKF"], writes=[simres], tag=f"A.sim{xi}.{h}")
                    sb_ = SIMB[h % 2]
                    prog.add("vector", lambda v, sim=sim, sb_=sb_: v.tensor_mul(
                        sb_[:, :], sim[:, :], CAUS[:, :]),
                        reads=[simres, "aCAUS"], writes=[f"aSIMB{h % 2}"], tag=f"A.sm{xi}.{h}")
                    def emY(t, h=h, j=j, ro=ro, sb_=sb_):
                        t.matmul(pYH[ro:ro + 64, :], VTK[:, h * 64:(h + 1) * 64], sb_[:, :],
                                 start=True, stop=False)
                        return t.matmul(pYH[ro:ro + 64, :],
                                        SST[ro:ro + 64, j * 64:(j + 1) * 64],
                                        QF[ro:ro + 64, j * 128:(j + 1) * 128],
                                        start=False, stop=True)
                    prog.add("tensor", emY, reads=["aVTK", f"aSIMB{h % 2}", "aSST", "aQF"],
                             writes=["apV"], tag=f"A.yh{xi}.{h}")
                    prog.add("tensor", lambda t, h=h, ro=ro: t.matmul(
                        pM[ro:ro + 64, :], KTK[:, h * 64:(h + 1) * 64],
                        VTK[:, h * 64:(h + 1) * 64], start=True, stop=True),
                        reads=["aKTK", "aVTK"], writes=["aPSM"], tag=f"A.m{xi}.{h}")
                    if h % 2 == 1:
                        prog.add("scalar", lambda s, j=j: s.copy(
                            YF[:, j * 128:(j + 1) * 128], pYH[:, :]),
                            reads=["apV"], writes=["aYF"], tag=f"A.ye{xi}.{j}")
                        prog.add("vector", lambda v, j=j: v.tensor_add(
                            SST[:, j * 64:(j + 1) * 64], SST[:, j * 64:(j + 1) * 64], pM[:, :]),
                            reads=["aPSM", "aSST"], writes=["aSST"], tag=f"A.su{xi}.{j}")
                        prog.add("vector", lambda v, j=j: v.tensor_scalar(
                            SST[:, j * 64:(j + 1) * 64], SST[:, j * 64:(j + 1) * 64],
                            GPC[:, j:j + 1], None, OP.mult),
                            reads=["aSST", "aGPC"], writes=["aSST"], tag=f"A.sg{xi}.{j}")
                def emO(t):
                    for half in range(2):
                        for a in range(DP):
                            t.matmul(pQ[:, half * 512:(half + 1) * 512],
                                     YF[:, a * 128:(a + 1) * 128],
                                     OWT()[:, a * 1024 + half * 512: a * 1024 + (half + 1) * 512],
                                     start=(a == 0), stop=False)
                        last = t.matmul(pQ[:, half * 512:(half + 1) * 512],
                                        ctx.sb["cOnesF"][0:1, 0:128],
                                        OBR[0:1, half * 512:(half + 1) * 512],
                                        start=False, stop=True)
                    return last
                prog.add("tensor", emO, reads=["aYF", "WBr3", "aOBR", "cOnesF"],
                         writes=["apQ"], tag=f"A.op{xi}")
                prog.add("vector", lambda v, i=i, xi=xi: v.tensor_add(
                    XMB[:, i * D:(i + 1) * D], pQ[:, :], XB[:, i * D:(i + 1) * D]),
                    reads=["apQ", R("XB", i)], writes=[R("aXMB", i)], tag=f"A.xm{xi}")
            load_w12()
            groupnorm_coefs(ctx, "aXMB", 0, G2, B2, AF2[:, :], BF2[:, :], "aAF2", "aBF2",
                            pQ, pK, pG, "apQ", "apK", GSUS, GSQS, GRS, GSM, GSQs,
                            ctx.sb["cOnesCB"], "A.gn2")
            for sl in range(TPB // 4):
                for jj in range(4):
                    i = sl * 4 + jj
                    norm_transpose_tile(ctx, "aXMB", i,
                                        lambda a, jj=jj: XN2F[:, a * 512 + jj * 128: a * 512 + (jj + 1) * 128],
                                        "aXN2F", AF2, "aAF2", BF2, "aBF2", pTR, "apTR",
                                        f"A.x2{b}.{i}")
                for oc in range(16):
                    ph = pH1[oc % 2]
                    def emH(t, oc=oc, ph=ph):
                        for a in range(DP):
                            last = t.matmul(ph[:, :],
                                            W1T()[:, a * 2048 + oc * 128: a * 2048 + (oc + 1) * 128],
                                            XN2F[:, a * 512:(a + 1) * 512],
                                            start=(a == 0), stop=(a == DP - 1))
                        return last
                    prog.add("tensor", emH, reads=["aXN2F", "WBr0", "WBr1"],
                             writes=["apQ" if oc % 2 == 0 else "apK"],
                             tag=f"A.h1{b}.{sl}.{oc}")
                    prog.add("scalar", lambda s, oc=oc, ph=ph: s.activation(
                        H1F[:, oc * 512:(oc + 1) * 512], ph[:, :], AF.Silu,
                        bias=B1F[:, oc:oc + 1], scale=1.0),
                        reads=["apQ" if oc % 2 == 0 else "apK", "aB1F2"],
                        writes=[R("aH1F", oc)], tag=f"A.si{b}.{sl}.{oc}")
                for jj in range(4):
                    i = sl * 4 + jj
                    xi = nt0 + i
                    def emF(t, jj=jj):
                        for half in range(2):
                            for oc in range(16):
                                t.matmul(pV[:, half * 512:(half + 1) * 512],
                                         H1F[:, oc * 512 + jj * 128: oc * 512 + (jj + 1) * 128],
                                         W2T()[:, oc * 1024 + half * 512: oc * 1024 + (half + 1) * 512],
                                         start=(oc == 0), stop=False)
                            last = t.matmul(pV[:, half * 512:(half + 1) * 512],
                                            ctx.sb["cOnesF"][0:1, 0:128],
                                            B2R[0:1, half * 512:(half + 1) * 512],
                                            start=False, stop=True)
                        return last
                    prog.add("tensor", emF,
                             reads=[R("aH1F", oc) for oc in range(16)] + ["WBr2", "WBr3", "aB2R2", "cOnesF"],
                             writes=["apV"], tag=f"A.w2{b}.{sl}.{jj}")
                    prog.add("vector", lambda v, i=i, xi=xi: v.tensor_add(
                        ctx.sb["LNZ"][:, :], pV[:, :], XB[:, i * D:(i + 1) * D]),
                        reads=["apV", R("XB", i)], writes=["LNZ"], tag=f"A.z1{xi}")
                    prog.add("vector", lambda v, i=i: v.tensor_add(
                        ctx.sb["LNZ"][:, :], ctx.sb["LNZ"][:, :], XMB[:, i * D:(i + 1) * D]),
                        reads=["LNZ", R("aXMB", i)], writes=["LNZ"], tag=f"A.z2{xi}")
                    emit_ln_out(ctx, xi, is_last, f"A.ln{xi}")


    ctx.cur_es = None

def build_stage_B(ctx, is_last):
    nc, prog, cfg = ctx.nc, ctx.prog, ctx.cfg
    es = contextlib.ExitStack()
    TPB = ctx.TPB
    T = ctx.T
    K = T // 4
    NRS = K // 128
    RG = max(1, K // 256)
    RPG = K // RG
    NSG = RPG // 128
    NBIS = 36
    ctx.cur_es = es
    with es:
        GWTs = ctx.sbuf("bGWTs", [128, DP * 8], BF16)
        XF1 = ctx.sbuf("bXF1", [128, DP * 128], BF16)
        EXPT = ctx.sbuf("bEXPT", [128, 8], F32)
        RLB = ctx.sbuf("bRLB", [128, 4], F32)
        AFFT = ctx.sbuf("bAFFT", [128, ctx.B * TPB * 8], F32)
        AFFB = ctx.sbuf("bAFFB", [128, ctx.B * TPB * 8], BF16)
        AFFE = ctx.sbuf("bAFFE", [64, T], F32)
        THL = ctx.sbuf("bTHL", [64, 8], F32)
        THR = ctx.sbuf("bTHR", [1, 16], F32)
        THBS = ctx.sbuf("bTHBS", [128, 8], F32)
        MSKS = ctx.sbuf("bMSKS", [128, 8], F32)
        RSUM = ctx.sbuf("bRSUM", [1, 8], F32)
        RKT = ctx.sbuf("bRKT", [128, ctx.B * TPB * 8], F32)
        IDENTF = ctx.sbuf("bIDF", [128, 128], F32)
        LTRIS = ctx.sbuf("bLTRI", [128, 128], F32)
        IOTA512 = ctx.sbuf("bIOTA", [128, K], F32)
        PTS = ctx.sbuf("bPTS", [128, TPB * 256], BF16)
        PTB = ctx.sbuf("bPTB", [128, K], BF16)
        PSL = ctx.sbuf("bPSL", [128, NRS * 128], BF16)
        XST = ctx.sbuf("bXST", [128, DP * 512], BF16)
        GCOL = ctx.sbuf("bGCOL", [128, max(NRS, 1)], F32)
        SIL = ctx.sbuf("bSIL", [128, K], BF16)
        HT = ctx.sbuf("bHT", [128, 32 * 512], BF16)
        YB = ctx.sbuf("bYB", [128, 8 * NRS * 1024], BF16)
        WG = ctx.sbuf("bWG", [128, 2 * DP * 512], BF16)
        WU = ctx.sbuf("bWU", [128, 2 * DP * 512], BF16)
        WD = ctx.sbuf("bWD", [128, 2 * 512], BF16)
        pTR = ctx.psum("bpTR", [128, 128], BF16)
        bpA = ctx.psum("bpA", [128, 512])
        bpB = ctx.psum("bpB", [128, 512])
        Y4 = [ctx.psum(f"bY{c}", [128, 512]) for c in "abcd"]
        Y4r = [f"bY{c}" for c in "abcd"]

        XB = ctx.sb["XB"]
        gw_src = ctx.dram["bGwT"].ap().rearrange("(a p) n -> p a n", p=128)
        gw_dst = GWTs[:, :].rearrange("p (a n) -> p a n", a=DP)
        prog.add("gpsimd", lambda g, d=gw_dst, s=gw_src: g.dma_start(out=d, in_=s),
            writes=["bGWTs"], dma=True, tag="B.ldgw")
        ctx.dma_in("sync", IDENTF[:, :], ctx.dram["cIdent"].ap(), ["bIDF"], "B.idf")
        ctx.dma_in("sync", LTRIS[:, :], ctx.dram["cLtri"].ap(), ["bLTRI"], "B.ltri")
        prog.add("gpsimd", lambda g: g.iota(IOTA512[:, :], [[1, K]], base=1,
                                            channel_multiplier=0,
                                            allow_small_or_imprecise_dtypes=True),
                 writes=["bIOTA"], tag="B.iota")
        prog.add("vector", lambda v: v.memset(AFFE[:, :], 0.5), writes=["bAFFE"], tag="B.ae0")

        for b in range(ctx.B):
            load_xb(ctx, b, f"Br.{b}")
            for i in range(TPB):
                xi = b * TPB + i
                ci = (b * TPB + i) * 8
                for a in range(DP):
                    transpose_tile(ctx, XF1[:, a * 128:(a + 1) * 128],
                                   XB[:, i * D + a * 128: i * D + (a + 1) * 128],
                                   pTR[:, :], ctx.sb["cIdentB"][:, :],
                                   reads=[R("XB", i)], writes=["bXF1"], pres="bpTR",
                                   tag=f"B.xf{xi}.{a}")
                def emL(t):
                    for a in range(DP):
                        last = t.matmul(bpA[:, 0:8], XF1[:, a * 128:(a + 1) * 128],
                                        GWTs[:, a * 8:(a + 1) * 8],
                                        start=(a == 0), stop=(a == DP - 1))
                    return last
                prog.add("tensor", emL, reads=["bXF1", "bGWTs"], writes=["bbpA"],
                         tag=f"B.lg{xi}")
                prog.add("vector", lambda v: v.tensor_copy(EXPT[:, :], bpA[:, 0:8]),
                         reads=["bbpA"], writes=["bEXPT"], tag=f"B.sm0{xi}")
                prog.add("vector", lambda v: v.reduce_max(RLB[:, 0:1], EXPT[:, :],
                                                          axis=AX.X, negate=True),
                         reads=["bEXPT"], writes=["bRLB"], tag=f"B.sm1{xi}")
                prog.add("scalar", lambda s: s.activation(EXPT[:, :], EXPT[:, :], AF.Exp,
                                                          bias=RLB[:, 0:1]),
                         reads=["bEXPT", "bRLB"], writes=["bEXPT"], tag=f"B.sm2{xi}")
                prog.add("vector", lambda v: v.reduce_sum(RLB[:, 1:2], EXPT[:, :], axis=AX.X),
                         reads=["bEXPT"], writes=["bRLB"], tag=f"B.sm3{xi}")
                prog.add("vector", lambda v: v.reciprocal(RLB[:, 2:3], RLB[:, 1:2]),
                         reads=["bRLB"], writes=["bRLB"], tag=f"B.sm4{xi}")
                prog.add("vector", lambda v, ci=ci: v.tensor_scalar(
                    AFFT[:, ci:ci + 8], EXPT[:, :], RLB[:, 2:3], None, OP.mult),
                    reads=["bEXPT", "bRLB"], writes=["bAFFT"], tag=f"B.sm5{xi}")
                prog.add("vector", lambda v, ci=ci: v.tensor_copy(
                    AFFB[:, ci:ci + 8], AFFT[:, ci:ci + 8]),
                    reads=["bAFFT"], writes=["bAFFB"], tag=f"B.sm6{xi}")
                prog.add("tensor", lambda t, ci=ci: t.transpose(
                    bpB[0:8, 0:128], AFFT[:, ci:ci + 8], IDENTF[:, :]),
                    reads=["bAFFT", "bIDF"], writes=["bbpB"], tag=f"B.tr{xi}")
                prog.add("vector", lambda v, b=b, i=i: v.tensor_copy(
                    AFFE[b * 32:b * 32 + 8, i * 128:(i + 1) * 128], bpB[0:8, 0:128]),
                    reads=["bbpB"], writes=["bAFFE"], tag=f"B.ae{xi}")
        prog.add("vector", lambda v: v.memset(THL[:, 0:1], 0.0), writes=["bTHL"], tag="B.th0")
        prog.add("vector", lambda v: v.memset(THL[:, 1:2], 1.0),
                 reads=["bTHL"], writes=["bTHL"], tag="B.th1")
        BSS = PTS[0:64, 0:T]
        for it in range(NBIS):
            prog.add("vector", lambda v: v.tensor_add(THL[:, 2:3], THL[:, 0:1], THL[:, 1:2]),
                     reads=["bTHL"], writes=["bTHL"], tag=f"B.b0.{it}")
            prog.add("vector", lambda v: v.tensor_scalar(THL[:, 2:3], THL[:, 2:3], 0.5, None, OP.mult),
                     reads=["bTHL"], writes=["bTHL"], tag=f"B.b1.{it}")
            prog.add("vector", lambda v: v.tensor_scalar(
                BSS, AFFE[:, :], THL[:, 2:3], None, OP.is_ge, OP.add, accum_out=THL[:, 3:4]),
                reads=["bAFFE", "bTHL"], writes=["bPTS", "bTHL"], tag=f"B.b2.{it}")
            prog.add("vector", lambda v: v.tensor_single_scalar(
                THL[:, 4:5], THL[:, 3:4], float(K), OP.is_ge),
                reads=["bTHL"], writes=["bTHL"], tag=f"B.b3.{it}")
            prog.add("vector", lambda v: v.tensor_sub(THL[:, 5:6], THL[:, 2:3], THL[:, 0:1]),
                     reads=["bTHL"], writes=["bTHL"], tag=f"B.b4.{it}")
            prog.add("vector", lambda v: v.scalar_tensor_tensor(
                THL[:, 0:1], THL[:, 5:6], THL[:, 4:5], THL[:, 0:1], OP.mult, OP.add),
                reads=["bTHL"], writes=["bTHL"], tag=f"B.b5.{it}")
            prog.add("vector", lambda v: v.tensor_sub(THL[:, 6:7], THL[:, 1:2], THL[:, 2:3]),
                     reads=["bTHL"], writes=["bTHL"], tag=f"B.b6.{it}")
            prog.add("vector", lambda v: v.scalar_tensor_tensor(
                THL[:, 1:2], THL[:, 6:7], THL[:, 4:5], THL[:, 2:3], OP.mult, OP.add),
                reads=["bTHL"], writes=["bTHL"], tag=f"B.b7.{it}")
        for b in range(ctx.B):
            prog.add("tensor", lambda t, b=b: t.transpose(
                bpB[0:1, 0:8], THL[b * 32:b * 32 + 8, 0:1], IDENTF[b * 32:b * 32 + 8, b * 32:b * 32 + 8]),
                reads=["bTHL", "bIDF"], writes=["bbpB"], tag=f"B.tt{b}")
            prog.add("vector", lambda v, b=b: v.tensor_copy(THR[0:1, b * 8:(b + 1) * 8],
                                                            bpB[0:1, 0:8]),
                     reads=["bbpB"], writes=["bTHR"], tag=f"B.te{b}")
        for b in range(ctx.B):
            prog.add("tensor", lambda t, b=b: t.matmul(
                bpA[:, 0:8], ctx.sb["cOnesF"][0:1, 0:128], THR[0:1, b * 8:(b + 1) * 8],
                start=True, stop=True),
                reads=["bTHR", "cOnesF"], writes=["bbpA"], tag=f"B.tb{b}")
            prog.add("vector", lambda v: v.tensor_copy(THBS[:, :], bpA[:, 0:8]),
                     reads=["bbpA"], writes=["bTHBS"], tag=f"B.tb2{b}")
            prog.add("vector", lambda v: v.memset(RSUM[:, :], 0.0),
                     reads=["bRSUM"], writes=["bRSUM"], tag=f"B.rs0{b}")
            for i in range(TPB):
                ci = (b * TPB + i) * 8
                prog.add("vector", lambda v, ci=ci: v.tensor_tensor(
                    MSKS[:, :], AFFT[:, ci:ci + 8], THBS[:, :], OP.is_ge),
                    reads=["bAFFT", "bTHBS"], writes=["bMSKS"], tag=f"B.mk{b}.{i}")
                def emRk(t):
                    t.matmul(bpA[:, 0:8], LTRIS[:, :], MSKS[:, :], start=True, stop=False)
                    return t.matmul(bpA[:, 0:8], ctx.sb["cOnesF"][0:1, 0:128],
                                    RSUM[0:1, 0:8], start=False, stop=True)
                prog.add("tensor", emRk, reads=["bMSKS", "bRSUM", "bLTRI", "cOnesF"],
                         writes=["bbpA"], tag=f"B.rk{b}.{i}")
                prog.add("vector", lambda v, ci=ci: v.tensor_mul(
                    RKT[:, ci:ci + 8], bpA[:, 0:8], MSKS[:, :]),
                    reads=["bbpA", "bMSKS"], writes=["bRKT"], tag=f"B.rm{b}.{i}")
                prog.add("tensor", lambda t: t.matmul(
                    bpB[0:1, 0:8], ctx.sb["cOnesF"][:, 0:1], MSKS[:, :], start=True, stop=True),
                    reads=["bMSKS", "cOnesF"], writes=["bbpB"], tag=f"B.cs{b}.{i}")
                prog.add("vector", lambda v: v.tensor_add(RSUM[0:1, 0:8], RSUM[0:1, 0:8],
                                                          bpB[0:1, 0:8]),
                         reads=["bbpB", "bRSUM"], writes=["bRSUM"], tag=f"B.ru{b}.{i}")
        for b in range(ctx.B):
            load_xb(ctx, b, f"Be.{b}")
            for e in range(8):
                for rg in range(RG):
                    for i in range(TPB):
                        ci = (b * TPB + i) * 8 + e
                        prog.add("vector", lambda v, i=i, ci=ci, rg=rg: v.tensor_scalar(
                            PTS[:, i * RPG:(i + 1) * RPG], IOTA512[:, rg * RPG:(rg + 1) * RPG],
                            RKT[:, ci:ci + 1], None, OP.is_equal),
                            reads=["bIOTA", "bRKT"], writes=[R("bPTS", i), "bPTS"],
                            tag=f"B.pt{b}.{e}.{rg}.{i}")
                    g4 = [bpA[:, 0:RPG], bpB[:, 0:RPG], Y4[0][:, 0:RPG], Y4[1][:, 0:RPG]]
                    g4res = ["bbpA", "bbpB", Y4r[0], Y4r[1]]
                    for pss in range(2):
                        for i in range(TPB):
                            def emG(t, i=i, pss=pss):
                                for dc in range(4):
                                    last = t.matmul(g4[dc],
                                                    XB[:, i * D + (pss * 4 + dc) * 128: i * D + (pss * 4 + dc + 1) * 128],
                                                    PTS[:, i * RPG:(i + 1) * RPG],
                                                    start=(i == 0), stop=(i == TPB - 1))
                                return last
                            prog.add("tensor", emG,
                                     reads=[R("XB", i), R("bPTS", i)],
                                     writes=list(set(g4res)), tag=f"B.g{pss}{b}.{e}.{rg}.{i}")
                            if pss == 0:
                                def emGC(t, i=i, b=b, e=e):
                                    for rc in range(NSG):
                                        last = t.matmul(Y4[2 + rc][:, 0:1],
                                                        PTS[:, i * RPG + rc * 128: i * RPG + (rc + 1) * 128],
                                                        AFFB[:, (b * TPB + i) * 8 + e:(b * TPB + i) * 8 + e + 1],
                                                        start=(i == 0), stop=(i == TPB - 1))
                                    return last
                                prog.add("tensor", emGC, reads=[R("bPTS", i), "bAFFB"],
                                         writes=[Y4r[2 + rc] for rc in range(NSG)],
                                         tag=f"B.gc{b}.{e}.{rg}.{i}")
                        for dc in range(4):
                            prog.add("scalar", lambda s, dc=dc, pss=pss, rg=rg: s.copy(
                                XST[:, (pss * 4 + dc) * 512 + rg * 256: (pss * 4 + dc) * 512 + rg * 256 + RPG],
                                g4[dc]),
                                reads=[g4res[dc]], writes=["bXST"], tag=f"B.ge{pss}{b}.{e}.{rg}.{dc}")
                    for rc in range(NSG):
                        rsub = rg * NSG + rc
                        prog.add("vector", lambda v, rc=rc, rsub=rsub: v.tensor_copy(
                            GCOL[:, rsub: rsub + 1], Y4[2 + rc][:, 0:1]),
                            reads=[Y4r[2 + rc]], writes=["bGCOL"], tag=f"B.gce{b}.{e}.{rg}.{rc}")
                for ic in range(8):
                    slot = ic % 2
                    for wnm, wsb, wres in [("bWgT", WG, "bWG"), ("bWuT", WU, "bWU")]:
                        srcw = ctx.dram[wnm].ap()[e, :, ic * 512:(ic + 1) * 512].rearrange(
                            "(a p) n -> p a n", p=128)
                        dstw = wsb[:, slot * 4096:(slot + 1) * 4096].rearrange(
                            "p (a n) -> p a n", a=DP)
                        prog.add("gpsimd", lambda g, dst=dstw, src=srcw: g.dma_start(out=dst, in_=src),
                                 writes=[R(wres, slot)], dma=True, tag=f"B.lw{wnm}{b}.{e}.{ic}")
                    for isub in range(4):
                        def emGm(t, ic=ic, isub=isub, slot=slot):
                            for a in range(DP):
                                last = t.matmul(bpA[:, 0:K],
                                                WG[:, slot * 4096 + a * 512 + isub * 128: slot * 4096 + a * 512 + (isub + 1) * 128],
                                                XST[:, a * 512: a * 512 + K],
                                                start=(a == 0), stop=(a == DP - 1))
                            return last
                        prog.add("tensor", emGm, reads=[R("bWG", slot), "bXST"],
                                 writes=["bbpA"], tag=f"B.hg{b}.{e}.{ic}.{isub}")
                        prog.add("scalar", lambda s: s.activation(SIL[:, 0:K], bpA[:, 0:K], AF.Silu),
                                 reads=["bbpA"], writes=["bSIL"], tag=f"B.hs{b}.{e}.{ic}.{isub}")
                        def emUm(t, ic=ic, isub=isub, slot=slot):
                            for a in range(DP):
                                last = t.matmul(bpB[:, 0:K],
                                                WU[:, slot * 4096 + a * 512 + isub * 128: slot * 4096 + a * 512 + (isub + 1) * 128],
                                                XST[:, a * 512: a * 512 + K],
                                                start=(a == 0), stop=(a == DP - 1))
                            return last
                        prog.add("tensor", emUm, reads=[R("bWU", slot), "bXST"],
                                 writes=["bbpB"], tag=f"B.hu{b}.{e}.{ic}.{isub}")
                        hidx = ic * 4 + isub
                        prog.add("vector", lambda v, hidx=hidx: v.tensor_mul(
                            HT[:, hidx * 512: hidx * 512 + K],
                            SIL[:, 0:K], bpB[:, 0:K]),
                            reads=["bSIL", "bbpB"], writes=[R("bHT", hidx)],
                            tag=f"B.hm{b}.{e}.{ic}.{isub}")
                for dh in range(2):
                    for it in range(32):
                        slot = it % 2
                        srcw = ctx.dram["bWdT"].ap()[e, it * 128:(it + 1) * 128, dh * 512:(dh + 1) * 512]
                        prog.add("gpsimd", lambda g, slot=slot, src=srcw: g.dma_start(
                            out=WD[:, slot * 512:(slot + 1) * 512], in_=src),
                            writes=[R("bWD", slot)], dma=True, tag=f"B.lwd{b}.{e}.{dh}.{it}")
                        def emD(t, it=it, slot=slot):
                            for rsub in range(NRS):
                                last = t.matmul(Y4[rsub][:, 0:512],
                                                HT[:, it * 512 + rsub * 128: it * 512 + rsub * 128 + 128],
                                                WD[:, slot * 512:(slot + 1) * 512],
                                                start=(it == 0), stop=(it == 31))
                            return last
                        prog.add("tensor", emD,
                                 reads=[R("bHT", it), R("bWD", slot)],
                                 writes=[Y4r[rsub] for rsub in range(NRS)],
                                 tag=f"B.dn{b}.{e}.{dh}.{it}")
                    for rsub in range(NRS):
                        prog.add("vector", lambda v, rsub=rsub, dh=dh, e=e: v.tensor_scalar(
                            YB[:, (e * NRS + rsub) * 1024 + dh * 512:(e * NRS + rsub) * 1024 + dh * 512 + 512],
                            Y4[rsub][:, 0:512], GCOL[:, rsub: rsub + 1], None, OP.mult),
                            reads=[Y4r[rsub], "bGCOL"],
                            writes=[R("bYB", e * NRS + rsub)], tag=f"B.ye{b}.{e}.{dh}.{rsub}")
            for i in range(TPB):
                xi = b * TPB + i
                for e in range(8):
                    ci = (b * TPB + i) * 8 + e
                    prog.add("vector", lambda v, ci=ci: v.tensor_scalar(
                        PTB[:, :], IOTA512[:, :], RKT[:, ci:ci + 1], None, OP.is_equal),
                        reads=["bIOTA", "bRKT"], writes=["bPTB"], tag=f"B.sp{b}.{i}.{e}")
                    for rsub in range(NRS):
                        prog.add("tensor", lambda t, rsub=rsub: t.transpose(
                            pTR[:, :], PTB[:, rsub * 128:(rsub + 1) * 128], ctx.sb["cIdentB"][:, :]),
                            reads=["bPTB", "cIdentB"], writes=["bpTR"], tag=f"B.st{b}.{i}.{e}.{rsub}")
                        prog.add("vector", lambda v, rsub=rsub: v.tensor_copy(
                            PSL[:, rsub * 128:(rsub + 1) * 128], pTR[:, :]),
                            reads=["bpTR"], writes=["bPSL"], tag=f"B.se{b}.{i}.{e}.{rsub}")
                    def emSc(t, e=e):
                        for rsub in range(NRS):
                            for dh in range(2):
                                last = t.matmul(Y4[dh][:, 0:512],
                                                PSL[:, rsub * 128:(rsub + 1) * 128],
                                                YB[:, (e * NRS + rsub) * 1024 + dh * 512: (e * NRS + rsub) * 1024 + dh * 512 + 512],
                                                start=(e == 0 and rsub == 0), stop=(e == 7 and rsub == NRS - 1))
                        return last
                    prog.add("tensor", emSc,
                             reads=["bPSL"] + [R("bYB", e * NRS + rsub) for rsub in range(NRS)],
                             writes=[Y4r[0], Y4r[1]], tag=f"B.sc{b}.{i}.{e}")
                for dh in range(2):
                    prog.add("vector", lambda v, i=i, dh=dh: v.tensor_add(
                        ctx.sb["LNZ"][:, dh * 512:(dh + 1) * 512], Y4[dh][:, 0:512],
                        XB[:, i * D + dh * 512: i * D + dh * 512 + 512]),
                        reads=[Y4r[dh], R("XB", i)], writes=["LNZ"], tag=f"B.z{xi}.{dh}")
                emit_ln_out(ctx, xi, is_last, f"B.ln{xi}")

    ctx.cur_es = None

def build_stage_C(ctx, is_last):
    nc, prog, cfg = ctx.nc, ctx.prog, ctx.cfg
    es = contextlib.ExitStack()
    TPB = ctx.TPB
    NTOK = ctx.B * ctx.T
    NTOT = cfg["n_cores"] * NTOK if cfg["collective"] else NTOK
    NSL = TPB // 4
    ctx.cur_es = es
    with es:
        WC = ctx.sbuf("cWC", [128, 4 * 8192], BF16)
        XF = ctx.sbuf("cXF", [128, DP * 512], BF16)
        HB = ctx.sbuf("cHB", [128, 512], BF16)
        CSQ = ctx.sbuf("cCSQ", [128, 512], F32)
        SSTAT = ctx.sbuf("cSSTAT", [128, 48], F32)
        GSTAT = ctx.sbuf("cGSTAT", [128, 48], F32)
        RT = ctx.sbuf("cRT", [128, 4], F32)
        CF = ctx.sbuf("cCF", [128, 48], F32)
        A3 = ctx.sbuf("cA3", [128, 24], F32)
        B3 = ctx.sbuf("cB3", [128, 24], F32)
        HSB = ctx.sbuf("cHSB", [128, DP * 512], BF16)
        VF = ctx.sbuf("cVF", [128, DP * 512], BF16)
        QBF = ctx.sbuf("cQBF", [128, DP * 512], BF16)
        KBF = ctx.sbuf("cKBF", [128, DP * 512], BF16)
        QKF = ctx.sbuf("cQKF", [128, DP * 512], BF16)
        OUTF = ctx.sbuf("cOUTF", [128, DP * 512], BF16)
        VTOK = ctx.sbuf("cVTOK", [128, TPB * D], BF16)
        VVS = ctx.sbuf("cVVS", [128, 8 * DH], BF16)
        BPR = ctx.sbuf("cBPR", [1, D], F32)
        pTR = ctx.psum("cpTR", [128, 128], BF16)
        pB0 = ctx.psum("cpB0", [128, 512])
        pB1 = ctx.psum("cpB1", [128, 512])
        pO = ctx.psum("cpO", [128, D])
        hq = nc.dram_tensor("c_hq", [D, NTOK], BF16)
        hk = nc.dram_tensor("c_hk", [D, NTOK], BF16)
        hv = nc.dram_tensor("c_hv", [D, NTOK], BF16)
        ccin = nc.dram_tensor("c_ccin", [128, 48], F32)
        ccout = nc.dram_tensor("c_ccout", [128, 48], F32)

        for r, nm in enumerate(["cWqT", "cWkT", "cWvT", "cWpT"]):
            src = ctx.dram[nm].ap().rearrange("(a p) n -> p a n", p=128)
            dst = WC[:, r * 8192:(r + 1) * 8192].rearrange("p (a n) -> p a n", a=DP)
            prog.add("gpsimd", lambda g, dst=dst, src=src: g.dma_start(out=dst, in_=src),
                     writes=[f"cWC{r}"], dma=True, tag=f"C.ld{nm}")
        PAR = {}
        for nm in ["cGq_g", "cGq_b", "cGk_g", "cGk_b", "cGv_g", "cGv_b"]:
            t = ctx.sbuf(nm + "s", [128, DP], F32)
            ctx.dma_in("sync", t[:, :], ctx.dram[nm].ap(), [nm + "s"], f"C.{nm}")
            PAR[nm] = t
        ctx.dma_in("sync", BPR[:, :], ctx.dram["cBpR"].ap(), ["cBPR"], "C.bp")
        prog.add("vector", lambda v: v.memset(SSTAT[:, :], 0.0), writes=["cSSTAT"], tag="C.st0")

        XB = ctx.sb["XB"]
        branches = [("q", hq, 0, 0), ("k", hk, 1, 1), ("v", hv, 2, 2)]
        for b in range(ctx.B):
            load_xb(ctx, b, f"C1.{b}")
            for s in range(NSL):
                for j in range(4):
                    i_loc = s * 4 + j
                    xi = b * TPB + i_loc
                    for a in range(DP):
                        transpose_tile(
                            ctx, XF[:, a * 512 + j * 128: a * 512 + (j + 1) * 128],
                            XB[:, i_loc * D + a * 128: i_loc * D + (a + 1) * 128],
                            pTR[:, :], ctx.sb["cIdentB"][:, :],
                            reads=[R("XB", i_loc)], writes=[R("cXF", a)], pres="cpTR",
                            tag=f"C.xf{xi}.{a}")
                for bi, (bn, hd, wr, sc0) in enumerate(branches):
                    for oc in range(DP):
                        pb = pB0 if oc % 2 == 0 else pB1
                        pres = "cpB0" if oc % 2 == 0 else "cpB1"
                        def emH(t, wr=wr, oc=oc, pb=pb):
                            for a in range(DP):
                                last = t.matmul(pb[:, :],
                                                WC[:, wr * 8192 + a * 1024 + oc * 128: wr * 8192 + a * 1024 + (oc + 1) * 128],
                                                XF[:, a * 512:(a + 1) * 512],
                                                start=(a == 0), stop=(a == DP - 1))
                            return last
                        prog.add("tensor", emH,
                                 reads=[R("cXF", a) for a in range(DP)] + [f"cWC{wr}"],
                                 writes=[pres], tag=f"C.h{bn}{b}.{s}.{oc}")
                        prog.add("scalar", lambda sc, pb=pb: sc.copy(HB[:, :], pb[:, :]),
                                 reads=[pres], writes=["cHB"], tag=f"C.he{bn}{b}.{s}.{oc}")
                        dst = hd.ap()[oc * 128:(oc + 1) * 128, b * ctx.T + s * 512: b * ctx.T + (s + 1) * 512]
                        ctx.dma_out("sync", dst, HB[:, :], reads=["cHB"],
                                    writes=[R(f"c_h{bn}", (b, s, oc))], tag=f"C.hs{bn}{b}.{s}.{oc}")
                        sumc = SSTAT[:, bi * 16 + oc: bi * 16 + oc + 1]
                        sqc = SSTAT[:, bi * 16 + 8 + oc: bi * 16 + 8 + oc + 1]
                        prog.add("scalar", lambda sc, pb=pb: sc.activation(CSQ[:, :], pb[:, :], AF.Identity,
                                                                           accum_out=RT[:, 0:1]),
                                 reads=[pres, "cRT"], writes=["cCSQ", "cRT"], tag=f"C.r1{bn}{b}.{s}.{oc}")
                        prog.add("vector", lambda v, sumc=sumc: v.tensor_add(sumc, sumc, RT[:, 0:1]),
                                 reads=["cRT", "cSSTAT"], writes=["cSSTAT"], tag=f"C.r2{bn}{b}.{s}.{oc}")
                        prog.add("scalar", lambda sc, pb=pb: sc.activation(CSQ[:, :], pb[:, :], AF.Square,
                                                                           accum_out=RT[:, 1:2]),
                                 reads=[pres, "cRT", "cCSQ"], writes=["cCSQ", "cRT"], tag=f"C.r3{bn}{b}.{s}.{oc}")
                        prog.add("vector", lambda v, sqc=sqc: v.tensor_add(sqc, sqc, RT[:, 1:2]),
                                 reads=["cRT", "cSSTAT"], writes=["cSSTAT"], tag=f"C.r4{bn}{b}.{s}.{oc}")
        if cfg["collective"]:
            ctx.dma_out("gpsimd", ccin.ap(), SSTAT[:, :], reads=["cSSTAT"], writes=["c_ccin"], tag="C.cci")
            prog.add("gpsimd", lambda g: g.collective_compute(
                "AllReduce", OP.add, replica_groups=[list(range(cfg["n_cores"]))],
                ins=[ccin.ap()], outs=[ccout.ap()]),
                reads=["c_ccin"], writes=["c_ccout"], dma="cc", tag="C.cc")
            ctx.dma_out("gpsimd", GSTAT[:, :], ccout.ap(), reads=["c_ccout"],
                        writes=["cGSTAT"], tag="C.cco")
        else:
            prog.add("vector", lambda v: v.tensor_copy(GSTAT[:, :], SSTAT[:, :]),
                     reads=["cSSTAT"], writes=["cGSTAT"], tag="C.cc")
        for bi, (bn, hd, wr, sc0) in enumerate(branches):
            g_sb = PAR[f"cG{bn}_g"]
            b_sb = PAR[f"cG{bn}_b"]
            mu = CF[:, bi * 16:bi * 16 + 8]
            e2 = CF[:, bi * 16 + 8:bi * 16 + 16]
            a3 = A3[:, bi * 8:(bi + 1) * 8]
            b3 = B3[:, bi * 8:(bi + 1) * 8]
            prog.add("vector", lambda v, bi=bi, mu=mu: v.tensor_scalar(
                mu, GSTAT[:, bi * 16:bi * 16 + 8], 1.0 / NTOT, None, OP.mult),
                reads=["cGSTAT"], writes=["cCF"], tag=f"C.c1{bn}")
            prog.add("vector", lambda v, bi=bi, e2=e2: v.tensor_scalar(
                e2, GSTAT[:, bi * 16 + 8:bi * 16 + 16], 1.0 / NTOT, None, OP.mult),
                reads=["cGSTAT"], writes=["cCF"], tag=f"C.c2{bn}")
            prog.add("vector", lambda v, mu=mu, e2=e2, a3=a3: v.tensor_mul(a3, mu, mu),
                     reads=["cCF"], writes=["cA3"], tag=f"C.c3{bn}")
            prog.add("vector", lambda v, e2=e2, a3=a3: v.tensor_sub(a3, e2, a3),
                     reads=["cCF", "cA3"], writes=["cA3"], tag=f"C.c4{bn}")
            prog.add("scalar", lambda s, a3=a3: s.activation(a3, a3, AF.Sqrt,
                                                             bias=ctx.sb["CONSTV"][:, 0:1]),
                     reads=["cA3", "CONSTV"], writes=["cA3"], tag=f"C.c5{bn}")
            prog.add("vector", lambda v, a3=a3: v.reciprocal(a3, a3),
                     reads=["cA3"], writes=["cA3"], tag=f"C.c6{bn}")
            prog.add("vector", lambda v, a3=a3, g_sb=g_sb: v.tensor_mul(a3, a3, g_sb[:, :]),
                     reads=["cA3", f"cG{bn}_gs"], writes=["cA3"], tag=f"C.c7{bn}")
            prog.add("vector", lambda v, b3=b3, mu=mu, a3=a3: v.tensor_mul(b3, mu, a3),
                     reads=["cCF", "cA3"], writes=["cB3"], tag=f"C.c8{bn}")
            prog.add("vector", lambda v, b3=b3, b_sb=b_sb: v.tensor_sub(b3, b_sb[:, :], b3),
                     reads=["cB3", f"cG{bn}_bs"], writes=["cB3"], tag=f"C.c9{bn}")
            prog.add("vector", lambda v, b3=b3: v.tensor_scalar(b3, b3, 4.0, -4.0, OP.mult, OP.add),
                     reads=["cB3"], writes=["cB3"], tag=f"C.cA{bn}")
            prog.add("vector", lambda v, a3=a3: v.tensor_scalar(a3, a3, 4.0, None, OP.mult),
                     reads=["cA3"], writes=["cA3"], tag=f"C.cB{bn}")

        def sig_stream(bn, bi, hd, b, s, dstF, dres):
            src = hd.ap()[:, b * ctx.T + s * 512: b * ctx.T + (s + 1) * 512].rearrange(
                "(a p) n -> p a n", p=128)
            dstH = HSB[:, :].rearrange("p (a n) -> p a n", a=DP)
            prog.add("sync", lambda g: g.dma_start(out=dstH, in_=src),
                     reads=[R(f"c_h{bn}", (b, s, oc)) for oc in range(DP)],
                     writes=["cHSB"], dma=True, tag=f"C.l{bn}{b}.{s}")
            for oc in range(DP):
                prog.add("scalar", lambda sc, oc=oc, bi=bi, dstF=dstF: sc.activation(
                    dstF[:, oc * 512:(oc + 1) * 512], HSB[:, oc * 512:(oc + 1) * 512],
                    AF.Sigmoid, bias=B3[:, bi * 8 + oc:bi * 8 + oc + 1],
                    scale=A3[:, bi * 8 + oc:bi * 8 + oc + 1]),
                    reads=["cHSB", "cA3", "cB3"], writes=[dres], tag=f"C.sg{bn}{b}.{s}.{oc}")

        for b in range(ctx.B):
            load_xb(ctx, b, f"C2.{b}")
            for s in range(NSL):
                sig_stream("v", 2, hv, b, s, VF, "cVF")
                for j in range(4):
                    for a in range(DP):
                        transpose_tile(
                            ctx, VTOK[:, (s * 4 + j) * D + a * 128: (s * 4 + j) * D + (a + 1) * 128],
                            VF[:, a * 512 + j * 128: a * 512 + (j + 1) * 128],
                            pTR[:, :], ctx.sb["cIdentB"][:, :],
                            reads=["cVF"], writes=[R("cVTOK", s * 4 + j)], pres="cpTR",
                            tag=f"C.vt{b}.{s * 4 + j}.{a}")
            for j in range(8):
                def emVV(t, j=j):
                    for hh in range(2):
                        h = 2 * j + hh
                        ro = hh * 64
                        for it in range(TPB):
                            last = t.matmul(pB0[ro:ro + 64, 0:64],
                                            VTOK[:, it * D + h * 64: it * D + h * 64 + 64],
                                            VTOK[:, it * D + h * 64: it * D + h * 64 + 64],
                                            start=(it == 0), stop=(it == TPB - 1))
                    return last
                prog.add("tensor", emVV, reads=[R("cVTOK", it) for it in range(TPB)],
                         writes=["cpB0"], tag=f"C.vv{b}.{j}")
                prog.add("vector", lambda v, j=j: v.tensor_scalar(
                    VVS[:, j * 64:(j + 1) * 64], pB0[:, 0:64], 0.125, None, OP.mult),
                    reads=["cpB0"], writes=["cVVS"], tag=f"C.ve{b}.{j}")
            for s in range(NSL):
                sig_stream("q", 0, hq, b, s, QBF, "cQBF")
                sig_stream("k", 1, hk, b, s, KBF, "cKBF")
                for a in range(DP):
                    prog.add("vector", lambda v, a=a: v.tensor_mul(
                        QKF[:, a * 512:(a + 1) * 512], QBF[:, a * 512:(a + 1) * 512],
                        KBF[:, a * 512:(a + 1) * 512]),
                        reads=["cQBF", "cKBF"], writes=["cQKF"], tag=f"C.qk{b}.{s}.{a}")
                for j in range(8):
                    def emO2(t, j=j):
                        for hh in range(2):
                            ro = hh * 64
                            last = t.matmul(pB1[ro:ro + 64, :],
                                            VVS[ro:ro + 64, j * 64:(j + 1) * 64],
                                            QKF[ro:ro + 64, j * 512:(j + 1) * 512],
                                            start=True, stop=True)
                        return last
                    prog.add("tensor", emO2, reads=["cVVS", "cQKF"], writes=["cpB1"],
                             tag=f"C.o{b}.{s}.{j}")
                    prog.add("scalar", lambda sc, j=j: sc.copy(OUTF[:, j * 512:(j + 1) * 512], pB1[:, :]),
                             reads=["cpB1"], writes=["cOUTF"], tag=f"C.oe{b}.{s}.{j}")
                for j in range(4):
                    i_loc = s * 4 + j
                    xi = b * TPB + i_loc
                    def emP(t, j=j):
                        for half in range(2):
                            for a in range(DP):
                                t.matmul(pO[:, half * 512:(half + 1) * 512],
                                         OUTF[:, a * 512 + j * 128: a * 512 + (j + 1) * 128],
                                         WC[:, 3 * 8192 + a * 1024 + half * 512: 3 * 8192 + a * 1024 + (half + 1) * 512],
                                         start=(a == 0), stop=False)
                            last = t.matmul(pO[:, half * 512:(half + 1) * 512],
                                            ctx.sb["cOnesF"][0:1, 0:128],
                                            BPR[0:1, half * 512:(half + 1) * 512],
                                            start=False, stop=True)
                        return last
                    prog.add("tensor", emP, reads=["cOUTF", "cWC3", "cBPR", "cOnesF"],
                             writes=["cpO"], tag=f"C.p{xi}")
                    prog.add("vector", lambda v, xi=xi, i_loc=i_loc: v.tensor_add(
                        ctx.sb["LNZ"][:, :], pO[:, :], XB[:, i_loc * D:(i_loc + 1) * D]),
                        reads=["cpO", R("XB", i_loc)], writes=["LNZ"], tag=f"C.z{xi}")
                    emit_ln_out(ctx, xi, is_last, f"C.ln{xi}")


    ctx.cur_es = None


def kernel(**inputs):
    cfg = dict(CFG)
    n_cores = cfg["n_cores"]
    T, Bpc = cfg["T"], cfg["B_per_core"]
    stage_groups = [cfg["stages"]] if cfg["fused"] else [(s,) for s in cfg["stages"]]

    x = np.asarray(inputs["L0_attn_x"], dtype=np.float32)
    B_total = x.shape[0]
    cur = inputs
    out_parts = None
    for gi, grp in enumerate(stage_groups):
        gcfg = dict(cfg)
        gcfg["stages"] = grp
        in_maps = stage_inputs(cur, grp, T, Bpc, n_cores)
        if gi > 0:
            for c in range(n_cores):
                in_maps[c]["x"] = out_parts[c]
        nc = build_program(gcfg)
        res = bass_utils.run_bass_kernel_spmd(nc, in_maps, core_ids=list(range(n_cores)))
        out_parts = [np.asarray(r["xout"]) for r in res.results]
    out = np.concatenate([p.reshape(Bpc, T, D) for p in out_parts], axis=0)
    return out.astype(np.float32)


# version 06
# speedup vs baseline: 1.1042x; 1.0881x over previous
import math
import contextlib
import numpy as np

import concourse.bass as bass
import concourse.mybir as mybir
import concourse.bass_utils as bass_utils

F32 = mybir.dt.float32
BF16 = mybir.dt.bfloat16
OP = mybir.AluOpType
AF = mybir.ActivationFunctionType
AX = mybir.AxisListType

D = 1024
DP = D // 128
NHEAD = 16
DH = 64
CHUNK = 128

CFG = {
    "stages": ("A", "B", "C", "D"),
    "fused": True,
    "n_cores": 8,
    "T": 2048,
    "B_per_core": 2,
    "collective": True,
    "gelu_lut": False,
}

EPS = 1e-5



ENGINES = ("tensor", "vector", "scalar", "gpsimd", "sync")


class Prog:

    def __init__(self):
        self.ops = []
        self.last_w = {}
        self.readers = {}

    def add(self, engine, emit, reads=(), writes=(), dma=False, tag=""):
        assert engine in ENGINES, engine
        idx = len(self.ops)
        deps = set()
        for r in reads:
            w = self.last_w.get(r)
            if w is not None:
                deps.add(w)
        for r in writes:
            w = self.last_w.get(r)
            if w is not None:
                deps.add(w)
            deps.update(self.readers.get(r, ()))
        deps.discard(idx)
        self.ops.append(dict(i=idx, engine=engine, emit=emit, dma=dma,
                             tag=tag, deps=deps, barrier=False))
        for r in reads:
            self.readers.setdefault(r, set()).add(idx)
        for r in writes:
            self.last_w[r] = idx
            self.readers[r] = {idx}
        return idx

    def barrier(self):
        self.ops.append(dict(i=len(self.ops), engine=None, emit=None, dma=False,
                             tag="BARRIER", deps=set(), barrier=True))

    NDMA_SLOTS = 6

    def finalize_into(self, nc, es):
        R_ = Prog.NDMA_SLOTS
        sems = {e: es.enter_context(nc.semaphore(f"s_{e}")) for e in ENGINES}
        dsems = {}
        for e in ENGINES:
            for sl in range(R_):
                dsems[(e, sl)] = es.enter_context(nc.semaphore(f"d_{e}{sl}"))
        dsems[("cc", 0)] = es.enter_context(nc.semaphore("d_cc"))

        pos = {}
        cnt = {e: 0 for e in ENGINES}
        dcnt = {e: 0 for e in ENGINES}
        duse = {(e, sl): 0 for e in ENGINES for sl in range(R_)}
        per_engine = {e: [] for e in ENGINES}
        pending_bar = None
        bar_seen = set()
        for o in self.ops:
            if o["engine"] is None:
                pending_bar = {"c": dict(cnt), "d": dict(duse)}
                bar_seen = set()
                continue
            e = o["engine"]
            if o["dma"] == "cc":
                duse.setdefault(("cc", 0), 0)
                duse[("cc", 0)] += 1
                o["slot"] = ("cc", 0)
                o["slot_use"] = duse[("cc", 0)]
                pos[o["i"]] = ("D", ("cc", 0), duse[("cc", 0)])
            elif o["dma"]:
                sl = dcnt[e] % R_
                duse[(e, sl)] += 1
                dcnt[e] += 1
                o["slot"] = (e, sl)
                o["slot_use"] = duse[(e, sl)]
                pos[o["i"]] = ("D", (e, sl), duse[(e, sl)] * 16)
            else:
                cnt[e] += 1
                pos[o["i"]] = ("C", e, cnt[e])
            o["bar"] = None
            if pending_bar is not None and e not in bar_seen:
                o["bar"] = pending_bar
                bar_seen.add(e)
                if bar_seen == set(ENGINES):
                    pending_bar = None
            per_engine[e].append(o)

        for e in ENGINES:
            hw_c = {}
            hw_d = {}
            for o in per_engine[e]:
                waits = {}
                if o.get("bar"):
                    for e2, c in o["bar"]["c"].items():
                        if c > 0:
                            waits[("C", e2)] = max(waits.get(("C", e2), 0), c)
                    for key, u in o["bar"]["d"].items():
                        if u > 0:
                            step = 1 if key[0] == "cc" else 16
                            waits[("D", key)] = max(waits.get(("D", key), 0), u * step)
                for dep in o["deps"]:
                    kind, who, dv = pos[dep]
                    if kind == "C" and who == e == "tensor" and not o["dma"]:
                        continue
                    waits[(kind, who)] = max(waits.get((kind, who), 0), dv)
                if o["dma"]:
                    key = ("D", o["slot"])
                    step = 1 if o["dma"] == "cc" else 16
                    need = (o["slot_use"] - 1) * step
                    if need > 0:
                        waits[key] = max(waits.get(key, 0), need)
                o["w_list"] = []
                for (kind, who), v in sorted(waits.items(), key=str):
                    hw = hw_d if kind == "D" else hw_c
                    if v > hw.get(who, 0):
                        hw[who] = v
                        o["w_list"].append((kind, who, v))

        def section(eng_obj, e):
            for o in per_engine[e]:
                for (kind, who, v) in o["w_list"]:
                    eng_obj.wait_ge(dsems[who] if kind == "D" else sems[who], v)
                inst = o["emit"](eng_obj)
                if inst is None:
                    raise RuntimeError(f"op {o['i']} ({o['tag']}) returned None")
                if o["dma"] == "cc":
                    inst.then_inc(dsems[o["slot"]])
                elif o["dma"]:
                    inst.then_inc(dsems[o["slot"]], 16)
                else:
                    inst.then_inc(sems[e], 1)

        with nc.Block() as block:
            block.tensor(lambda t: section(t, "tensor"))
            block.vector(lambda v: section(v, "vector"))
            block.scalar(lambda s: section(s, "scalar"))
            block.gpsimd(lambda g: section(g, "gpsimd"))
            block.sync(lambda sy: section(sy, "sync"))
        return cnt, dcnt


def R(name, i=None):
    return name if i is None else f"{name}:{i}"



def decay_gammas():
    return 1.0 - np.exp(np.linspace(math.log(1 / 32), math.log(1 / 512), NHEAD))


def host_constants():
    gam = decay_gammas()
    a = np.arange(CHUNK, dtype=np.float64)[:, None]
    Gq = np.concatenate([(g ** a) * np.ones((1, DH)) for g in gam], axis=1) / math.sqrt(DH)
    Gk = np.concatenate([(g ** (-a)) * np.ones((1, DH)) for g in gam], axis=1)
    causal = (np.arange(CHUNK)[:, None] <= np.arange(CHUNK)[None, :]).astype(np.float32)
    ltri = (np.arange(128)[:, None] <= np.arange(128)[None, :]).astype(np.float32).T
    ltri = (np.arange(128)[:, None][:, :] <= np.arange(128)[None, :]).astype(np.float32)
    return {
        "cGq": Gq.astype(np.float32),
        "cGk": Gk.astype(np.float32),
        "cCausal": causal,
        "cIdent": np.eye(128, dtype=np.float32),
        "cLtri": ltri.astype(np.float32),
        "cOnes": np.ones((128, 128), dtype=np.float32),
        "cGpc": np.concatenate([
            np.concatenate([np.full((64, 1), g2 ** CHUNK), np.full((64, 1), g21 ** CHUNK)])
            for g2, g21 in zip(gam[0::2], gam[1::2])], axis=1).astype(np.float32),
    }


def fm(vec):
    v = np.asarray(vec, dtype=np.float32)
    k = v.shape[0] // 128
    return np.ascontiguousarray(v.reshape(k, 128).T)


def rowv(vec):
    return np.ascontiguousarray(np.asarray(vec, dtype=np.float32)[None, :])


def tr(w):
    return np.ascontiguousarray(np.asarray(w, dtype=np.float32).T)


def stage_inputs(inputs, stages, T, B_per_core, n_cores):
    x = np.ascontiguousarray(np.asarray(inputs["L0_attn_x"], dtype=np.float32))
    B_total, T_in, D_in = x.shape
    assert D_in == D
    shared = dict(host_constants())
    if "A" in stages:
        shared.update({
            "aQwT": tr(inputs["L0_attn_qw"]), "aKwT": tr(inputs["L0_attn_kw"]),
            "aVwT": tr(inputs["L0_attn_vw"]), "aOwT": tr(inputs["L0_attn_ow"]),
            "aW1T": tr(inputs["L0_attn_w1"]), "aW2T": tr(inputs["L0_attn_w2"]),
            "aGn1g": fm(inputs["L0_attn_norm_g"]), "aGn1b": fm(inputs["L0_attn_norm_b"]),
            "aGn2g": fm(inputs["L0_attn_ffn_norm_g"]), "aGn2b": fm(inputs["L0_attn_ffn_norm_b"]),
            "aObR": rowv(inputs["L0_attn_ob"]), "aB2R": rowv(inputs["L0_attn_b2"]),
            "aB1F": fm(inputs["L0_attn_b1"]),
        })
    if "B" in stages:
        shared.update({
            "bGwT": tr(inputs["L0_ffn_gate_w"]),
            "bWgT": np.ascontiguousarray(np.asarray(inputs["L0_ffn_w_gate"], np.float32).transpose(0, 2, 1)),
            "bWuT": np.ascontiguousarray(np.asarray(inputs["L0_ffn_w_up"], np.float32).transpose(0, 2, 1)),
            "bWdT": np.ascontiguousarray(np.asarray(inputs["L0_ffn_w_down"], np.float32).transpose(0, 2, 1)),
        })
    if "C" in stages:
        shared.update({
            "cWqT": tr(inputs["L1_attn_Wq"]), "cWkT": tr(inputs["L1_attn_Wk"]),
            "cWvT": tr(inputs["L1_attn_Wv"]), "cWpT": tr(inputs["L1_attn_Wp"]),
            "cGq_g": fm(inputs["L1_attn_gq"]), "cGq_b": fm(inputs["L1_attn_bq"]),
            "cGk_g": fm(inputs["L1_attn_gk"]), "cGk_b": fm(inputs["L1_attn_bk"]),
            "cGv_g": fm(inputs["L1_attn_gv"]), "cGv_b": fm(inputs["L1_attn_bv"]),
            "cBpR": rowv(inputs["L1_attn_bp"]),
        })
    if "D" in stages:
        shared.update({
            "dU1": np.ascontiguousarray(np.asarray(inputs["L1_ffn_U1"], np.float32)),
            "dV1": np.ascontiguousarray(np.asarray(inputs["L1_ffn_V1"], np.float32)),
            "dU2": np.ascontiguousarray(np.asarray(inputs["L1_ffn_U2"], np.float32)),
            "dV2": np.ascontiguousarray(np.asarray(inputs["L1_ffn_V2"], np.float32)),
            "dB1F": fm(inputs["L1_ffn_b1"]), "dB2R": rowv(inputs["L1_ffn_b2"]),
        })
    in_maps = []
    for c in range(n_cores):
        m = dict(shared)
        xs = x[c * B_per_core:(c + 1) * B_per_core, :T, :]
        m["x"] = np.ascontiguousarray(xs.reshape(B_per_core * T, D))
        in_maps.append(m)
    return in_maps



class Ctx:
    def __init__(self, nc, prog, es, cfg):
        self.nc = nc
        self.prog = prog
        self.es = es
        self.cfg = cfg
        self.T = cfg["T"]
        self.B = cfg["B_per_core"]
        self.NT = self.B * self.T // 128
        self.TPB = self.T // 128
        self.sb = {}
        self.dram = {}

    def inp(self, name, shape):
        t = self.nc.dram_tensor(name, list(shape), F32, kind="ExternalInput")
        self.dram[name] = t
        return t

    def sbuf(self, name, shape, dtype):
        es = getattr(self, "cur_es", None) or self.es
        t = es.enter_context(self.nc.sbuf_tensor(name, list(shape), dtype))
        self.sb[name] = t
        return t

    def psum(self, name, shape, dtype=F32):
        es = getattr(self, "cur_es", None) or self.es
        t = es.enter_context(self.nc.psum_tensor(name, list(shape), dtype))
        self.sb[name] = t
        return t

    def load_weight_fm(self, dst_sb, src_dram, n_ptiles, cols, res, srcsl=None, tag=""):
        src = src_dram.ap() if srcsl is None else srcsl
        src = src.rearrange("(a p) n -> p a n", p=128)
        dst = dst_sb[:, 0:n_ptiles * cols].rearrange("p (a n) -> p a n", a=n_ptiles)
        self.prog.add("gpsimd", lambda g: g.dma_start(out=dst, in_=src),
                      reads=(), writes=[res], dma=True, tag=tag or f"load:{res}")

    def dma_in(self, engine, dst_ap, src_ap, writes, tag):
        self.prog.add(engine, lambda g: g.dma_start(out=dst_ap, in_=src_ap),
                      reads=(), writes=writes, dma=True, tag=tag)

    def dma_out(self, engine, dst_ap, src_ap, reads, writes, tag):
        self.prog.add(engine, lambda g: g.dma_start(out=dst_ap, in_=src_ap),
                      reads=reads, writes=writes, dma=True, tag=tag)


def transpose_tile(ctx, out_sb_ap, in_sb_ap, ptr_aps, ident_ap, reads, writes, pres, tag, slot=0):
    prog = ctx.prog
    ph_ = ptr_aps[slot][:, 0:128]
    prog.add("tensor", lambda t: t.transpose(ph_, in_sb_ap, ident_ap),
             reads=list(reads) + ["cIdentB"], writes=[R(pres, slot)], tag=tag + ".tr")
    prog.add("vector", lambda v: v.tensor_copy(out_sb_ap, ph_),
             reads=[R(pres, slot)], writes=list(writes), tag=tag + ".ev")



def ln_from_z(ctx, xi, out_ap, out_res, tag):
    prog = ctx.prog
    z = ctx.sb["LNZ"][:, :]
    st = ctx.sb["LNST"]
    mu, ez2, rstd, ssq, nvar = st[:, 0:1], st[:, 1:2], st[:, 2:3], st[:, 3:4], st[:, 4:5]
    prog.add("vector", lambda v: v.reduce_sum(mu, z, axis=AX.X),
             reads=["LNZ"], writes=["LNST"], tag=tag + ".mu")
    prog.add("scalar", lambda s: s.activation(ctx.sb["OUTT"][:, :], z, AF.Square, accum_out=ssq),
             reads=["LNZ", "LNST"], writes=["OUTT", "LNST"], tag=tag + ".sq")
    prog.add("vector", lambda v: v.tensor_scalar(mu, mu, 1.0 / D, None, OP.mult),
             reads=["LNST"], writes=["LNST"], tag=tag + ".m2")
    prog.add("vector", lambda v: v.tensor_scalar(ez2, ssq, 1.0 / D, None, OP.mult),
             reads=["LNST"], writes=["LNST"], tag=tag + ".v1")
    prog.add("vector", lambda v: v.scalar_tensor_tensor(nvar, mu, mu[:, 0:1], ez2, OP.mult, OP.subtract),
             reads=["LNST"], writes=["LNST"], tag=tag + ".v2")
    prog.add("scalar", lambda s: s.activation(rstd, nvar, AF.Sqrt, bias=ctx.sb["CONSTV"][:, 0:1], scale=-1.0),
             reads=["LNST", "CONSTV"], writes=["LNST"], tag=tag + ".rt")
    prog.add("vector", lambda v: v.reciprocal(rstd, rstd),
             reads=["LNST"], writes=["LNST"], tag=tag + ".rc")
    prog.add("vector", lambda v: v.tensor_scalar(out_ap, z, mu, rstd, OP.subtract, OP.mult),
             reads=["LNZ", "LNST"], writes=[out_res], tag=tag + ".o")



def build_stage_D(ctx, is_last):
    nc, prog, cfg = ctx.nc, ctx.prog, ctx.cfg
    es = contextlib.ExitStack()
    ctx.cur_es = es
    with es:
        U1 = ctx.sbuf("dU1s", [128, DP * 256], BF16)
        V1 = ctx.sbuf("dV1s", [128, 2 * 4096], BF16)
        U2 = ctx.sbuf("dU2s", [128, 32 * 256], BF16)
        V2 = ctx.sbuf("dV2s", [128, 2 * 1024], BF16)
        B1F = ctx.sbuf("dB1Fs", [128, 32], F32)
        B2R = ctx.sbuf("dB2Rs", [1, 1024], F32)
        XF = ctx.sbuf("dXF", [128, DP * 512], BF16)
        PT = ctx.sbuf("dPT", [128, 2 * 512], BF16)
        HT = ctx.sbuf("dHT", [128, 32 * 512], BF16)
        QT = ctx.sbuf("dQT", [128, 2 * 512], BF16)
        GS = ctx.sbuf("dGS", [128, 2 * 512], F32)
        T2 = ctx.sbuf("dT2", [128, 2 * 512], F32)
        pTR = [ctx.psum("dpTR0", [128, 128], BF16), ctx.psum("dpTR1", [128, 128], BF16)]
        pP = [ctx.psum(f"dpP{i}", [128, 512]) for i in range(2)]
        pH = [ctx.psum(f"dpH{i}", [128, 512]) for i in range(2)]
        pO = ctx.psum("dpO", [128, 1024])

        ctx.load_weight_fm(U1, ctx.dram["dU1"], DP, 256, "dU1s")
        ctx.load_weight_fm(V1, ctx.dram["dV1"], 2, 4096, "dV1s")
        ctx.load_weight_fm(U2, ctx.dram["dU2"], 32, 256, "dU2s")
        ctx.load_weight_fm(V2, ctx.dram["dV2"], 2, 1024, "dV2s")
        ctx.dma_in("sync", B1F[:, :], ctx.dram["dB1F"].ap(), ["dB1Fs"], "dB1F")
        ctx.dma_in("sync", B2R[:, :], ctx.dram["dB2R"].ap(), ["dB2Rs"], "dB2R")

        XB = ctx.sb["XB"]
        NS = ctx.TPB // 4
        for b in range(ctx.B):
          load_xb(ctx, b, f"D.{b}")
          for s in range(NS):
            for j in range(4):
                i_loc = s * 4 + j
                xi = b * ctx.TPB + i_loc
                for a in range(DP):
                    transpose_tile(
                        ctx, XF[:, a * 512 + j * 128: a * 512 + (j + 1) * 128],
                        XB[:, i_loc * D + a * 128: i_loc * D + (a + 1) * 128],
                        pTR, ctx.sb["cIdentB"][:, :],
                        reads=[R("XB", i_loc)], writes=[R("dXF", a)], pres="dpTR",
                        tag=f"D.xf{xi}.{a}", slot=a % 2)
            for rc in range(2):
                def em(t, rc=rc):
                    for a in range(DP):
                        last = t.matmul(pP[rc][:, :],
                                        U1[:, a * 256 + rc * 128: a * 256 + (rc + 1) * 128],
                                        XF[:, a * 512:(a + 1) * 512],
                                        start=(a == 0), stop=(a == DP - 1))
                    return last
                prog.add("tensor", em,
                         reads=["dU1s"] + [R("dXF", a) for a in range(DP)],
                         writes=[f"dpP{rc}"], tag=f"D.P{s}.{rc}")
                prog.add("scalar", lambda sc, rc=rc: sc.copy(PT[:, rc * 512:(rc + 1) * 512], pP[rc][:, :]),
                         reads=[f"dpP{rc}"], writes=[R("dPT", rc)], tag=f"D.Pe{s}.{rc}")
            for f in range(32):
                pb = pH[f % 2]
                def em(t, f=f, pb=pb):
                    for a in range(2):
                        last = t.matmul(pb[:, :],
                                        V1[:, a * 4096 + f * 128:a * 4096 + (f + 1) * 128],
                                        PT[:, a * 512:(a + 1) * 512],
                                        start=(a == 0), stop=(a == 1))
                    return last
                prog.add("tensor", em, reads=["dV1s", R("dPT", 0), R("dPT", 1)],
                         writes=[f"dpH{f % 2}"], tag=f"D.H{s}.{f}")
                hslice = HT[:, f * 512:(f + 1) * 512]
                bcol = B1F[:, f:f + 1]
                if cfg["gelu_lut"]:
                    prog.add("scalar", lambda sc, pb=pb, hs=hslice, bc=bcol:
                             sc.activation(hs, pb[:, :], AF.Gelu_apprx_tanh, bias=bc, scale=1.0),
                             reads=[f"dpH{f % 2}", "dB1Fs"], writes=[R("dHT", f)],
                             tag=f"D.g{s}.{f}")
                else:
                    par = f % 2
                    gsh = GS[:, par * 512:(par + 1) * 512]
                    t2 = T2[:, par * 512:(par + 1) * 512]
                    prog.add("scalar", lambda sc, pb=pb, bc=bcol, gsh=gsh:
                             sc.activation(gsh, pb[:, :], AF.Identity, bias=bc, scale=1.0),
                             reads=[f"dpH{f % 2}", "dB1Fs"], writes=[R("dGS", par)], tag=f"D.g0.{s}.{f}")
                    prog.add("scalar", lambda sc, pb=pb, gsh=gsh:
                             sc.activation(pb[:, :], gsh, AF.Square),
                             reads=[R("dGS", par)], writes=[f"dpH{f % 2}"], tag=f"D.g1.{s}.{f}")
                    prog.add("vector", lambda v, pb=pb, t2=t2:
                             v.tensor_scalar(t2, pb[:, :], 0.07135481627159174, 1.5957691216057308, OP.mult, OP.add),
                             reads=[f"dpH{f % 2}"], writes=[R("dT2", par)], tag=f"D.g2.{s}.{f}")
                    prog.add("vector", lambda v, t2=t2, gsh=gsh:
                             v.tensor_mul(t2, t2, gsh),
                             reads=[R("dT2", par), R("dGS", par)], writes=[R("dT2", par)], tag=f"D.g3.{s}.{f}")
                    prog.add("scalar", lambda sc, t2=t2:
                             sc.activation(t2, t2, AF.Sigmoid),
                             reads=[R("dT2", par)], writes=[R("dT2", par)], tag=f"D.g4.{s}.{f}")
                    prog.add("vector", lambda v, hs=hslice, t2=t2, gsh=gsh:
                             v.tensor_mul(hs, t2, gsh),
                             reads=[R("dT2", par), R("dGS", par)], writes=[R("dHT", f)], tag=f"D.g5.{s}.{f}")
            for rc in range(2):
                def em(t, rc=rc):
                    for a in range(32):
                        last = t.matmul(pP[rc][:, :],
                                        U2[:, a * 256 + rc * 128:a * 256 + (rc + 1) * 128],
                                        HT[:, a * 512:(a + 1) * 512],
                                        start=(a == 0), stop=(a == 31))
                    return last
                prog.add("tensor", em,
                         reads=["dU2s"] + [R("dHT", a) for a in range(32)],
                         writes=[f"dpP{rc}"], tag=f"D.Q{s}.{rc}")
                prog.add("scalar", lambda sc, rc=rc: sc.copy(QT[:, rc * 512:(rc + 1) * 512], pP[rc][:, :]),
                         reads=[f"dpP{rc}"], writes=[R("dQT", rc)], tag=f"D.Qe{s}.{rc}")
            for j in range(4):
                i_loc = s * 4 + j
                xi = b * ctx.TPB + i_loc
                def em(t, j=j):
                    for half in range(2):
                        for rc in range(2):
                            t.matmul(pO[:, half * 512:(half + 1) * 512],
                                     QT[:, rc * 512 + j * 128: rc * 512 + (j + 1) * 128],
                                     V2[:, rc * 1024 + half * 512: rc * 1024 + (half + 1) * 512],
                                     start=(rc == 0), stop=False)
                        last = t.matmul(pO[:, half * 512:(half + 1) * 512],
                                        ctx.sb["cOnesF"][0:1, 0:128],
                                        B2R[0:1, half * 512:(half + 1) * 512],
                                        start=False, stop=True)
                    return last
                prog.add("tensor", em,
                         reads=[R("dQT", 0), R("dQT", 1), "dV2s", "dB2Rs", "cOnesF"],
                         writes=["dpO"], tag=f"D.o{xi}")
                prog.add("vector", lambda v, xi=xi, i_loc=i_loc: v.tensor_add(
                    ctx.sb["LNZ"][:, :], pO[:, :], XB[:, i_loc * D:(i_loc + 1) * D]),
                    reads=["dpO", R("XB", i_loc)], writes=["LNZ"], tag=f"D.z{xi}")
                emit_ln_out(ctx, xi, is_last, f"D.ln{xi}")


    ctx.cur_es = None

def emit_ln_out(ctx, xi, is_last, tag):
    i = xi % ctx.TPB
    if is_last:
        o = ctx.sb["OUTT"][:, :]
        ln_from_z(ctx, xi, o, "OUTT", tag)
        dst = ctx.dram["xout"].ap()[xi * 128:(xi + 1) * 128, :]
        ctx.dma_out("sync", dst, o, reads=["OUTT"], writes=[R("xout", xi)], tag=tag + ".st")
    else:
        XB = ctx.sb["XB"]
        ln_from_z(ctx, xi, XB[:, i * D:(i + 1) * D], R("XB", i), tag)
        dst = ctx.dram["xres"].ap()[xi * 128:(xi + 1) * 128, :]
        ctx.dma_out("sync", dst, XB[:, i * D:(i + 1) * D], reads=[R("XB", i)],
                    writes=[R("xres", xi)], tag=tag + ".st")
        ctx.xsrc[xi] = "xres"


def load_xb(ctx, b, tag):
    XB = ctx.sb["XB"]
    for i in range(ctx.TPB):
        xi = b * ctx.TPB + i
        if ctx.xsrc[xi] == "x":
            srcap = ctx.dram["x"].ap()[xi * 128:(xi + 1) * 128, :]
            rr = []
        else:
            srcap = ctx.dram["xres"].ap()[xi * 128:(xi + 1) * 128, :]
            rr = [R("xres", xi)]
        ctx.prog.add("gpsimd", lambda g, d=XB[:, i * D:(i + 1) * D], s=srcap: g.dma_start(out=d, in_=s),
                     reads=rr, writes=[R("XB", i)], dma=True, tag=f"{tag}.lx{i}")



def build_program(cfg):
    nc = bass.Bass("TRN2", target_bir_lowering=False)
    es = contextlib.ExitStack()
    prog = Prog()
    ctx = Ctx(nc, prog, es, cfg)
    stages = cfg["stages"]
    NTOK = cfg["B_per_core"] * cfg["T"]

    with es:
        xin = ctx.inp("x", [NTOK, D])
        ctx.dram["xout"] = nc.dram_tensor("xout", [NTOK, D], F32, kind="ExternalOutput")
        for nm, shape in [("cGq", [128, D]), ("cGk", [128, D]), ("cCausal", [128, 128]),
                          ("cIdent", [128, 128]), ("cLtri", [128, 128]), ("cOnes", [128, 128]),
                          ("cGpc", [128, 8])]:
            ctx.inp(nm, shape)
        wshapes = {
            "A": [("aQwT", [D, D]), ("aKwT", [D, D]), ("aVwT", [D, D]), ("aOwT", [D, D]),
                  ("aW1T", [D, 2 * D]), ("aW2T", [2 * D, D]),
                  ("aGn1g", [128, DP]), ("aGn1b", [128, DP]), ("aGn2g", [128, DP]), ("aGn2b", [128, DP]),
                  ("aObR", [1, D]), ("aB2R", [1, D]), ("aB1F", [128, 16])],
            "B": [("bGwT", [D, 8]), ("bWgT", [8, D, 4096]), ("bWuT", [8, D, 4096]),
                  ("bWdT", [8, 4096, D])],
            "C": [("cWqT", [D, D]), ("cWkT", [D, D]), ("cWvT", [D, D]), ("cWpT", [D, D]),
                  ("cGq_g", [128, DP]), ("cGq_b", [128, DP]), ("cGk_g", [128, DP]),
                  ("cGk_b", [128, DP]), ("cGv_g", [128, DP]), ("cGv_b", [128, DP]),
                  ("cBpR", [1, D])],
            "D": [("dU1", [D, 256]), ("dV1", [256, 4096]), ("dU2", [4096, 256]),
                  ("dV2", [256, D]), ("dB1F", [128, 32]), ("dB2R", [1, D])],
        }
        for st in stages:
            for nm, shape in wshapes[st]:
                ctx.inp(nm, shape)

        XB = ctx.sbuf("XB", [128, ctx.TPB * D], BF16)
        ctx.dram["xres"] = nc.dram_tensor("xres", [NTOK, D], BF16)
        ctx.xsrc = ["x"] * ctx.NT
        ctx.sbuf("LNZ", [128, D], F32)
        ctx.sbuf("LNST", [128, 8], F32)
        ctx.sbuf("OUTT", [128, D], F32)
        identb = ctx.sbuf("cIdentB", [128, 128], BF16)
        constv = ctx.sbuf("CONSTV", [128, 4], F32)
        prog.add("vector", lambda v: v.memset(constv[:, 0:1], float(EPS)), writes=["CONSTV"], tag="c_eps")
        onesf = ctx.sbuf("cOnesF", [128, 128], F32)
        onescb = ctx.sbuf("cOnesCB", [128, 1], BF16)

        ctx.dma_in("gpsimd", identb[:, :], ctx.dram["cIdent"].ap(), ["cIdentB"], "ldid")
        ctx.dma_in("sync", onesf[:, :], ctx.dram["cOnes"].ap(), ["cOnesF"], "ldon")
        prog.add("vector", lambda v: v.memset(onescb[:, :], 1.0), writes=["cOnesCB"], tag="onescb")

        for si, st in enumerate(stages):
            is_last = (si == len(stages) - 1)
            if st == "A":
                build_stage_A(ctx, is_last)
            elif st == "B":
                build_stage_B(ctx, is_last)
            elif st == "C":
                build_stage_C(ctx, is_last)
            elif st == "D":
                build_stage_D(ctx, is_last)
            prog.barrier()

        prog.add("sync", lambda sy: sy.nop(), reads=[R("xout", i) for i in range(ctx.NT)],
                 tag="final")
        cnt, dcnt = prog.finalize_into(nc, es)
    return nc


def groupnorm_coefs(ctx, src_name, nt0, g_sb, b_sb, af_ap, bf_ap, af_res, bf_res,
                    psU, psQ, psG, psU_res, psQ_res, sus, sqs, grs, gsm, sqscr, onesc_bf, tag):
    prog, T = ctx.prog, ctx.T
    src = ctx.sb[src_name]
    ntiles = ctx.TPB
    for i in range(ntiles):
        xi = nt0 + i
        def emS(t, i=i, xi=xi):
            for half in range(2):
                last = t.matmul(psU[0:1, half * 512:(half + 1) * 512], onesc_bf[:, 0:1],
                                src[:, xi * D + half * 512: xi * D + (half + 1) * 512],
                                start=(i == 0), stop=(i == ntiles - 1))
            return last
        prog.add("tensor", emS, reads=[R(src_name, xi), "cOnesCB"], writes=[psU_res],
                 tag=f"{tag}.s{i}")
        prog.add("scalar", lambda s, xi=xi: s.activation(
            sqscr[:, :], src[:, xi * D:(xi + 1) * D], AF.Square),
            reads=[R(src_name, xi)], writes=["GSQs"], tag=f"{tag}.q{i}")
        def emQ(t, i=i):
            for half in range(2):
                last = t.matmul(psQ[0:1, half * 512:(half + 1) * 512], ctx.sb["cOnesF"][:, 0:1],
                                sqscr[:, half * 512:(half + 1) * 512],
                                start=(i == 0), stop=(i == ntiles - 1))
            return last
        prog.add("tensor", emQ, reads=["GSQs", "cOnesF"], writes=[psQ_res],
                 tag=f"{tag}.qs{i}")
    grrow = ctx.sb["GRROW"]
    prog.add("vector", lambda v: v.tensor_copy(grrow[0:1, :], psU[0:1, :]),
             reads=[psU_res], writes=["GRROW"], tag=f"{tag}.cU")
    prog.add("vector", lambda v: v.reduce_sum(
        grs[0:1, 0:16], grrow[0:1, :].rearrange("p (a hi c) -> p hi a c", hi=2, c=64),
        axis=AX.X), reads=["GRROW"], writes=["GRS"], tag=f"{tag}.rU")
    prog.add("vector", lambda v: v.tensor_copy(grrow[0:1, :], psQ[0:1, :]),
             reads=[psQ_res], writes=["GRROW"], tag=f"{tag}.cQ")
    prog.add("vector", lambda v: v.reduce_sum(
        grs[0:1, 16:32], grrow[0:1, :].rearrange("p (a hi c) -> p hi a c", hi=2, c=64),
        axis=AX.X), reads=["GRROW"], writes=["GRS"], tag=f"{tag}.rQ")
    ONES = ctx.sb["cOnesF"]
    kk = 1.0 / (64.0 * T)
    prog.add("vector", lambda v: v.tensor_scalar(gsm[0:1, 0:16], grs[0:1, 0:16], kk, None, OP.mult),
             reads=["GRS"], writes=["GSM"], tag=f"{tag}.k1")
    prog.add("vector", lambda v: v.tensor_scalar(gsm[0:1, 16:32], grs[0:1, 16:32], kk, None, OP.mult),
             reads=["GRS"], writes=["GSM"], tag=f"{tag}.k2")
    prog.add("vector", lambda v: v.tensor_mul(gsm[0:1, 32:48], gsm[0:1, 0:16], gsm[0:1, 0:16]),
             reads=["GSM"], writes=["GSM"], tag=f"{tag}.m2")
    prog.add("vector", lambda v: v.tensor_sub(gsm[0:1, 48:64], gsm[0:1, 16:32], gsm[0:1, 32:48]),
             reads=["GSM"], writes=["GSM"], tag=f"{tag}.var")
    prog.add("scalar", lambda s: s.activation(gsm[0:1, 32:48], gsm[0:1, 48:64], AF.Sqrt,
                                              bias=ctx.sb["CONSTV"][0:1, 0:1]),
             reads=["GSM", "CONSTV"], writes=["GSM"], tag=f"{tag}.sq")
    prog.add("vector", lambda v: v.reciprocal(gsm[0:1, 32:48], gsm[0:1, 32:48]),
             reads=["GSM"], writes=["GSM"], tag=f"{tag}.rc")
    def emB(t):
        t.matmul(psU[0:64, 0:8], ONES[0:1, 0:64], gsm[0:1, 32:40], start=True, stop=True)
        t.matmul(psU[64:128, 0:8], ONES[0:1, 0:64], gsm[0:1, 40:48], start=True, stop=True)
        t.matmul(psQ[0:64, 0:8], ONES[0:1, 0:64], gsm[0:1, 0:8], start=True, stop=True)
        return t.matmul(psQ[64:128, 0:8], ONES[0:1, 0:64], gsm[0:1, 8:16], start=True, stop=True)
    prog.add("tensor", emB, reads=["GSM", "cOnesF"], writes=[psU_res, psQ_res], tag=f"{tag}.bc")
    prog.add("vector", lambda v: v.tensor_mul(af_ap, psU[:, 0:8], g_sb[:, :]),
             reads=[psU_res, f"{tag}.gsb"], writes=[af_res], tag=f"{tag}.af")
    prog.add("vector", lambda v: v.tensor_mul(ctx.sb["GTMP"][:, 0:8], psQ[:, 0:8], af_ap),
             reads=[psQ_res, af_res], writes=["GTMP"], tag=f"{tag}.mb")
    prog.add("vector", lambda v: v.tensor_sub(bf_ap, b_sb[:, :], ctx.sb["GTMP"][:, 0:8]),
             reads=["GTMP", f"{tag}.bsb"], writes=[bf_res], tag=f"{tag}.bf")


def norm_transpose_tile(ctx, src_name, xi, dst_ap_fn, dst_res, af_ap, af_res, bf_ap, bf_res,
                        ptr, ptr_res, tag):
    prog = ctx.prog
    src = ctx.sb[src_name]
    for a in range(DP):
        prog.add("tensor", lambda t, a=a, xi=xi: t.transpose(
            ptr[:, :], src[:, xi * D + a * 128: xi * D + (a + 1) * 128], ctx.sb["cIdentB"][:, :]),
            reads=[R(src_name, xi), "cIdentB"], writes=[ptr_res], tag=f"{tag}.t{a}")
        prog.add("vector", lambda v, a=a: v.tensor_scalar(
            dst_ap_fn(a), ptr[:, :], af_ap[:, a:a + 1], bf_ap[:, a:a + 1], OP.mult, OP.add),
            reads=[ptr_res, af_res, bf_res], writes=[dst_res], tag=f"{tag}.n{a}")


def build_stage_A(ctx, is_last):
    nc, prog, cfg = ctx.nc, ctx.prog, ctx.cfg
    es = contextlib.ExitStack()
    TPB = ctx.TPB
    ctx.cur_es = es
    with es:
        WB = ctx.sbuf("aWB", [128, 4 * 8192], BF16)
        XMB = ctx.sbuf("aXMB", [128, TPB * D], BF16)
        GQ = ctx.sbuf("aGQ", [128, D], BF16)
        GK = ctx.sbuf("aGK", [128, D], BF16)
        GPC = ctx.sbuf("aGPC", [128, 8], F32)
        CAUS = ctx.sbuf("aCAUS", [128, 128], F32)
        G1 = ctx.sbuf("aG1", [128, DP], F32); B1 = ctx.sbuf("aB1", [128, DP], F32)
        G2 = ctx.sbuf("aG2", [128, DP], F32); B2 = ctx.sbuf("aB2", [128, DP], F32)
        AF1 = ctx.sbuf("aAF1", [128, DP], F32); BF1 = ctx.sbuf("aBF1", [128, DP], F32)
        AF2 = ctx.sbuf("aAF2", [128, DP], F32); BF2 = ctx.sbuf("aBF2", [128, DP], F32)
        GTMP = ctx.sbuf("GTMP", [128, 8], F32)
        OBR = ctx.sbuf("aOBR", [1, D], F32)
        B2R = ctx.sbuf("aB2R2", [1, D], F32)
        B1F = ctx.sbuf("aB1F2", [128, 16], F32)
        XNF = ctx.sbuf("aXNF", [128, D], BF16)
        QTK = ctx.sbuf("aQTK", [128, D], BF16)
        KTK = ctx.sbuf("aKTK", [128, D], BF16)
        VTK = ctx.sbuf("aVTK", [128, D], BF16)
        QF = ctx.sbuf("aQF", [128, D], BF16)
        KF = ctx.sbuf("aKF", [128, D], BF16)
        YF = ctx.sbuf("aYF", [128, D], BF16)
        SIMB = [ctx.sbuf(f"aSIMB{i}", [128, 128], BF16) for i in range(2)]
        SST = ctx.sbuf("aSST", [128, 8 * DH], BF16)
        RL = ctx.sbuf("aRL", [128, 8], F32)
        GSQs = ctx.sbuf("GSQs", [128, D], F32)
        GSUS = ctx.sbuf("GSUS", [128, 8], F32)
        GSQS = ctx.sbuf("GSQS", [128, 8], F32)
        GRS = ctx.sbuf("GRS", [1, 64], F32)
        ctx.sbuf("GRROW", [1, 1024], F32)
        GSM = ctx.sbuf("GSM", [1, 64], F32)
        XN2F = ctx.sbuf("aXN2F", [128, DP * 512], BF16)
        H1F = ctx.sbuf("aH1F", [128, 16 * 512], BF16)
        pTR = ctx.psum("apTR", [128, 128], BF16)
        pQ = ctx.psum("apQ", [128, D])
        pK = ctx.psum("apK", [128, D])
        pV = ctx.psum("apV", [128, D])
        PSM = ctx.psum("aPSM", [128, 512])
        pSIM = [pQ[:, 0:128], pK[:, 0:128]]
        pYH = pV[:, 0:128]
        pM = PSM[:, 0:64]
        pG = PSM[0:1, 64:96]
        pH1 = [pQ[:, 0:512], pK[:, 0:512]]

        def load_qkvo():
            for r, nm in enumerate(["aQwT", "aKwT", "aVwT", "aOwT"]):
                src = ctx.dram[nm].ap().rearrange("(a p) n -> p a n", p=128)
                dst = WB[:, r * 8192:(r + 1) * 8192].rearrange("p (a n) -> p a n", a=DP)
                prog.add("gpsimd", lambda g, dst=dst, src=src: g.dma_start(out=dst, in_=src),
                         writes=[f"WBr{r}"], dma=True, tag=f"A.ld{nm}")

        def load_w12():
            src = ctx.dram["aW1T"].ap().rearrange("(a p) n -> p a n", p=128)
            dst = WB[:, 0:16384].rearrange("p (a n) -> p a n", a=DP)
            prog.add("gpsimd", lambda g: g.dma_start(out=dst, in_=src),
                     writes=["WBr0", "WBr1"], dma=True, tag="A.ldw1")
            src2 = ctx.dram["aW2T"].ap().rearrange("(a p) n -> p a n", p=128)
            dst2 = WB[:, 16384:32768].rearrange("p (a n) -> p a n", a=16)
            prog.add("gpsimd", lambda g: g.dma_start(out=dst2, in_=src2),
                     writes=["WBr2", "WBr3"], dma=True, tag="A.ldw2")

        QWT = lambda: WB[:, 0:8192]
        KWT = lambda: WB[:, 8192:16384]
        VWT = lambda: WB[:, 16384:24576]
        OWT = lambda: WB[:, 24576:32768]
        W1T = lambda: WB[:, 0:16384]
        W2T = lambda: WB[:, 16384:32768]

        for nm, dst, res in [("cGq", GQ, "aGQ"), ("cGk", GK, "aGK")]:
            ctx.dma_in("gpsimd", dst[:, :], ctx.dram[nm].ap(), [res], f"A.{nm}")
        ctx.dma_in("sync", CAUS[:, :], ctx.dram["cCausal"].ap(), ["aCAUS"], "A.caus")
        ctx.dma_in("sync", GPC[:, :], ctx.dram["cGpc"].ap(), ["aGPC"], "A.gpc")
        for nm, dst, res in [("aGn1g", G1, "A.gn1.gsb"), ("aGn1b", B1, "A.gn1.bsb"),
                             ("aGn2g", G2, "A.gn2.gsb"), ("aGn2b", B2, "A.gn2.bsb"),
                             ("aObR", OBR, "aOBR"), ("aB2R", B2R, "aB2R2"), ("aB1F", B1F, "aB1F2")]:
            ctx.dma_in("sync", dst[:, :], ctx.dram[nm].ap(), [res], f"A.{nm}")

        XB = ctx.sb["XB"]

        for b in range(ctx.B):
            nt0 = b * TPB
            load_xb(ctx, b, f"A.{b}")
            load_qkvo()
            groupnorm_coefs(ctx, "XB", 0, G1, B1, AF1[:, :], BF1[:, :], "aAF1", "aBF1",
                            pQ, pK, pG, "apQ", "apK", GSUS, GSQS, GRS, GSM, GSQs,
                            ctx.sb["cOnesCB"], "A.gn1")
            prog.add("vector", lambda v: v.memset(SST[:, :], 0.0), writes=["aSST"], tag="A.sst0")
            for i in range(TPB):
                xi = nt0 + i
                norm_transpose_tile(ctx, "XB", i, lambda a: XNF[:, a * 128:(a + 1) * 128],
                                    "aXNF", AF1, "aAF1", BF1, "aBF1", pTR, "apTR", f"A.xn{xi}")
                for nm, ps, psres, wfn, wres in [("q", pQ, "apQ", QWT, "WBr0"),
                                                 ("k", pK, "apK", KWT, "WBr1"),
                                                 ("v", pV, "apV", VWT, "WBr2")]:
                    def emP(t, ps=ps, wfn=wfn):
                        for half in range(2):
                            for a in range(DP):
                                last = t.matmul(ps[:, half * 512:(half + 1) * 512],
                                                XNF[:, a * 128:(a + 1) * 128],
                                                wfn()[:, a * 1024 + half * 512: a * 1024 + (half + 1) * 512],
                                                start=(a == 0), stop=(a == DP - 1))
                        return last
                    prog.add("tensor", emP, reads=["aXNF", wres], writes=[psres],
                             tag=f"A.{nm}p{xi}")
                for nm, ps, psres, col, gg, ggres, dst, dres in [
                        ("q", pQ, "apQ", 0, GQ, "aGQ", QTK, "aQTK"),
                        ("k", pK, "apK", 3, GK, "aGK", KTK, "aKTK")]:
                    prog.add("scalar", lambda s, ps=ps, col=col: s.activation(
                        GSQs[:, :], ps[:, :], AF.Square, accum_out=RL[:, col:col + 1]),
                        reads=[psres], writes=["GSQs", "aRL"], tag=f"A.n2{nm}{xi}")
                    prog.add("vector", lambda v, col=col: v.tensor_scalar_max(
                        RL[:, col + 1:col + 2], RL[:, col:col + 1], 1e-24),
                        reads=["aRL"], writes=["aRL"], tag=f"A.mx{nm}{xi}")
                    prog.add("scalar", lambda s, col=col: s.activation(
                        RL[:, col + 1:col + 2], RL[:, col + 1:col + 2], AF.Sqrt),
                        reads=["aRL"], writes=["aRL"], tag=f"A.sq{nm}{xi}")
                    prog.add("vector", lambda v, col=col: v.reciprocal(
                        RL[:, col + 2:col + 3], RL[:, col + 1:col + 2]),
                        reads=["aRL"], writes=["aRL"], tag=f"A.rc{nm}{xi}")
                    prog.add("vector", lambda v, ps=ps, col=col, gg=gg, dst=dst: v.scalar_tensor_tensor(
                        dst[:, :], ps[:, :], RL[:, col + 2:col + 3], gg[:, :], OP.mult, OP.mult),
                        reads=[psres, "aRL", ggres], writes=[dres], tag=f"A.sc{nm}{xi}")
                prog.add("scalar", lambda s: s.copy(VTK[:, :], pV[:, :]),
                         reads=["apV"], writes=["aVTK"], tag=f"A.ve{xi}")
                for srcn, sres, dstn, dres in [("aQTK", "aQTK", QF, "aQF"), ("aKTK", "aKTK", KF, "aKF")]:
                    srcb = ctx.sb[srcn]
                    for a in range(DP):
                        prog.add("tensor", lambda t, a=a, srcb=srcb: t.transpose(
                            pTR[:, :], srcb[:, a * 128:(a + 1) * 128], ctx.sb["cIdentB"][:, :]),
                            reads=[sres, "cIdentB"], writes=["apTR"], tag=f"A.tf{srcn}{xi}{a}")
                        prog.add("vector", lambda v, a=a, dstn=dstn: v.tensor_copy(
                            dstn[:, a * 128:(a + 1) * 128], pTR[:, :]),
                            reads=["apTR"], writes=[dres], tag=f"A.te{srcn}{xi}{a}")
                for h in range(NHEAD):
                    j = h // 2
                    ro = (h % 2) * 64
                    sim = pSIM[h % 2]
                    simres = "apQ" if h % 2 == 0 else "apK"
                    prog.add("tensor", lambda t, sim=sim, h=h, j=j, ro=ro: t.matmul(
                        sim[:, :], KF[ro:ro + 64, j * 128:(j + 1) * 128],
                        QF[ro:ro + 64, j * 128:(j + 1) * 128], start=True, stop=True),
                        reads=["aQF", "aKF"], writes=[simres], tag=f"A.sim{xi}.{h}")
                    sb_ = SIMB[h % 2]
                    prog.add("vector", lambda v, sim=sim, sb_=sb_: v.tensor_mul(
                        sb_[:, :], sim[:, :], CAUS[:, :]),
                        reads=[simres, "aCAUS"], writes=[f"aSIMB{h % 2}"], tag=f"A.sm{xi}.{h}")
                    def emY(t, h=h, j=j, ro=ro, sb_=sb_):
                        t.matmul(pYH[ro:ro + 64, :], VTK[:, h * 64:(h + 1) * 64], sb_[:, :],
                                 start=True, stop=False)
                        return t.matmul(pYH[ro:ro + 64, :],
                                        SST[ro:ro + 64, j * 64:(j + 1) * 64],
                                        QF[ro:ro + 64, j * 128:(j + 1) * 128],
                                        start=False, stop=True)
                    prog.add("tensor", emY, reads=["aVTK", f"aSIMB{h % 2}", "aSST", "aQF"],
                             writes=["apV"], tag=f"A.yh{xi}.{h}")
                    prog.add("tensor", lambda t, h=h, ro=ro: t.matmul(
                        pM[ro:ro + 64, :], KTK[:, h * 64:(h + 1) * 64],
                        VTK[:, h * 64:(h + 1) * 64], start=True, stop=True),
                        reads=["aKTK", "aVTK"], writes=["aPSM"], tag=f"A.m{xi}.{h}")
                    if h % 2 == 1:
                        prog.add("scalar", lambda s, j=j: s.copy(
                            YF[:, j * 128:(j + 1) * 128], pYH[:, :]),
                            reads=["apV"], writes=["aYF"], tag=f"A.ye{xi}.{j}")
                        prog.add("vector", lambda v, j=j: v.tensor_add(
                            SST[:, j * 64:(j + 1) * 64], SST[:, j * 64:(j + 1) * 64], pM[:, :]),
                            reads=["aPSM", "aSST"], writes=["aSST"], tag=f"A.su{xi}.{j}")
                        prog.add("vector", lambda v, j=j: v.tensor_scalar(
                            SST[:, j * 64:(j + 1) * 64], SST[:, j * 64:(j + 1) * 64],
                            GPC[:, j:j + 1], None, OP.mult),
                            reads=["aSST", "aGPC"], writes=["aSST"], tag=f"A.sg{xi}.{j}")
                def emO(t):
                    for half in range(2):
                        for a in range(DP):
                            t.matmul(pQ[:, half * 512:(half + 1) * 512],
                                     YF[:, a * 128:(a + 1) * 128],
                                     OWT()[:, a * 1024 + half * 512: a * 1024 + (half + 1) * 512],
                                     start=(a == 0), stop=False)
                        last = t.matmul(pQ[:, half * 512:(half + 1) * 512],
                                        ctx.sb["cOnesF"][0:1, 0:128],
                                        OBR[0:1, half * 512:(half + 1) * 512],
                                        start=False, stop=True)
                    return last
                prog.add("tensor", emO, reads=["aYF", "WBr3", "aOBR", "cOnesF"],
                         writes=["apQ"], tag=f"A.op{xi}")
                prog.add("vector", lambda v, i=i, xi=xi: v.tensor_add(
                    XMB[:, i * D:(i + 1) * D], pQ[:, :], XB[:, i * D:(i + 1) * D]),
                    reads=["apQ", R("XB", i)], writes=[R("aXMB", i)], tag=f"A.xm{xi}")
            load_w12()
            groupnorm_coefs(ctx, "aXMB", 0, G2, B2, AF2[:, :], BF2[:, :], "aAF2", "aBF2",
                            pQ, pK, pG, "apQ", "apK", GSUS, GSQS, GRS, GSM, GSQs,
                            ctx.sb["cOnesCB"], "A.gn2")
            for sl in range(TPB // 4):
                for jj in range(4):
                    i = sl * 4 + jj
                    norm_transpose_tile(ctx, "aXMB", i,
                                        lambda a, jj=jj: XN2F[:, a * 512 + jj * 128: a * 512 + (jj + 1) * 128],
                                        "aXN2F", AF2, "aAF2", BF2, "aBF2", pTR, "apTR",
                                        f"A.x2{b}.{i}")
                for oc in range(16):
                    ph = pH1[oc % 2]
                    def emH(t, oc=oc, ph=ph):
                        for a in range(DP):
                            last = t.matmul(ph[:, :],
                                            W1T()[:, a * 2048 + oc * 128: a * 2048 + (oc + 1) * 128],
                                            XN2F[:, a * 512:(a + 1) * 512],
                                            start=(a == 0), stop=(a == DP - 1))
                        return last
                    prog.add("tensor", emH, reads=["aXN2F", "WBr0", "WBr1"],
                             writes=["apQ" if oc % 2 == 0 else "apK"],
                             tag=f"A.h1{b}.{sl}.{oc}")
                    prog.add("scalar", lambda s, oc=oc, ph=ph: s.activation(
                        H1F[:, oc * 512:(oc + 1) * 512], ph[:, :], AF.Silu,
                        bias=B1F[:, oc:oc + 1], scale=1.0),
                        reads=["apQ" if oc % 2 == 0 else "apK", "aB1F2"],
                        writes=[R("aH1F", oc)], tag=f"A.si{b}.{sl}.{oc}")
                for jj in range(4):
                    i = sl * 4 + jj
                    xi = nt0 + i
                    def emF(t, jj=jj):
                        for half in range(2):
                            for oc in range(16):
                                t.matmul(pV[:, half * 512:(half + 1) * 512],
                                         H1F[:, oc * 512 + jj * 128: oc * 512 + (jj + 1) * 128],
                                         W2T()[:, oc * 1024 + half * 512: oc * 1024 + (half + 1) * 512],
                                         start=(oc == 0), stop=False)
                            last = t.matmul(pV[:, half * 512:(half + 1) * 512],
                                            ctx.sb["cOnesF"][0:1, 0:128],
                                            B2R[0:1, half * 512:(half + 1) * 512],
                                            start=False, stop=True)
                        return last
                    prog.add("tensor", emF,
                             reads=[R("aH1F", oc) for oc in range(16)] + ["WBr2", "WBr3", "aB2R2", "cOnesF"],
                             writes=["apV"], tag=f"A.w2{b}.{sl}.{jj}")
                    prog.add("vector", lambda v, i=i, xi=xi: v.tensor_add(
                        ctx.sb["LNZ"][:, :], pV[:, :], XB[:, i * D:(i + 1) * D]),
                        reads=["apV", R("XB", i)], writes=["LNZ"], tag=f"A.z1{xi}")
                    prog.add("vector", lambda v, i=i: v.tensor_add(
                        ctx.sb["LNZ"][:, :], ctx.sb["LNZ"][:, :], XMB[:, i * D:(i + 1) * D]),
                        reads=["LNZ", R("aXMB", i)], writes=["LNZ"], tag=f"A.z2{xi}")
                    emit_ln_out(ctx, xi, is_last, f"A.ln{xi}")


    ctx.cur_es = None

def build_stage_B(ctx, is_last):
    nc, prog, cfg = ctx.nc, ctx.prog, ctx.cfg
    es = contextlib.ExitStack()
    TPB = ctx.TPB
    T = ctx.T
    K = T // 4
    NRS = K // 128
    RG = max(1, K // 256)
    RPG = K // RG
    NSG = RPG // 128
    NBIS = 36
    ctx.cur_es = es
    with es:
        GWTs = ctx.sbuf("bGWTs", [128, DP * 8], BF16)
        XF1 = ctx.sbuf("bXF1", [128, DP * 128], BF16)
        EXPT = ctx.sbuf("bEXPT", [128, 8], F32)
        RLB = ctx.sbuf("bRLB", [128, 4], F32)
        AFFT = ctx.sbuf("bAFFT", [128, ctx.B * TPB * 8], F32)
        AFFE = ctx.sbuf("bAFFE", [64, T], F32)
        THL = ctx.sbuf("bTHL", [64, 8], F32)
        THR = ctx.sbuf("bTHR", [1, 16], F32)
        THBS = ctx.sbuf("bTHBS", [128, 8], F32)
        MSKS = ctx.sbuf("bMSKS", [128, 8], F32)
        RSUM = ctx.sbuf("bRSUM", [1, 8], F32)
        RKT = ctx.sbuf("bRKT", [128, ctx.B * TPB * 8], F32)
        IDENTF = ctx.sbuf("bIDF", [128, 128], F32)
        LTRIS = ctx.sbuf("bLTRI", [128, 128], F32)
        IOTA512 = ctx.sbuf("bIOTA", [128, K], F32)
        PTS = ctx.sbuf("bPTS", [128, TPB * 256], BF16)
        PTB = ctx.sbuf("bPTB", [128, K], BF16)
        PSL = ctx.sbuf("bPSL", [128, NRS * 128], BF16)
        XST = ctx.sbuf("bXST", [128, DP * 512], BF16)
        SIL = ctx.sbuf("bSIL", [128, K], BF16)
        HT = ctx.sbuf("bHT", [128, 32 * 512], BF16)
        YB = ctx.sbuf("bYB", [128, 8 * NRS * 1024], BF16)
        WG = ctx.sbuf("bWG", [128, 2 * DP * 512], BF16)
        WU = ctx.sbuf("bWU", [128, 2 * DP * 512], BF16)
        WD = ctx.sbuf("bWD", [128, 2 * 512], BF16)
        pTR = [ctx.psum("bpTR0", [128, 128], BF16), ctx.psum("bpTR1", [128, 128], BF16)]
        bpA = ctx.psum("bpA", [128, 512])
        bpB = ctx.psum("bpB", [128, 512])
        Y4 = [ctx.psum(f"bY{c}", [128, 512]) for c in "abcd"]
        Y4r = [f"bY{c}" for c in "abcd"]

        XB = ctx.sb["XB"]
        gw_src = ctx.dram["bGwT"].ap().rearrange("(a p) n -> p a n", p=128)
        gw_dst = GWTs[:, :].rearrange("p (a n) -> p a n", a=DP)
        prog.add("gpsimd", lambda g, d=gw_dst, s=gw_src: g.dma_start(out=d, in_=s),
            writes=["bGWTs"], dma=True, tag="B.ldgw")
        ctx.dma_in("sync", IDENTF[:, :], ctx.dram["cIdent"].ap(), ["bIDF"], "B.idf")
        ctx.dma_in("sync", LTRIS[:, :], ctx.dram["cLtri"].ap(), ["bLTRI"], "B.ltri")
        prog.add("gpsimd", lambda g: g.iota(IOTA512[:, :], [[1, K]], base=1,
                                            channel_multiplier=0,
                                            allow_small_or_imprecise_dtypes=True),
                 writes=["bIOTA"], tag="B.iota")
        prog.add("vector", lambda v: v.memset(AFFE[:, :], 0.5), writes=["bAFFE"], tag="B.ae0")

        for b in range(ctx.B):
            load_xb(ctx, b, f"Br.{b}")
            for i in range(TPB):
                xi = b * TPB + i
                ci = (b * TPB + i) * 8
                for a in range(DP):
                    transpose_tile(ctx, XF1[:, a * 128:(a + 1) * 128],
                                   XB[:, i * D + a * 128: i * D + (a + 1) * 128],
                                   pTR, ctx.sb["cIdentB"][:, :],
                                   reads=[R("XB", i)], writes=["bXF1"], pres="bpTR",
                                   tag=f"B.xf{xi}.{a}", slot=a % 2)
                def emL(t):
                    for a in range(DP):
                        last = t.matmul(bpA[:, 0:8], XF1[:, a * 128:(a + 1) * 128],
                                        GWTs[:, a * 8:(a + 1) * 8],
                                        start=(a == 0), stop=(a == DP - 1))
                    return last
                prog.add("tensor", emL, reads=["bXF1", "bGWTs"], writes=["bbpA"],
                         tag=f"B.lg{xi}")
                prog.add("vector", lambda v: v.tensor_copy(EXPT[:, :], bpA[:, 0:8]),
                         reads=["bbpA"], writes=["bEXPT"], tag=f"B.sm0{xi}")
                prog.add("vector", lambda v: v.reduce_max(RLB[:, 0:1], EXPT[:, :],
                                                          axis=AX.X, negate=True),
                         reads=["bEXPT"], writes=["bRLB"], tag=f"B.sm1{xi}")
                prog.add("scalar", lambda s: s.activation(EXPT[:, :], EXPT[:, :], AF.Exp,
                                                          bias=RLB[:, 0:1]),
                         reads=["bEXPT", "bRLB"], writes=["bEXPT"], tag=f"B.sm2{xi}")
                prog.add("vector", lambda v: v.reduce_sum(RLB[:, 1:2], EXPT[:, :], axis=AX.X),
                         reads=["bEXPT"], writes=["bRLB"], tag=f"B.sm3{xi}")
                prog.add("vector", lambda v: v.reciprocal(RLB[:, 2:3], RLB[:, 1:2]),
                         reads=["bRLB"], writes=["bRLB"], tag=f"B.sm4{xi}")
                prog.add("vector", lambda v, ci=ci: v.tensor_scalar(
                    AFFT[:, ci:ci + 8], EXPT[:, :], RLB[:, 2:3], None, OP.mult),
                    reads=["bEXPT", "bRLB"], writes=["bAFFT"], tag=f"B.sm5{xi}")
                prog.add("tensor", lambda t, ci=ci: t.transpose(
                    bpB[0:8, 0:128], AFFT[:, ci:ci + 8], IDENTF[:, :]),
                    reads=["bAFFT", "bIDF"], writes=["bbpB"], tag=f"B.tr{xi}")
                prog.add("vector", lambda v, b=b, i=i: v.tensor_copy(
                    AFFE[b * 32:b * 32 + 8, i * 128:(i + 1) * 128], bpB[0:8, 0:128]),
                    reads=["bbpB"], writes=["bAFFE"], tag=f"B.ae{xi}")
        prog.add("vector", lambda v: v.memset(THL[:, 0:1], 0.0), writes=["bTHL"], tag="B.th0")
        prog.add("vector", lambda v: v.memset(THL[:, 1:2], 1.0),
                 reads=["bTHL"], writes=["bTHL"], tag="B.th1")
        BSS = PTS[0:64, 0:T]
        for it in range(NBIS):
            prog.add("vector", lambda v: v.tensor_add(THL[:, 2:3], THL[:, 0:1], THL[:, 1:2]),
                     reads=["bTHL"], writes=["bTHL"], tag=f"B.b0.{it}")
            prog.add("vector", lambda v: v.tensor_scalar(THL[:, 2:3], THL[:, 2:3], 0.5, None, OP.mult),
                     reads=["bTHL"], writes=["bTHL"], tag=f"B.b1.{it}")
            prog.add("vector", lambda v: v.tensor_scalar(
                BSS, AFFE[:, :], THL[:, 2:3], None, OP.is_ge, OP.add, accum_out=THL[:, 3:4]),
                reads=["bAFFE", "bTHL"], writes=["bPTS", "bTHL"], tag=f"B.b2.{it}")
            prog.add("vector", lambda v: v.tensor_single_scalar(
                THL[:, 4:5], THL[:, 3:4], float(K), OP.is_ge),
                reads=["bTHL"], writes=["bTHL"], tag=f"B.b3.{it}")
            prog.add("vector", lambda v: v.tensor_sub(THL[:, 5:6], THL[:, 2:3], THL[:, 0:1]),
                     reads=["bTHL"], writes=["bTHL"], tag=f"B.b4.{it}")
            prog.add("vector", lambda v: v.scalar_tensor_tensor(
                THL[:, 0:1], THL[:, 5:6], THL[:, 4:5], THL[:, 0:1], OP.mult, OP.add),
                reads=["bTHL"], writes=["bTHL"], tag=f"B.b5.{it}")
            prog.add("vector", lambda v: v.tensor_sub(THL[:, 6:7], THL[:, 1:2], THL[:, 2:3]),
                     reads=["bTHL"], writes=["bTHL"], tag=f"B.b6.{it}")
            prog.add("vector", lambda v: v.scalar_tensor_tensor(
                THL[:, 1:2], THL[:, 6:7], THL[:, 4:5], THL[:, 2:3], OP.mult, OP.add),
                reads=["bTHL"], writes=["bTHL"], tag=f"B.b7.{it}")
        for b in range(ctx.B):
            prog.add("tensor", lambda t, b=b: t.transpose(
                bpB[0:1, 0:8], THL[b * 32:b * 32 + 8, 0:1], IDENTF[b * 32:b * 32 + 8, b * 32:b * 32 + 8]),
                reads=["bTHL", "bIDF"], writes=["bbpB"], tag=f"B.tt{b}")
            prog.add("vector", lambda v, b=b: v.tensor_copy(THR[0:1, b * 8:(b + 1) * 8],
                                                            bpB[0:1, 0:8]),
                     reads=["bbpB"], writes=["bTHR"], tag=f"B.te{b}")
        for b in range(ctx.B):
            prog.add("tensor", lambda t, b=b: t.matmul(
                bpA[:, 0:8], ctx.sb["cOnesF"][0:1, 0:128], THR[0:1, b * 8:(b + 1) * 8],
                start=True, stop=True),
                reads=["bTHR", "cOnesF"], writes=["bbpA"], tag=f"B.tb{b}")
            prog.add("vector", lambda v: v.tensor_copy(THBS[:, :], bpA[:, 0:8]),
                     reads=["bbpA"], writes=["bTHBS"], tag=f"B.tb2{b}")
            prog.add("vector", lambda v: v.memset(RSUM[:, :], 0.0),
                     reads=["bRSUM"], writes=["bRSUM"], tag=f"B.rs0{b}")
            for i in range(TPB):
                ci = (b * TPB + i) * 8
                prog.add("vector", lambda v, ci=ci: v.tensor_tensor(
                    MSKS[:, :], AFFT[:, ci:ci + 8], THBS[:, :], OP.is_ge),
                    reads=["bAFFT", "bTHBS"], writes=["bMSKS"], tag=f"B.mk{b}.{i}")
                def emRk(t):
                    t.matmul(bpA[:, 0:8], LTRIS[:, :], MSKS[:, :], start=True, stop=False)
                    return t.matmul(bpA[:, 0:8], ctx.sb["cOnesF"][0:1, 0:128],
                                    RSUM[0:1, 0:8], start=False, stop=True)
                prog.add("tensor", emRk, reads=["bMSKS", "bRSUM", "bLTRI", "cOnesF"],
                         writes=["bbpA"], tag=f"B.rk{b}.{i}")
                prog.add("vector", lambda v, ci=ci: v.tensor_mul(
                    RKT[:, ci:ci + 8], bpA[:, 0:8], MSKS[:, :]),
                    reads=["bbpA", "bMSKS"], writes=["bRKT"], tag=f"B.rm{b}.{i}")
                prog.add("tensor", lambda t: t.matmul(
                    bpB[0:1, 0:8], ctx.sb["cOnesF"][:, 0:1], MSKS[:, :], start=True, stop=True),
                    reads=["bMSKS", "cOnesF"], writes=["bbpB"], tag=f"B.cs{b}.{i}")
                prog.add("vector", lambda v: v.tensor_add(RSUM[0:1, 0:8], RSUM[0:1, 0:8],
                                                          bpB[0:1, 0:8]),
                         reads=["bbpB", "bRSUM"], writes=["bRSUM"], tag=f"B.ru{b}.{i}")
        for b in range(ctx.B):
            load_xb(ctx, b, f"Be.{b}")
            for e in range(8):
                for rg in range(RG):
                    for i in range(TPB):
                        ci = (b * TPB + i) * 8 + e
                        prog.add("vector", lambda v, i=i, ci=ci, rg=rg: v.tensor_scalar(
                            PTS[:, i * RPG:(i + 1) * RPG], IOTA512[:, rg * RPG:(rg + 1) * RPG],
                            RKT[:, ci:ci + 1], None, OP.is_equal),
                            reads=["bIOTA", "bRKT"], writes=[R("bPTS", i), "bPTS"],
                            tag=f"B.pt{b}.{e}.{rg}.{i}")
                    g4 = [bpA[:, 0:RPG], bpB[:, 0:RPG], Y4[0][:, 0:RPG], Y4[1][:, 0:RPG]]
                    g4res = ["bbpA", "bbpB", Y4r[0], Y4r[1]]
                    for pss in range(2):
                        for i in range(TPB):
                            def emG(t, i=i, pss=pss):
                                for dc in range(4):
                                    last = t.matmul(g4[dc],
                                                    XB[:, i * D + (pss * 4 + dc) * 128: i * D + (pss * 4 + dc + 1) * 128],
                                                    PTS[:, i * RPG:(i + 1) * RPG],
                                                    start=(i == 0), stop=(i == TPB - 1))
                                return last
                            prog.add("tensor", emG,
                                     reads=[R("XB", i), R("bPTS", i)],
                                     writes=list(set(g4res)), tag=f"B.g{pss}{b}.{e}.{rg}.{i}")
                        for dc in range(4):
                            prog.add("scalar", lambda s, dc=dc, pss=pss, rg=rg: s.copy(
                                XST[:, (pss * 4 + dc) * 512 + rg * 256: (pss * 4 + dc) * 512 + rg * 256 + RPG],
                                g4[dc]),
                                reads=[g4res[dc]], writes=["bXST"], tag=f"B.ge{pss}{b}.{e}.{rg}.{dc}")
                for ic in range(8):
                    slot = ic % 2
                    for wnm, wsb, wres in [("bWgT", WG, "bWG"), ("bWuT", WU, "bWU")]:
                        srcw = ctx.dram[wnm].ap()[e, :, ic * 512:(ic + 1) * 512].rearrange(
                            "(a p) n -> p a n", p=128)
                        dstw = wsb[:, slot * 4096:(slot + 1) * 4096].rearrange(
                            "p (a n) -> p a n", a=DP)
                        prog.add("gpsimd", lambda g, dst=dstw, src=srcw: g.dma_start(out=dst, in_=src),
                                 writes=[R(wres, slot)], dma=True, tag=f"B.lw{wnm}{b}.{e}.{ic}")
                    for isub in range(4):
                        def emGm(t, ic=ic, isub=isub, slot=slot):
                            for a in range(DP):
                                last = t.matmul(bpA[:, 0:K],
                                                WG[:, slot * 4096 + a * 512 + isub * 128: slot * 4096 + a * 512 + (isub + 1) * 128],
                                                XST[:, a * 512: a * 512 + K],
                                                start=(a == 0), stop=(a == DP - 1))
                            return last
                        prog.add("tensor", emGm, reads=[R("bWG", slot), "bXST"],
                                 writes=["bbpA"], tag=f"B.hg{b}.{e}.{ic}.{isub}")
                        prog.add("scalar", lambda s: s.activation(SIL[:, 0:K], bpA[:, 0:K], AF.Silu),
                                 reads=["bbpA"], writes=["bSIL"], tag=f"B.hs{b}.{e}.{ic}.{isub}")
                        def emUm(t, ic=ic, isub=isub, slot=slot):
                            for a in range(DP):
                                last = t.matmul(bpB[:, 0:K],
                                                WU[:, slot * 4096 + a * 512 + isub * 128: slot * 4096 + a * 512 + (isub + 1) * 128],
                                                XST[:, a * 512: a * 512 + K],
                                                start=(a == 0), stop=(a == DP - 1))
                            return last
                        prog.add("tensor", emUm, reads=[R("bWU", slot), "bXST"],
                                 writes=["bbpB"], tag=f"B.hu{b}.{e}.{ic}.{isub}")
                        hidx = ic * 4 + isub
                        prog.add("vector", lambda v, hidx=hidx: v.tensor_mul(
                            HT[:, hidx * 512: hidx * 512 + K],
                            SIL[:, 0:K], bpB[:, 0:K]),
                            reads=["bSIL", "bbpB"], writes=[R("bHT", hidx)],
                            tag=f"B.hm{b}.{e}.{ic}.{isub}")
                for dh in range(2):
                    for it in range(32):
                        slot = it % 2
                        srcw = ctx.dram["bWdT"].ap()[e, it * 128:(it + 1) * 128, dh * 512:(dh + 1) * 512]
                        prog.add("gpsimd", lambda g, slot=slot, src=srcw: g.dma_start(
                            out=WD[:, slot * 512:(slot + 1) * 512], in_=src),
                            writes=[R("bWD", slot)], dma=True, tag=f"B.lwd{b}.{e}.{dh}.{it}")
                        def emD(t, it=it, slot=slot):
                            for rsub in range(NRS):
                                last = t.matmul(Y4[rsub][:, 0:512],
                                                HT[:, it * 512 + rsub * 128: it * 512 + rsub * 128 + 128],
                                                WD[:, slot * 512:(slot + 1) * 512],
                                                start=(it == 0), stop=(it == 31))
                            return last
                        prog.add("tensor", emD,
                                 reads=[R("bHT", it), R("bWD", slot)],
                                 writes=[Y4r[rsub] for rsub in range(NRS)],
                                 tag=f"B.dn{b}.{e}.{dh}.{it}")
                    for rsub in range(NRS):
                        prog.add("vector", lambda v, rsub=rsub, dh=dh, e=e: v.tensor_copy(
                            YB[:, (e * NRS + rsub) * 1024 + dh * 512:(e * NRS + rsub) * 1024 + dh * 512 + 512],
                            Y4[rsub][:, 0:512]),
                            reads=[Y4r[rsub]],
                            writes=[R("bYB", e * NRS + rsub)], tag=f"B.ye{b}.{e}.{dh}.{rsub}")
            for i in range(TPB):
                xi = b * TPB + i
                for e in range(8):
                    ci = (b * TPB + i) * 8 + e
                    prog.add("vector", lambda v, ci=ci: v.tensor_scalar(
                        PTB[:, :], IOTA512[:, :], RKT[:, ci:ci + 1], AFFT[:, ci:ci + 1],
                        OP.is_equal, OP.mult),
                        reads=["bIOTA", "bRKT", "bAFFT"], writes=["bPTB"], tag=f"B.sp{b}.{i}.{e}")
                    for rsub in range(NRS):
                        pa = rsub % 2
                        ph_ = pTR[pa][:, 0:128]
                        prog.add("tensor", lambda t, rsub=rsub, ph_=ph_: t.transpose(
                            ph_, PTB[:, rsub * 128:(rsub + 1) * 128], ctx.sb["cIdentB"][:, :]),
                            reads=["bPTB", "cIdentB"], writes=[R("bpTR", pa)], tag=f"B.st{b}.{i}.{e}.{rsub}")
                        prog.add("vector", lambda v, rsub=rsub, ph_=ph_: v.tensor_copy(
                            PSL[:, rsub * 128:(rsub + 1) * 128], ph_),
                            reads=[R("bpTR", pa)], writes=["bPSL"], tag=f"B.se{b}.{i}.{e}.{rsub}")
                    def emSc(t, e=e):
                        for rsub in range(NRS):
                            for dh in range(2):
                                last = t.matmul(Y4[dh][:, 0:512],
                                                PSL[:, rsub * 128:(rsub + 1) * 128],
                                                YB[:, (e * NRS + rsub) * 1024 + dh * 512: (e * NRS + rsub) * 1024 + dh * 512 + 512],
                                                start=(e == 0 and rsub == 0), stop=(e == 7 and rsub == NRS - 1))
                        return last
                    prog.add("tensor", emSc,
                             reads=["bPSL"] + [R("bYB", e * NRS + rsub) for rsub in range(NRS)],
                             writes=[Y4r[0], Y4r[1]], tag=f"B.sc{b}.{i}.{e}")
                for dh in range(2):
                    prog.add("vector", lambda v, i=i, dh=dh: v.tensor_add(
                        ctx.sb["LNZ"][:, dh * 512:(dh + 1) * 512], Y4[dh][:, 0:512],
                        XB[:, i * D + dh * 512: i * D + dh * 512 + 512]),
                        reads=[Y4r[dh], R("XB", i)], writes=["LNZ"], tag=f"B.z{xi}.{dh}")
                emit_ln_out(ctx, xi, is_last, f"B.ln{xi}")

    ctx.cur_es = None

def build_stage_C(ctx, is_last):
    nc, prog, cfg = ctx.nc, ctx.prog, ctx.cfg
    es = contextlib.ExitStack()
    TPB = ctx.TPB
    NTOK = ctx.B * ctx.T
    NTOT = cfg["n_cores"] * NTOK if cfg["collective"] else NTOK
    NSL = TPB // 4
    ctx.cur_es = es
    with es:
        WC = ctx.sbuf("cWC", [128, 4 * 8192], BF16)
        XF = ctx.sbuf("cXF", [128, DP * 512], BF16)
        HB = ctx.sbuf("cHB", [128, 2 * 512], BF16)
        CSQ = ctx.sbuf("cCSQ", [128, 2 * 512], F32)
        SSTAT = ctx.sbuf("cSSTAT", [128, 48], F32)
        GSTAT = ctx.sbuf("cGSTAT", [128, 48], F32)
        RT = ctx.sbuf("cRT", [128, 4], F32)
        CF = ctx.sbuf("cCF", [128, 48], F32)
        A3 = ctx.sbuf("cA3", [128, 24], F32)
        B3 = ctx.sbuf("cB3", [128, 24], F32)
        VF = ctx.sbuf("cVF", [128, DP * 512], BF16)
        QBF = ctx.sbuf("cQBF", [128, DP * 512], BF16)
        KBF = ctx.sbuf("cKBF", [128, DP * 512], BF16)
        QKF = ctx.sbuf("cQKF", [128, DP * 512], BF16)
        OUTF = ctx.sbuf("cOUTF", [128, DP * 512], BF16)
        VTOK = ctx.sbuf("cVTOK", [128, TPB * D], BF16)
        VVS = ctx.sbuf("cVVS", [128, 8 * DH], BF16)
        BPR = ctx.sbuf("cBPR", [1, D], F32)
        pTR = [ctx.psum("cpTR0", [128, 128], BF16), ctx.psum("cpTR1", [128, 128], BF16)]
        pB0 = ctx.psum("cpB0", [128, 512])
        pB1 = ctx.psum("cpB1", [128, 512])
        pO = ctx.psum("cpO", [128, D])
        hq = nc.dram_tensor("c_hq", [D, NTOK], BF16)
        hk = nc.dram_tensor("c_hk", [D, NTOK], BF16)
        hv = nc.dram_tensor("c_hv", [D, NTOK], BF16)
        ccin = nc.dram_tensor("c_ccin", [128, 48], F32)
        ccout = nc.dram_tensor("c_ccout", [128, 48], F32)

        for r, nm in enumerate(["cWqT", "cWkT", "cWvT", "cWpT"]):
            src = ctx.dram[nm].ap().rearrange("(a p) n -> p a n", p=128)
            dst = WC[:, r * 8192:(r + 1) * 8192].rearrange("p (a n) -> p a n", a=DP)
            prog.add("gpsimd", lambda g, dst=dst, src=src: g.dma_start(out=dst, in_=src),
                     writes=[f"cWC{r}"], dma=True, tag=f"C.ld{nm}")
        PAR = {}
        for nm in ["cGq_g", "cGq_b", "cGk_g", "cGk_b", "cGv_g", "cGv_b"]:
            t = ctx.sbuf(nm + "s", [128, DP], F32)
            ctx.dma_in("sync", t[:, :], ctx.dram[nm].ap(), [nm + "s"], f"C.{nm}")
            PAR[nm] = t
        ctx.dma_in("sync", BPR[:, :], ctx.dram["cBpR"].ap(), ["cBPR"], "C.bp")
        prog.add("vector", lambda v: v.memset(SSTAT[:, :], 0.0), writes=["cSSTAT"], tag="C.st0")

        XB = ctx.sb["XB"]
        branches = [("q", hq, 0, 0), ("k", hk, 1, 1), ("v", hv, 2, 2)]
        for b in range(ctx.B):
            load_xb(ctx, b, f"C1.{b}")
            for s in range(NSL):
                for j in range(4):
                    i_loc = s * 4 + j
                    xi = b * TPB + i_loc
                    for a in range(DP):
                        transpose_tile(
                            ctx, XF[:, a * 512 + j * 128: a * 512 + (j + 1) * 128],
                            XB[:, i_loc * D + a * 128: i_loc * D + (a + 1) * 128],
                            pTR, ctx.sb["cIdentB"][:, :],
                            reads=[R("XB", i_loc)], writes=[R("cXF", a)], pres="cpTR",
                            tag=f"C.xf{xi}.{a}", slot=a % 2)
                for bi, (bn, hd, wr, sc0) in enumerate(branches):
                    for oc in range(DP):
                        pb = pB0 if oc % 2 == 0 else pB1
                        pres = "cpB0" if oc % 2 == 0 else "cpB1"
                        def emH(t, wr=wr, oc=oc, pb=pb):
                            for a in range(DP):
                                last = t.matmul(pb[:, :],
                                                WC[:, wr * 8192 + a * 1024 + oc * 128: wr * 8192 + a * 1024 + (oc + 1) * 128],
                                                XF[:, a * 512:(a + 1) * 512],
                                                start=(a == 0), stop=(a == DP - 1))
                            return last
                        prog.add("tensor", emH,
                                 reads=[R("cXF", a) for a in range(DP)] + [f"cWC{wr}"],
                                 writes=[pres], tag=f"C.h{bn}{b}.{s}.{oc}")
                        par = oc % 2
                        hb = HB[:, par * 512:(par + 1) * 512]
                        csq = CSQ[:, par * 512:(par + 1) * 512]
                        prog.add("scalar", lambda sc, pb=pb, hb=hb: sc.copy(hb, pb[:, :]),
                                 reads=[pres], writes=[R("cHB", par)], tag=f"C.he{bn}{b}.{s}.{oc}")
                        dst = hd.ap()[oc * 128:(oc + 1) * 128, b * ctx.T + s * 512: b * ctx.T + (s + 1) * 512]
                        ctx.dma_out("sync", dst, hb, reads=[R("cHB", par)],
                                    writes=[R(f"c_h{bn}", (b, s, oc))], tag=f"C.hs{bn}{b}.{s}.{oc}")
                        sumc = SSTAT[:, bi * 16 + oc: bi * 16 + oc + 1]
                        sqc = SSTAT[:, bi * 16 + 8 + oc: bi * 16 + 8 + oc + 1]
                        prog.add("scalar", lambda sc, pb=pb, csq=csq, par=par: sc.activation(
                                 csq, pb[:, :], AF.Identity, accum_out=RT[:, 2 * par:2 * par + 1]),
                                 reads=[pres], writes=[R("cCSQ", par), R("cRT", par)], tag=f"C.r1{bn}{b}.{s}.{oc}")
                        prog.add("vector", lambda v, sumc=sumc, par=par: v.tensor_add(
                                 sumc, sumc, RT[:, 2 * par:2 * par + 1]),
                                 reads=[R("cRT", par), "cSSTAT"], writes=["cSSTAT"], tag=f"C.r2{bn}{b}.{s}.{oc}")
                        prog.add("scalar", lambda sc, pb=pb, csq=csq, par=par: sc.activation(
                                 csq, pb[:, :], AF.Square, accum_out=RT[:, 2 * par + 1:2 * par + 2]),
                                 reads=[pres, R("cCSQ", par)], writes=[R("cCSQ", par), R("cRT", par)], tag=f"C.r3{bn}{b}.{s}.{oc}")
                        prog.add("vector", lambda v, sqc=sqc, par=par: v.tensor_add(
                                 sqc, sqc, RT[:, 2 * par + 1:2 * par + 2]),
                                 reads=[R("cRT", par), "cSSTAT"], writes=["cSSTAT"], tag=f"C.r4{bn}{b}.{s}.{oc}")
        if cfg["collective"]:
            ctx.dma_out("gpsimd", ccin.ap(), SSTAT[:, :], reads=["cSSTAT"], writes=["c_ccin"], tag="C.cci")
            prog.add("gpsimd", lambda g: g.collective_compute(
                "AllReduce", OP.add, replica_groups=[list(range(cfg["n_cores"]))],
                ins=[ccin.ap()], outs=[ccout.ap()]),
                reads=["c_ccin"], writes=["c_ccout"], dma="cc", tag="C.cc")
            ctx.dma_out("gpsimd", GSTAT[:, :], ccout.ap(), reads=["c_ccout"],
                        writes=["cGSTAT"], tag="C.cco")
        else:
            prog.add("vector", lambda v: v.tensor_copy(GSTAT[:, :], SSTAT[:, :]),
                     reads=["cSSTAT"], writes=["cGSTAT"], tag="C.cc")
        for bi, (bn, hd, wr, sc0) in enumerate(branches):
            g_sb = PAR[f"cG{bn}_g"]
            b_sb = PAR[f"cG{bn}_b"]
            mu = CF[:, bi * 16:bi * 16 + 8]
            e2 = CF[:, bi * 16 + 8:bi * 16 + 16]
            a3 = A3[:, bi * 8:(bi + 1) * 8]
            b3 = B3[:, bi * 8:(bi + 1) * 8]
            prog.add("vector", lambda v, bi=bi, mu=mu: v.tensor_scalar(
                mu, GSTAT[:, bi * 16:bi * 16 + 8], 1.0 / NTOT, None, OP.mult),
                reads=["cGSTAT"], writes=["cCF"], tag=f"C.c1{bn}")
            prog.add("vector", lambda v, bi=bi, e2=e2: v.tensor_scalar(
                e2, GSTAT[:, bi * 16 + 8:bi * 16 + 16], 1.0 / NTOT, None, OP.mult),
                reads=["cGSTAT"], writes=["cCF"], tag=f"C.c2{bn}")
            prog.add("vector", lambda v, mu=mu, e2=e2, a3=a3: v.tensor_mul(a3, mu, mu),
                     reads=["cCF"], writes=["cA3"], tag=f"C.c3{bn}")
            prog.add("vector", lambda v, e2=e2, a3=a3: v.tensor_sub(a3, e2, a3),
                     reads=["cCF", "cA3"], writes=["cA3"], tag=f"C.c4{bn}")
            prog.add("scalar", lambda s, a3=a3: s.activation(a3, a3, AF.Sqrt,
                                                             bias=ctx.sb["CONSTV"][:, 0:1]),
                     reads=["cA3", "CONSTV"], writes=["cA3"], tag=f"C.c5{bn}")
            prog.add("vector", lambda v, a3=a3: v.reciprocal(a3, a3),
                     reads=["cA3"], writes=["cA3"], tag=f"C.c6{bn}")
            prog.add("vector", lambda v, a3=a3, g_sb=g_sb: v.tensor_mul(a3, a3, g_sb[:, :]),
                     reads=["cA3", f"cG{bn}_gs"], writes=["cA3"], tag=f"C.c7{bn}")
            prog.add("vector", lambda v, b3=b3, mu=mu, a3=a3: v.tensor_mul(b3, mu, a3),
                     reads=["cCF", "cA3"], writes=["cB3"], tag=f"C.c8{bn}")
            prog.add("vector", lambda v, b3=b3, b_sb=b_sb: v.tensor_sub(b3, b_sb[:, :], b3),
                     reads=["cB3", f"cG{bn}_bs"], writes=["cB3"], tag=f"C.c9{bn}")
            prog.add("vector", lambda v, b3=b3: v.tensor_scalar(b3, b3, 4.0, -4.0, OP.mult, OP.add),
                     reads=["cB3"], writes=["cB3"], tag=f"C.cA{bn}")
            prog.add("vector", lambda v, a3=a3: v.tensor_scalar(a3, a3, 4.0, None, OP.mult),
                     reads=["cA3"], writes=["cA3"], tag=f"C.cB{bn}")

        def sig_stream(bn, bi, hd, b, s, dstF, dres):
            src = hd.ap()[:, b * ctx.T + s * 512: b * ctx.T + (s + 1) * 512].rearrange(
                "(a p) n -> p a n", p=128)
            dstH = dstF[:, :].rearrange("p (a n) -> p a n", a=DP)
            prog.add("sync", lambda g: g.dma_start(out=dstH, in_=src),
                     reads=[R(f"c_h{bn}", (b, s, oc)) for oc in range(DP)],
                     writes=[dres], dma=True, tag=f"C.l{bn}{b}.{s}")
            for oc in range(DP):
                prog.add("scalar", lambda sc, oc=oc, bi=bi, dstF=dstF: sc.activation(
                    dstF[:, oc * 512:(oc + 1) * 512], dstF[:, oc * 512:(oc + 1) * 512],
                    AF.Sigmoid, bias=B3[:, bi * 8 + oc:bi * 8 + oc + 1],
                    scale=A3[:, bi * 8 + oc:bi * 8 + oc + 1]),
                    reads=[dres, "cA3", "cB3"], writes=[dres], tag=f"C.sg{bn}{b}.{s}.{oc}")

        for b in range(ctx.B):
            load_xb(ctx, b, f"C2.{b}")
            for s in range(NSL):
                sig_stream("v", 2, hv, b, s, VF, "cVF")
                for j in range(4):
                    for a in range(DP):
                        transpose_tile(
                            ctx, VTOK[:, (s * 4 + j) * D + a * 128: (s * 4 + j) * D + (a + 1) * 128],
                            VF[:, a * 512 + j * 128: a * 512 + (j + 1) * 128],
                            pTR, ctx.sb["cIdentB"][:, :],
                            reads=["cVF"], writes=[R("cVTOK", s * 4 + j)], pres="cpTR",
                            tag=f"C.vt{b}.{s * 4 + j}.{a}", slot=a % 2)
            for j in range(8):
                def emVV(t, j=j):
                    for hh in range(2):
                        h = 2 * j + hh
                        ro = hh * 64
                        for it in range(TPB):
                            last = t.matmul(pB0[ro:ro + 64, 0:64],
                                            VTOK[:, it * D + h * 64: it * D + h * 64 + 64],
                                            VTOK[:, it * D + h * 64: it * D + h * 64 + 64],
                                            start=(it == 0), stop=(it == TPB - 1))
                    return last
                prog.add("tensor", emVV, reads=[R("cVTOK", it) for it in range(TPB)],
                         writes=["cpB0"], tag=f"C.vv{b}.{j}")
                prog.add("vector", lambda v, j=j: v.tensor_scalar(
                    VVS[:, j * 64:(j + 1) * 64], pB0[:, 0:64], 0.125, None, OP.mult),
                    reads=["cpB0"], writes=["cVVS"], tag=f"C.ve{b}.{j}")
            for s in range(NSL):
                sig_stream("q", 0, hq, b, s, QBF, "cQBF")
                sig_stream("k", 1, hk, b, s, KBF, "cKBF")
                for a in range(DP):
                    prog.add("vector", lambda v, a=a: v.tensor_mul(
                        QKF[:, a * 512:(a + 1) * 512], QBF[:, a * 512:(a + 1) * 512],
                        KBF[:, a * 512:(a + 1) * 512]),
                        reads=["cQBF", "cKBF"], writes=["cQKF"], tag=f"C.qk{b}.{s}.{a}")
                for j in range(8):
                    def emO2(t, j=j):
                        for hh in range(2):
                            ro = hh * 64
                            last = t.matmul(pB1[ro:ro + 64, :],
                                            VVS[ro:ro + 64, j * 64:(j + 1) * 64],
                                            QKF[ro:ro + 64, j * 512:(j + 1) * 512],
                                            start=True, stop=True)
                        return last
                    prog.add("tensor", emO2, reads=["cVVS", "cQKF"], writes=["cpB1"],
                             tag=f"C.o{b}.{s}.{j}")
                    prog.add("scalar", lambda sc, j=j: sc.copy(OUTF[:, j * 512:(j + 1) * 512], pB1[:, :]),
                             reads=["cpB1"], writes=["cOUTF"], tag=f"C.oe{b}.{s}.{j}")
                for j in range(4):
                    i_loc = s * 4 + j
                    xi = b * TPB + i_loc
                    def emP(t, j=j):
                        for half in range(2):
                            for a in range(DP):
                                t.matmul(pO[:, half * 512:(half + 1) * 512],
                                         OUTF[:, a * 512 + j * 128: a * 512 + (j + 1) * 128],
                                         WC[:, 3 * 8192 + a * 1024 + half * 512: 3 * 8192 + a * 1024 + (half + 1) * 512],
                                         start=(a == 0), stop=False)
                            last = t.matmul(pO[:, half * 512:(half + 1) * 512],
                                            ctx.sb["cOnesF"][0:1, 0:128],
                                            BPR[0:1, half * 512:(half + 1) * 512],
                                            start=False, stop=True)
                        return last
                    prog.add("tensor", emP, reads=["cOUTF", "cWC3", "cBPR", "cOnesF"],
                             writes=["cpO"], tag=f"C.p{xi}")
                    prog.add("vector", lambda v, xi=xi, i_loc=i_loc: v.tensor_add(
                        ctx.sb["LNZ"][:, :], pO[:, :], XB[:, i_loc * D:(i_loc + 1) * D]),
                        reads=["cpO", R("XB", i_loc)], writes=["LNZ"], tag=f"C.z{xi}")
                    emit_ln_out(ctx, xi, is_last, f"C.ln{xi}")


    ctx.cur_es = None


def kernel(**inputs):
    cfg = dict(CFG)
    n_cores = cfg["n_cores"]
    T, Bpc = cfg["T"], cfg["B_per_core"]
    stage_groups = [cfg["stages"]] if cfg["fused"] else [(s,) for s in cfg["stages"]]

    x = np.asarray(inputs["L0_attn_x"], dtype=np.float32)
    B_total = x.shape[0]
    cur = inputs
    out_parts = None
    for gi, grp in enumerate(stage_groups):
        gcfg = dict(cfg)
        gcfg["stages"] = grp
        in_maps = stage_inputs(cur, grp, T, Bpc, n_cores)
        if gi > 0:
            for c in range(n_cores):
                in_maps[c]["x"] = out_parts[c]
        nc = build_program(gcfg)
        res = bass_utils.run_bass_kernel_spmd(nc, in_maps, core_ids=list(range(n_cores)))
        out_parts = [np.asarray(r["xout"]) for r in res.results]
    out = np.concatenate([p.reshape(Bpc, T, D) for p in out_parts], axis=0)
    return out.astype(np.float32)


# version 07 (final)
# speedup vs baseline: 1.1554x; 1.0464x over previous
import math
import contextlib
import numpy as np

import concourse.bass as bass
import concourse.mybir as mybir
import concourse.bass_utils as bass_utils

F32 = mybir.dt.float32
BF16 = mybir.dt.bfloat16
OP = mybir.AluOpType
AF = mybir.ActivationFunctionType
AX = mybir.AxisListType

D = 1024
DP = D // 128
NHEAD = 16
DH = 64
CHUNK = 128

CFG = {
    "stages": ("A", "B", "C", "D"),
    "fused": True,
    "n_cores": 8,
    "T": 2048,
    "B_per_core": 2,
    "collective": True,
    "gelu_lut": False,
}

EPS = 1e-5



ENGINES = ("tensor", "vector", "scalar", "gpsimd", "sync")


class Prog:

    def __init__(self):
        self.ops = []
        self.last_w = {}
        self.readers = {}

    def add(self, engine, emit, reads=(), writes=(), dma=False, tag=""):
        assert engine in ENGINES, engine
        idx = len(self.ops)
        deps = set()
        for r in reads:
            w = self.last_w.get(r)
            if w is not None:
                deps.add(w)
        for r in writes:
            w = self.last_w.get(r)
            if w is not None:
                deps.add(w)
            deps.update(self.readers.get(r, ()))
        deps.discard(idx)
        self.ops.append(dict(i=idx, engine=engine, emit=emit, dma=dma,
                             tag=tag, deps=deps, barrier=False))
        for r in reads:
            self.readers.setdefault(r, set()).add(idx)
        for r in writes:
            self.last_w[r] = idx
            self.readers[r] = {idx}
        return idx

    def barrier(self):
        self.ops.append(dict(i=len(self.ops), engine=None, emit=None, dma=False,
                             tag="BARRIER", deps=set(), barrier=True))

    NDMA_SLOTS = 6

    def finalize_into(self, nc, es):
        R_ = Prog.NDMA_SLOTS
        sems = {e: es.enter_context(nc.semaphore(f"s_{e}")) for e in ENGINES}
        dsems = {}
        for e in ENGINES:
            for sl in range(R_):
                dsems[(e, sl)] = es.enter_context(nc.semaphore(f"d_{e}{sl}"))
        dsems[("cc", 0)] = es.enter_context(nc.semaphore("d_cc"))

        pos = {}
        cnt = {e: 0 for e in ENGINES}
        dcnt = {e: 0 for e in ENGINES}
        duse = {(e, sl): 0 for e in ENGINES for sl in range(R_)}
        per_engine = {e: [] for e in ENGINES}
        pending_bar = None
        bar_seen = set()
        for o in self.ops:
            if o["engine"] is None:
                pending_bar = {"c": dict(cnt), "d": dict(duse)}
                bar_seen = set()
                continue
            e = o["engine"]
            if o["dma"] == "cc":
                duse.setdefault(("cc", 0), 0)
                duse[("cc", 0)] += 1
                o["slot"] = ("cc", 0)
                o["slot_use"] = duse[("cc", 0)]
                pos[o["i"]] = ("D", ("cc", 0), duse[("cc", 0)])
            elif o["dma"]:
                sl = dcnt[e] % R_
                duse[(e, sl)] += 1
                dcnt[e] += 1
                o["slot"] = (e, sl)
                o["slot_use"] = duse[(e, sl)]
                pos[o["i"]] = ("D", (e, sl), duse[(e, sl)] * 16)
            else:
                cnt[e] += 1
                pos[o["i"]] = ("C", e, cnt[e])
            o["bar"] = None
            if pending_bar is not None and e not in bar_seen:
                o["bar"] = pending_bar
                bar_seen.add(e)
                if bar_seen == set(ENGINES):
                    pending_bar = None
            per_engine[e].append(o)

        for e in ENGINES:
            hw_c = {}
            hw_d = {}
            for o in per_engine[e]:
                waits = {}
                if o.get("bar"):
                    for e2, c in o["bar"]["c"].items():
                        if c > 0:
                            waits[("C", e2)] = max(waits.get(("C", e2), 0), c)
                    for key, u in o["bar"]["d"].items():
                        if u > 0:
                            step = 1 if key[0] == "cc" else 16
                            waits[("D", key)] = max(waits.get(("D", key), 0), u * step)
                for dep in o["deps"]:
                    kind, who, dv = pos[dep]
                    if kind == "C" and who == e == "tensor" and not o["dma"]:
                        continue
                    waits[(kind, who)] = max(waits.get((kind, who), 0), dv)
                if o["dma"]:
                    key = ("D", o["slot"])
                    step = 1 if o["dma"] == "cc" else 16
                    need = (o["slot_use"] - 1) * step
                    if need > 0:
                        waits[key] = max(waits.get(key, 0), need)
                o["w_list"] = []
                for (kind, who), v in sorted(waits.items(), key=str):
                    hw = hw_d if kind == "D" else hw_c
                    if v > hw.get(who, 0):
                        hw[who] = v
                        o["w_list"].append((kind, who, v))

        def section(eng_obj, e):
            for o in per_engine[e]:
                for (kind, who, v) in o["w_list"]:
                    eng_obj.wait_ge(dsems[who] if kind == "D" else sems[who], v)
                inst = o["emit"](eng_obj)
                if inst is None:
                    raise RuntimeError(f"op {o['i']} ({o['tag']}) returned None")
                if o["dma"] == "cc":
                    inst.then_inc(dsems[o["slot"]])
                elif o["dma"]:
                    inst.then_inc(dsems[o["slot"]], 16)
                else:
                    inst.then_inc(sems[e], 1)

        with nc.Block() as block:
            block.tensor(lambda t: section(t, "tensor"))
            block.vector(lambda v: section(v, "vector"))
            block.scalar(lambda s: section(s, "scalar"))
            block.gpsimd(lambda g: section(g, "gpsimd"))
            block.sync(lambda sy: section(sy, "sync"))
        return cnt, dcnt


def R(name, i=None):
    return name if i is None else f"{name}:{i}"



def decay_gammas():
    return 1.0 - np.exp(np.linspace(math.log(1 / 32), math.log(1 / 512), NHEAD))


def host_constants():
    gam = decay_gammas()
    a = np.arange(CHUNK, dtype=np.float64)[:, None]
    Gq = np.concatenate([(g ** a) * np.ones((1, DH)) for g in gam], axis=1) / math.sqrt(DH)
    Gk = np.concatenate([(g ** (-a)) * np.ones((1, DH)) for g in gam], axis=1)
    causal = (np.arange(CHUNK)[:, None] <= np.arange(CHUNK)[None, :]).astype(np.float32)
    ltri = (np.arange(128)[:, None] <= np.arange(128)[None, :]).astype(np.float32).T
    ltri = (np.arange(128)[:, None][:, :] <= np.arange(128)[None, :]).astype(np.float32)
    return {
        "cGq": Gq.astype(np.float32),
        "cGk": Gk.astype(np.float32),
        "cCausal": causal,
        "cIdent": np.eye(128, dtype=np.float32),
        "cLtri": ltri.astype(np.float32),
        "cOnes": np.ones((128, 128), dtype=np.float32),
        "cGpc": np.concatenate([
            np.concatenate([np.full((64, 1), g2 ** CHUNK), np.full((64, 1), g21 ** CHUNK)])
            for g2, g21 in zip(gam[0::2], gam[1::2])], axis=1).astype(np.float32),
    }


def fm(vec):
    v = np.asarray(vec, dtype=np.float32)
    k = v.shape[0] // 128
    return np.ascontiguousarray(v.reshape(k, 128).T)


def rowv(vec):
    return np.ascontiguousarray(np.asarray(vec, dtype=np.float32)[None, :])


def tr(w):
    return np.ascontiguousarray(np.asarray(w, dtype=np.float32).T)


def stage_inputs(inputs, stages, T, B_per_core, n_cores):
    x = np.ascontiguousarray(np.asarray(inputs["L0_attn_x"], dtype=np.float32))
    B_total, T_in, D_in = x.shape
    assert D_in == D
    shared = dict(host_constants())
    if "A" in stages:
        shared.update({
            "aQwT": tr(inputs["L0_attn_qw"]), "aKwT": tr(inputs["L0_attn_kw"]),
            "aVwT": tr(inputs["L0_attn_vw"]), "aOwT": tr(inputs["L0_attn_ow"]),
            "aW1T": tr(inputs["L0_attn_w1"]), "aW2T": tr(inputs["L0_attn_w2"]),
            "aGn1g": fm(inputs["L0_attn_norm_g"]), "aGn1b": fm(inputs["L0_attn_norm_b"]),
            "aGn2g": fm(inputs["L0_attn_ffn_norm_g"]), "aGn2b": fm(inputs["L0_attn_ffn_norm_b"]),
            "aObR": rowv(inputs["L0_attn_ob"]), "aB2R": rowv(inputs["L0_attn_b2"]),
            "aB1F": fm(inputs["L0_attn_b1"]),
        })
    if "B" in stages:
        shared.update({
            "bGwT": tr(inputs["L0_ffn_gate_w"]),
            "bWgT": np.ascontiguousarray(np.asarray(inputs["L0_ffn_w_gate"], np.float32).transpose(0, 2, 1)),
            "bWuT": np.ascontiguousarray(np.asarray(inputs["L0_ffn_w_up"], np.float32).transpose(0, 2, 1)),
            "bWdT": np.ascontiguousarray(np.asarray(inputs["L0_ffn_w_down"], np.float32).transpose(0, 2, 1)),
        })
    if "C" in stages:
        shared.update({
            "cWqT": tr(inputs["L1_attn_Wq"]), "cWkT": tr(inputs["L1_attn_Wk"]),
            "cWvT": tr(inputs["L1_attn_Wv"]), "cWpT": tr(inputs["L1_attn_Wp"]),
            "cGq_g": fm(inputs["L1_attn_gq"]), "cGq_b": fm(inputs["L1_attn_bq"]),
            "cGk_g": fm(inputs["L1_attn_gk"]), "cGk_b": fm(inputs["L1_attn_bk"]),
            "cGv_g": fm(inputs["L1_attn_gv"]), "cGv_b": fm(inputs["L1_attn_bv"]),
            "cBpR": rowv(inputs["L1_attn_bp"]),
        })
    if "D" in stages:
        shared.update({
            "dU1": np.ascontiguousarray(np.asarray(inputs["L1_ffn_U1"], np.float32)),
            "dV1": np.ascontiguousarray(np.asarray(inputs["L1_ffn_V1"], np.float32)),
            "dU2": np.ascontiguousarray(np.asarray(inputs["L1_ffn_U2"], np.float32)),
            "dV2": np.ascontiguousarray(np.asarray(inputs["L1_ffn_V2"], np.float32)),
            "dB1F": fm(inputs["L1_ffn_b1"]), "dB2R": rowv(inputs["L1_ffn_b2"]),
        })
    in_maps = []
    for c in range(n_cores):
        m = dict(shared)
        xs = x[c * B_per_core:(c + 1) * B_per_core, :T, :]
        m["x"] = np.ascontiguousarray(xs.reshape(B_per_core * T, D))
        in_maps.append(m)
    return in_maps



class Ctx:
    def __init__(self, nc, prog, es, cfg):
        self.nc = nc
        self.prog = prog
        self.es = es
        self.cfg = cfg
        self.T = cfg["T"]
        self.B = cfg["B_per_core"]
        self.NT = self.B * self.T // 128
        self.TPB = self.T // 128
        self.sb = {}
        self.dram = {}

    def inp(self, name, shape):
        t = self.nc.dram_tensor(name, list(shape), F32, kind="ExternalInput")
        self.dram[name] = t
        return t

    def sbuf(self, name, shape, dtype):
        es = getattr(self, "cur_es", None) or self.es
        t = es.enter_context(self.nc.sbuf_tensor(name, list(shape), dtype))
        self.sb[name] = t
        return t

    def psum(self, name, shape, dtype=F32):
        es = getattr(self, "cur_es", None) or self.es
        t = es.enter_context(self.nc.psum_tensor(name, list(shape), dtype))
        self.sb[name] = t
        return t

    def load_weight_fm(self, dst_sb, src_dram, n_ptiles, cols, res, srcsl=None, tag=""):
        src = src_dram.ap() if srcsl is None else srcsl
        src = src.rearrange("(a p) n -> p a n", p=128)
        dst = dst_sb[:, 0:n_ptiles * cols].rearrange("p (a n) -> p a n", a=n_ptiles)
        self.prog.add("gpsimd", lambda g: g.dma_start(out=dst, in_=src),
                      reads=(), writes=[res], dma=True, tag=tag or f"load:{res}")

    def dma_in(self, engine, dst_ap, src_ap, writes, tag):
        self.prog.add(engine, lambda g: g.dma_start(out=dst_ap, in_=src_ap),
                      reads=(), writes=writes, dma=True, tag=tag)

    def dma_out(self, engine, dst_ap, src_ap, reads, writes, tag):
        self.prog.add(engine, lambda g: g.dma_start(out=dst_ap, in_=src_ap),
                      reads=reads, writes=writes, dma=True, tag=tag)


def transpose_tile(ctx, out_sb_ap, in_sb_ap, ptr_aps, ident_ap, reads, writes, pres, tag, slot=0):
    prog = ctx.prog
    ph_ = ptr_aps[slot][:, 0:128]
    prog.add("tensor", lambda t: t.transpose(ph_, in_sb_ap, ident_ap),
             reads=list(reads) + ["cIdentB"], writes=[R(pres, slot)], tag=tag + ".tr")
    prog.add("vector", lambda v: v.tensor_copy(out_sb_ap, ph_),
             reads=[R(pres, slot)], writes=list(writes), tag=tag + ".ev")



def ln_from_z(ctx, xi, out_ap, out_res, tag):
    prog = ctx.prog
    z = ctx.sb["LNZ"][:, :]
    st = ctx.sb["LNST"]
    mu, ez2, rstd, ssq, nvar = st[:, 0:1], st[:, 1:2], st[:, 2:3], st[:, 3:4], st[:, 4:5]
    prog.add("vector", lambda v: v.reduce_sum(mu, z, axis=AX.X),
             reads=["LNZ"], writes=["LNST"], tag=tag + ".mu")
    prog.add("scalar", lambda s: s.activation(ctx.sb["OUTT"][:, :], z, AF.Square, accum_out=ssq),
             reads=["LNZ", "LNST"], writes=["OUTT", "LNST"], tag=tag + ".sq")
    prog.add("vector", lambda v: v.tensor_scalar(mu, mu, 1.0 / D, None, OP.mult),
             reads=["LNST"], writes=["LNST"], tag=tag + ".m2")
    prog.add("vector", lambda v: v.tensor_scalar(ez2, ssq, 1.0 / D, None, OP.mult),
             reads=["LNST"], writes=["LNST"], tag=tag + ".v1")
    prog.add("vector", lambda v: v.scalar_tensor_tensor(nvar, mu, mu[:, 0:1], ez2, OP.mult, OP.subtract),
             reads=["LNST"], writes=["LNST"], tag=tag + ".v2")
    prog.add("scalar", lambda s: s.activation(rstd, nvar, AF.Sqrt, bias=ctx.sb["CONSTV"][:, 0:1], scale=-1.0),
             reads=["LNST", "CONSTV"], writes=["LNST"], tag=tag + ".rt")
    prog.add("vector", lambda v: v.reciprocal(rstd, rstd),
             reads=["LNST"], writes=["LNST"], tag=tag + ".rc")
    prog.add("vector", lambda v: v.tensor_scalar(out_ap, z, mu, rstd, OP.subtract, OP.mult),
             reads=["LNZ", "LNST"], writes=[out_res], tag=tag + ".o")



def build_stage_D(ctx, is_last):
    nc, prog, cfg = ctx.nc, ctx.prog, ctx.cfg
    es = contextlib.ExitStack()
    ctx.cur_es = es
    with es:
        U1 = ctx.sbuf("dU1s", [128, DP * 256], BF16)
        V1 = ctx.sbuf("dV1s", [128, 2 * 4096], BF16)
        U2 = ctx.sbuf("dU2s", [128, 32 * 256], BF16)
        V2 = ctx.sbuf("dV2s", [128, 2 * 1024], BF16)
        B1F = ctx.sbuf("dB1Fs", [128, 32], F32)
        B2R = ctx.sbuf("dB2Rs", [1, 1024], F32)
        XF = ctx.sbuf("dXF", [128, DP * 512], BF16)
        PT = ctx.sbuf("dPT", [128, 2 * 512], BF16)
        HT = ctx.sbuf("dHT", [128, 32 * 512], BF16)
        QT = ctx.sbuf("dQT", [128, 2 * 512], BF16)
        GS = ctx.sbuf("dGS", [128, 2 * 512], F32)
        T2 = ctx.sbuf("dT2", [128, 2 * 512], F32)
        pTR = [ctx.psum("dpTR0", [128, 128], BF16), ctx.psum("dpTR1", [128, 128], BF16)]
        pP = [ctx.psum(f"dpP{i}", [128, 512]) for i in range(2)]
        pH = [ctx.psum(f"dpH{i}", [128, 512]) for i in range(2)]
        pO = ctx.psum("dpO", [128, 1024])

        ctx.load_weight_fm(U1, ctx.dram["dU1"], DP, 256, "dU1s")
        ctx.load_weight_fm(V1, ctx.dram["dV1"], 2, 4096, "dV1s")
        ctx.load_weight_fm(U2, ctx.dram["dU2"], 32, 256, "dU2s")
        ctx.load_weight_fm(V2, ctx.dram["dV2"], 2, 1024, "dV2s")
        ctx.dma_in("sync", B1F[:, :], ctx.dram["dB1F"].ap(), ["dB1Fs"], "dB1F")
        ctx.dma_in("sync", B2R[:, :], ctx.dram["dB2R"].ap(), ["dB2Rs"], "dB2R")

        XB = ctx.sb["XB"]
        NS = ctx.TPB // 4
        for b in range(ctx.B):
          load_xb(ctx, b, f"D.{b}")
          for s in range(NS):
            for j in range(4):
                i_loc = s * 4 + j
                xi = b * ctx.TPB + i_loc
                for a in range(DP):
                    transpose_tile(
                        ctx, XF[:, a * 512 + j * 128: a * 512 + (j + 1) * 128],
                        XB[:, i_loc * D + a * 128: i_loc * D + (a + 1) * 128],
                        pTR, ctx.sb["cIdentB"][:, :],
                        reads=[R("XB", i_loc)], writes=[R("dXF", a)], pres="dpTR",
                        tag=f"D.xf{xi}.{a}", slot=a % 2)
            for rc in range(2):
                def em(t, rc=rc):
                    for a in range(DP):
                        last = t.matmul(pP[rc][:, :],
                                        U1[:, a * 256 + rc * 128: a * 256 + (rc + 1) * 128],
                                        XF[:, a * 512:(a + 1) * 512],
                                        start=(a == 0), stop=(a == DP - 1))
                    return last
                prog.add("tensor", em,
                         reads=["dU1s"] + [R("dXF", a) for a in range(DP)],
                         writes=[f"dpP{rc}"], tag=f"D.P{s}.{rc}")
                prog.add("scalar", lambda sc, rc=rc: sc.copy(PT[:, rc * 512:(rc + 1) * 512], pP[rc][:, :]),
                         reads=[f"dpP{rc}"], writes=[R("dPT", rc)], tag=f"D.Pe{s}.{rc}")
            for f in range(32):
                pb = pH[f % 2]
                def em(t, f=f, pb=pb):
                    for a in range(2):
                        last = t.matmul(pb[:, :],
                                        V1[:, a * 4096 + f * 128:a * 4096 + (f + 1) * 128],
                                        PT[:, a * 512:(a + 1) * 512],
                                        start=(a == 0), stop=(a == 1))
                    return last
                prog.add("tensor", em, reads=["dV1s", R("dPT", 0), R("dPT", 1)],
                         writes=[f"dpH{f % 2}"], tag=f"D.H{s}.{f}")
                hslice = HT[:, f * 512:(f + 1) * 512]
                bcol = B1F[:, f:f + 1]
                if cfg["gelu_lut"]:
                    prog.add("scalar", lambda sc, pb=pb, hs=hslice, bc=bcol:
                             sc.activation(hs, pb[:, :], AF.Gelu_apprx_tanh, bias=bc, scale=1.0),
                             reads=[f"dpH{f % 2}", "dB1Fs"], writes=[R("dHT", f)],
                             tag=f"D.g{s}.{f}")
                else:
                    par = f % 2
                    gsh = GS[:, par * 512:(par + 1) * 512]
                    t2 = T2[:, par * 512:(par + 1) * 512]
                    prog.add("scalar", lambda sc, pb=pb, bc=bcol, gsh=gsh:
                             sc.activation(gsh, pb[:, :], AF.Identity, bias=bc, scale=1.0),
                             reads=[f"dpH{f % 2}", "dB1Fs"], writes=[R("dGS", par)], tag=f"D.g0.{s}.{f}")
                    prog.add("scalar", lambda sc, pb=pb, gsh=gsh:
                             sc.activation(pb[:, :], gsh, AF.Square),
                             reads=[R("dGS", par)], writes=[f"dpH{f % 2}"], tag=f"D.g1.{s}.{f}")
                    prog.add("vector", lambda v, pb=pb, t2=t2, gsh=gsh:
                             v.scalar_tensor_tensor(t2, pb[:, :], 22.36528637357561, gsh, OP.add, OP.mult),
                             reads=[f"dpH{f % 2}", R("dGS", par)], writes=[R("dT2", par)], tag=f"D.g2.{s}.{f}")
                    prog.add("scalar", lambda sc, t2=t2:
                             sc.activation(t2, t2, AF.Sigmoid, scale=0.07135481627159174),
                             reads=[R("dT2", par)], writes=[R("dT2", par)], tag=f"D.g4.{s}.{f}")
                    prog.add("vector", lambda v, hs=hslice, t2=t2, gsh=gsh:
                             v.tensor_mul(hs, t2, gsh),
                             reads=[R("dT2", par), R("dGS", par)], writes=[R("dHT", f)], tag=f"D.g5.{s}.{f}")
            for rc in range(2):
                def em(t, rc=rc):
                    for a in range(32):
                        last = t.matmul(pP[rc][:, :],
                                        U2[:, a * 256 + rc * 128:a * 256 + (rc + 1) * 128],
                                        HT[:, a * 512:(a + 1) * 512],
                                        start=(a == 0), stop=(a == 31))
                    return last
                prog.add("tensor", em,
                         reads=["dU2s"] + [R("dHT", a) for a in range(32)],
                         writes=[f"dpP{rc}"], tag=f"D.Q{s}.{rc}")
                prog.add("scalar", lambda sc, rc=rc: sc.copy(QT[:, rc * 512:(rc + 1) * 512], pP[rc][:, :]),
                         reads=[f"dpP{rc}"], writes=[R("dQT", rc)], tag=f"D.Qe{s}.{rc}")
            for j in range(4):
                i_loc = s * 4 + j
                xi = b * ctx.TPB + i_loc
                def em(t, j=j):
                    for half in range(2):
                        for rc in range(2):
                            t.matmul(pO[:, half * 512:(half + 1) * 512],
                                     QT[:, rc * 512 + j * 128: rc * 512 + (j + 1) * 128],
                                     V2[:, rc * 1024 + half * 512: rc * 1024 + (half + 1) * 512],
                                     start=(rc == 0), stop=False)
                        last = t.matmul(pO[:, half * 512:(half + 1) * 512],
                                        ctx.sb["cOnesF"][0:1, 0:128],
                                        B2R[0:1, half * 512:(half + 1) * 512],
                                        start=False, stop=True)
                    return last
                prog.add("tensor", em,
                         reads=[R("dQT", 0), R("dQT", 1), "dV2s", "dB2Rs", "cOnesF"],
                         writes=["dpO"], tag=f"D.o{xi}")
                prog.add("vector", lambda v, xi=xi, i_loc=i_loc: v.tensor_add(
                    ctx.sb["LNZ"][:, :], pO[:, :], XB[:, i_loc * D:(i_loc + 1) * D]),
                    reads=["dpO", R("XB", i_loc)], writes=["LNZ"], tag=f"D.z{xi}")
                emit_ln_out(ctx, xi, is_last, f"D.ln{xi}")


    ctx.cur_es = None

def emit_ln_out(ctx, xi, is_last, tag):
    i = xi % ctx.TPB
    if is_last:
        o = ctx.sb["OUTT"][:, :]
        ln_from_z(ctx, xi, o, "OUTT", tag)
        dst = ctx.dram["xout"].ap()[xi * 128:(xi + 1) * 128, :]
        ctx.dma_out("sync", dst, o, reads=["OUTT"], writes=[R("xout", xi)], tag=tag + ".st")
    else:
        XB = ctx.sb["XB"]
        ln_from_z(ctx, xi, XB[:, i * D:(i + 1) * D], R("XB", i), tag)
        dst = ctx.dram["xres"].ap()[xi * 128:(xi + 1) * 128, :]
        ctx.dma_out("sync", dst, XB[:, i * D:(i + 1) * D], reads=[R("XB", i)],
                    writes=[R("xres", xi)], tag=tag + ".st")
        ctx.xsrc[xi] = "xres"


def load_xb(ctx, b, tag):
    XB = ctx.sb["XB"]
    for i in range(ctx.TPB):
        xi = b * ctx.TPB + i
        if ctx.xsrc[xi] == "x":
            srcap = ctx.dram["x"].ap()[xi * 128:(xi + 1) * 128, :]
            rr = []
        else:
            srcap = ctx.dram["xres"].ap()[xi * 128:(xi + 1) * 128, :]
            rr = [R("xres", xi)]
        ctx.prog.add("gpsimd", lambda g, d=XB[:, i * D:(i + 1) * D], s=srcap: g.dma_start(out=d, in_=s),
                     reads=rr, writes=[R("XB", i)], dma=True, tag=f"{tag}.lx{i}")



def build_program(cfg):
    nc = bass.Bass("TRN2", target_bir_lowering=False)
    es = contextlib.ExitStack()
    prog = Prog()
    ctx = Ctx(nc, prog, es, cfg)
    stages = cfg["stages"]
    NTOK = cfg["B_per_core"] * cfg["T"]

    with es:
        xin = ctx.inp("x", [NTOK, D])
        ctx.dram["xout"] = nc.dram_tensor("xout", [NTOK, D], F32, kind="ExternalOutput")
        for nm, shape in [("cGq", [128, D]), ("cGk", [128, D]), ("cCausal", [128, 128]),
                          ("cIdent", [128, 128]), ("cLtri", [128, 128]), ("cOnes", [128, 128]),
                          ("cGpc", [128, 8])]:
            ctx.inp(nm, shape)
        wshapes = {
            "A": [("aQwT", [D, D]), ("aKwT", [D, D]), ("aVwT", [D, D]), ("aOwT", [D, D]),
                  ("aW1T", [D, 2 * D]), ("aW2T", [2 * D, D]),
                  ("aGn1g", [128, DP]), ("aGn1b", [128, DP]), ("aGn2g", [128, DP]), ("aGn2b", [128, DP]),
                  ("aObR", [1, D]), ("aB2R", [1, D]), ("aB1F", [128, 16])],
            "B": [("bGwT", [D, 8]), ("bWgT", [8, D, 4096]), ("bWuT", [8, D, 4096]),
                  ("bWdT", [8, 4096, D])],
            "C": [("cWqT", [D, D]), ("cWkT", [D, D]), ("cWvT", [D, D]), ("cWpT", [D, D]),
                  ("cGq_g", [128, DP]), ("cGq_b", [128, DP]), ("cGk_g", [128, DP]),
                  ("cGk_b", [128, DP]), ("cGv_g", [128, DP]), ("cGv_b", [128, DP]),
                  ("cBpR", [1, D])],
            "D": [("dU1", [D, 256]), ("dV1", [256, 4096]), ("dU2", [4096, 256]),
                  ("dV2", [256, D]), ("dB1F", [128, 32]), ("dB2R", [1, D])],
        }
        for st in stages:
            for nm, shape in wshapes[st]:
                ctx.inp(nm, shape)

        XB = ctx.sbuf("XB", [128, ctx.TPB * D], BF16)
        ctx.dram["xres"] = nc.dram_tensor("xres", [NTOK, D], BF16)
        ctx.xsrc = ["x"] * ctx.NT
        ctx.sbuf("LNZ", [128, D], F32)
        ctx.sbuf("LNST", [128, 8], F32)
        ctx.sbuf("OUTT", [128, D], F32)
        identb = ctx.sbuf("cIdentB", [128, 128], BF16)
        constv = ctx.sbuf("CONSTV", [128, 4], F32)
        prog.add("vector", lambda v: v.memset(constv[:, 0:1], float(EPS)), writes=["CONSTV"], tag="c_eps")
        onesf = ctx.sbuf("cOnesF", [128, 128], F32)
        onescb = ctx.sbuf("cOnesCB", [128, 1], BF16)

        ctx.dma_in("gpsimd", identb[:, :], ctx.dram["cIdent"].ap(), ["cIdentB"], "ldid")
        ctx.dma_in("sync", onesf[:, :], ctx.dram["cOnes"].ap(), ["cOnesF"], "ldon")
        prog.add("vector", lambda v: v.memset(onescb[:, :], 1.0), writes=["cOnesCB"], tag="onescb")

        for si, st in enumerate(stages):
            is_last = (si == len(stages) - 1)
            if st == "A":
                build_stage_A(ctx, is_last)
            elif st == "B":
                build_stage_B(ctx, is_last)
            elif st == "C":
                build_stage_C(ctx, is_last)
            elif st == "D":
                build_stage_D(ctx, is_last)
            prog.barrier()

        prog.add("sync", lambda sy: sy.nop(), reads=[R("xout", i) for i in range(ctx.NT)],
                 tag="final")
        cnt, dcnt = prog.finalize_into(nc, es)
    return nc


def groupnorm_coefs(ctx, src_name, nt0, g_sb, b_sb, af_ap, bf_ap, af_res, bf_res,
                    psU, psQ, psG, psU_res, psQ_res, sus, sqs, grs, gsm, sqscr, onesc_bf, tag):
    prog, T = ctx.prog, ctx.T
    src = ctx.sb[src_name]
    ntiles = ctx.TPB
    for i in range(ntiles):
        xi = nt0 + i
        def emS(t, i=i, xi=xi):
            for half in range(2):
                last = t.matmul(psU[0:1, half * 512:(half + 1) * 512], onesc_bf[:, 0:1],
                                src[:, xi * D + half * 512: xi * D + (half + 1) * 512],
                                start=(i == 0), stop=(i == ntiles - 1))
            return last
        prog.add("tensor", emS, reads=[R(src_name, xi), "cOnesCB"], writes=[psU_res],
                 tag=f"{tag}.s{i}")
        prog.add("scalar", lambda s, xi=xi: s.activation(
            sqscr[:, :], src[:, xi * D:(xi + 1) * D], AF.Square),
            reads=[R(src_name, xi)], writes=["GSQs"], tag=f"{tag}.q{i}")
        def emQ(t, i=i):
            for half in range(2):
                last = t.matmul(psQ[0:1, half * 512:(half + 1) * 512], ctx.sb["cOnesF"][:, 0:1],
                                sqscr[:, half * 512:(half + 1) * 512],
                                start=(i == 0), stop=(i == ntiles - 1))
            return last
        prog.add("tensor", emQ, reads=["GSQs", "cOnesF"], writes=[psQ_res],
                 tag=f"{tag}.qs{i}")
    grrow = ctx.sb["GRROW"]
    prog.add("vector", lambda v: v.tensor_copy(grrow[0:1, :], psU[0:1, :]),
             reads=[psU_res], writes=["GRROW"], tag=f"{tag}.cU")
    prog.add("vector", lambda v: v.reduce_sum(
        grs[0:1, 0:16], grrow[0:1, :].rearrange("p (a hi c) -> p hi a c", hi=2, c=64),
        axis=AX.X), reads=["GRROW"], writes=["GRS"], tag=f"{tag}.rU")
    prog.add("vector", lambda v: v.tensor_copy(grrow[0:1, :], psQ[0:1, :]),
             reads=[psQ_res], writes=["GRROW"], tag=f"{tag}.cQ")
    prog.add("vector", lambda v: v.reduce_sum(
        grs[0:1, 16:32], grrow[0:1, :].rearrange("p (a hi c) -> p hi a c", hi=2, c=64),
        axis=AX.X), reads=["GRROW"], writes=["GRS"], tag=f"{tag}.rQ")
    ONES = ctx.sb["cOnesF"]
    kk = 1.0 / (64.0 * T)
    prog.add("vector", lambda v: v.tensor_scalar(gsm[0:1, 0:16], grs[0:1, 0:16], kk, None, OP.mult),
             reads=["GRS"], writes=["GSM"], tag=f"{tag}.k1")
    prog.add("vector", lambda v: v.tensor_scalar(gsm[0:1, 16:32], grs[0:1, 16:32], kk, None, OP.mult),
             reads=["GRS"], writes=["GSM"], tag=f"{tag}.k2")
    prog.add("vector", lambda v: v.tensor_mul(gsm[0:1, 32:48], gsm[0:1, 0:16], gsm[0:1, 0:16]),
             reads=["GSM"], writes=["GSM"], tag=f"{tag}.m2")
    prog.add("vector", lambda v: v.tensor_sub(gsm[0:1, 48:64], gsm[0:1, 16:32], gsm[0:1, 32:48]),
             reads=["GSM"], writes=["GSM"], tag=f"{tag}.var")
    prog.add("scalar", lambda s: s.activation(gsm[0:1, 32:48], gsm[0:1, 48:64], AF.Sqrt,
                                              bias=ctx.sb["CONSTV"][0:1, 0:1]),
             reads=["GSM", "CONSTV"], writes=["GSM"], tag=f"{tag}.sq")
    prog.add("vector", lambda v: v.reciprocal(gsm[0:1, 32:48], gsm[0:1, 32:48]),
             reads=["GSM"], writes=["GSM"], tag=f"{tag}.rc")
    def emB(t):
        t.matmul(psU[0:64, 0:8], ONES[0:1, 0:64], gsm[0:1, 32:40], start=True, stop=True)
        t.matmul(psU[64:128, 0:8], ONES[0:1, 0:64], gsm[0:1, 40:48], start=True, stop=True)
        t.matmul(psQ[0:64, 0:8], ONES[0:1, 0:64], gsm[0:1, 0:8], start=True, stop=True)
        return t.matmul(psQ[64:128, 0:8], ONES[0:1, 0:64], gsm[0:1, 8:16], start=True, stop=True)
    prog.add("tensor", emB, reads=["GSM", "cOnesF"], writes=[psU_res, psQ_res], tag=f"{tag}.bc")
    prog.add("vector", lambda v: v.tensor_mul(af_ap, psU[:, 0:8], g_sb[:, :]),
             reads=[psU_res, f"{tag}.gsb"], writes=[af_res], tag=f"{tag}.af")
    prog.add("vector", lambda v: v.tensor_mul(ctx.sb["GTMP"][:, 0:8], psQ[:, 0:8], af_ap),
             reads=[psQ_res, af_res], writes=["GTMP"], tag=f"{tag}.mb")
    prog.add("vector", lambda v: v.tensor_sub(bf_ap, b_sb[:, :], ctx.sb["GTMP"][:, 0:8]),
             reads=["GTMP", f"{tag}.bsb"], writes=[bf_res], tag=f"{tag}.bf")


def norm_transpose_tile(ctx, src_name, xi, dst_ap_fn, dst_res, af_ap, af_res, bf_ap, bf_res,
                        ptrs, ptr_res, tag):
    prog = ctx.prog
    src = ctx.sb[src_name]
    for a in range(DP):
        pa = a % 2
        ph_ = ptrs[pa][:, 0:128]
        prog.add("tensor", lambda t, a=a, xi=xi, ph_=ph_: t.transpose(
            ph_, src[:, xi * D + a * 128: xi * D + (a + 1) * 128], ctx.sb["cIdentB"][:, :]),
            reads=[R(src_name, xi), "cIdentB"], writes=[R(ptr_res, pa)], tag=f"{tag}.t{a}")
        prog.add("vector", lambda v, a=a, ph_=ph_: v.tensor_scalar(
            dst_ap_fn(a), ph_, af_ap[:, a:a + 1], bf_ap[:, a:a + 1], OP.mult, OP.add),
            reads=[R(ptr_res, pa), af_res, bf_res], writes=[dst_res], tag=f"{tag}.n{a}")


def build_stage_A(ctx, is_last):
    nc, prog, cfg = ctx.nc, ctx.prog, ctx.cfg
    es = contextlib.ExitStack()
    TPB = ctx.TPB
    ctx.cur_es = es
    with es:
        WB = ctx.sbuf("aWB", [128, 4 * 8192], BF16)
        XMB = ctx.sbuf("aXMB", [128, TPB * D], BF16)
        GQ = ctx.sbuf("aGQ", [128, D], BF16)
        GK = ctx.sbuf("aGK", [128, D], BF16)
        GPC = ctx.sbuf("aGPC", [128, 8], F32)
        CAUS = ctx.sbuf("aCAUS", [128, 128], F32)
        G1 = ctx.sbuf("aG1", [128, DP], F32); B1 = ctx.sbuf("aB1", [128, DP], F32)
        G2 = ctx.sbuf("aG2", [128, DP], F32); B2 = ctx.sbuf("aB2", [128, DP], F32)
        AF1 = ctx.sbuf("aAF1", [128, DP], F32); BF1 = ctx.sbuf("aBF1", [128, DP], F32)
        AF2 = ctx.sbuf("aAF2", [128, DP], F32); BF2 = ctx.sbuf("aBF2", [128, DP], F32)
        GTMP = ctx.sbuf("GTMP", [128, 8], F32)
        OBR = ctx.sbuf("aOBR", [1, D], F32)
        B2R = ctx.sbuf("aB2R2", [1, D], F32)
        B1F = ctx.sbuf("aB1F2", [128, 16], F32)
        XNF = ctx.sbuf("aXNF", [128, D], BF16)
        QTK = ctx.sbuf("aQTK", [128, D], BF16)
        KTK = ctx.sbuf("aKTK", [128, D], BF16)
        VTK = ctx.sbuf("aVTK", [128, D], BF16)
        QF = ctx.sbuf("aQF", [128, D], BF16)
        KF = ctx.sbuf("aKF", [128, D], BF16)
        YF = ctx.sbuf("aYF", [128, D], BF16)
        SIMB = [ctx.sbuf(f"aSIMB{i}", [128, 128], BF16) for i in range(2)]
        SST = ctx.sbuf("aSST", [128, 8 * DH], BF16)
        RL = ctx.sbuf("aRL", [128, 8], F32)
        GSQs = ctx.sbuf("GSQs", [128, D], F32)
        GSUS = ctx.sbuf("GSUS", [128, 8], F32)
        GSQS = ctx.sbuf("GSQS", [128, 8], F32)
        GRS = ctx.sbuf("GRS", [1, 64], F32)
        ctx.sbuf("GRROW", [1, 1024], F32)
        GSM = ctx.sbuf("GSM", [1, 64], F32)
        XN2F = ctx.sbuf("aXN2F", [128, DP * 512], BF16)
        H1F = ctx.sbuf("aH1F", [128, 16 * 512], BF16)
        pTR = ctx.psum("apTR", [128, 128], BF16)
        pTR2 = ctx.psum("apTR2", [128, 128], BF16)
        pTRS = [pTR, pTR2]
        pQ = ctx.psum("apQ", [128, D])
        pK = ctx.psum("apK", [128, D])
        pV = ctx.psum("apV", [128, D])
        pSIM = [pQ[:, 0:128], pK[:, 0:128]]
        pYH = pV[:, 0:128]
        pM = pV[:, 512:576]
        pG = None
        pH1 = [pQ[:, 0:512], pK[:, 0:512]]

        def load_qkvo():
            for r, nm in enumerate(["aQwT", "aKwT", "aVwT", "aOwT"]):
                src = ctx.dram[nm].ap().rearrange("(a p) n -> p a n", p=128)
                dst = WB[:, r * 8192:(r + 1) * 8192].rearrange("p (a n) -> p a n", a=DP)
                prog.add("gpsimd", lambda g, dst=dst, src=src: g.dma_start(out=dst, in_=src),
                         writes=[f"WBr{r}"], dma=True, tag=f"A.ld{nm}")

        def load_w12():
            src = ctx.dram["aW1T"].ap().rearrange("(a p) n -> p a n", p=128)
            dst = WB[:, 0:16384].rearrange("p (a n) -> p a n", a=DP)
            prog.add("gpsimd", lambda g: g.dma_start(out=dst, in_=src),
                     writes=["WBr0", "WBr1"], dma=True, tag="A.ldw1")
            src2 = ctx.dram["aW2T"].ap().rearrange("(a p) n -> p a n", p=128)
            dst2 = WB[:, 16384:32768].rearrange("p (a n) -> p a n", a=16)
            prog.add("gpsimd", lambda g: g.dma_start(out=dst2, in_=src2),
                     writes=["WBr2", "WBr3"], dma=True, tag="A.ldw2")

        QWT = lambda: WB[:, 0:8192]
        KWT = lambda: WB[:, 8192:16384]
        VWT = lambda: WB[:, 16384:24576]
        OWT = lambda: WB[:, 24576:32768]
        W1T = lambda: WB[:, 0:16384]
        W2T = lambda: WB[:, 16384:32768]

        for nm, dst, res in [("cGq", GQ, "aGQ"), ("cGk", GK, "aGK")]:
            ctx.dma_in("gpsimd", dst[:, :], ctx.dram[nm].ap(), [res], f"A.{nm}")
        ctx.dma_in("sync", CAUS[:, :], ctx.dram["cCausal"].ap(), ["aCAUS"], "A.caus")
        ctx.dma_in("sync", GPC[:, :], ctx.dram["cGpc"].ap(), ["aGPC"], "A.gpc")
        for nm, dst, res in [("aGn1g", G1, "A.gn1.gsb"), ("aGn1b", B1, "A.gn1.bsb"),
                             ("aGn2g", G2, "A.gn2.gsb"), ("aGn2b", B2, "A.gn2.bsb"),
                             ("aObR", OBR, "aOBR"), ("aB2R", B2R, "aB2R2"), ("aB1F", B1F, "aB1F2")]:
            ctx.dma_in("sync", dst[:, :], ctx.dram[nm].ap(), [res], f"A.{nm}")

        XB = ctx.sb["XB"]

        for b in range(ctx.B):
            nt0 = b * TPB
            load_xb(ctx, b, f"A.{b}")
            load_qkvo()
            groupnorm_coefs(ctx, "XB", 0, G1, B1, AF1[:, :], BF1[:, :], "aAF1", "aBF1",
                            pQ, pK, pG, "apQ", "apK", GSUS, GSQS, GRS, GSM, GSQs,
                            ctx.sb["cOnesCB"], "A.gn1")
            prog.add("vector", lambda v: v.memset(SST[:, :], 0.0), writes=["aSST"], tag="A.sst0")
            for i in range(TPB):
                xi = nt0 + i
                norm_transpose_tile(ctx, "XB", i, lambda a: XNF[:, a * 128:(a + 1) * 128],
                                    "aXNF", AF1, "aAF1", BF1, "aBF1", pTRS, "apTR", f"A.xn{xi}")
                for nm, ps, psres, wfn, wres in [("q", pQ, "apQ", QWT, "WBr0"),
                                                 ("k", pK, "apK", KWT, "WBr1"),
                                                 ("v", pV, "apV", VWT, "WBr2")]:
                    def emP(t, ps=ps, wfn=wfn):
                        for half in range(2):
                            for a in range(DP):
                                last = t.matmul(ps[:, half * 512:(half + 1) * 512],
                                                XNF[:, a * 128:(a + 1) * 128],
                                                wfn()[:, a * 1024 + half * 512: a * 1024 + (half + 1) * 512],
                                                start=(a == 0), stop=(a == DP - 1))
                        return last
                    prog.add("tensor", emP, reads=["aXNF", wres], writes=[psres],
                             tag=f"A.{nm}p{xi}")
                for nm, ps, psres, col, gg, ggres, dst, dres in [
                        ("q", pQ, "apQ", 0, GQ, "aGQ", QTK, "aQTK"),
                        ("k", pK, "apK", 3, GK, "aGK", KTK, "aKTK")]:
                    prog.add("scalar", lambda s, ps=ps, col=col: s.activation(
                        GSQs[:, :], ps[:, :], AF.Square, accum_out=RL[:, col:col + 1]),
                        reads=[psres], writes=["GSQs", "aRL"], tag=f"A.n2{nm}{xi}")
                    prog.add("vector", lambda v, col=col: v.tensor_scalar_max(
                        RL[:, col + 1:col + 2], RL[:, col:col + 1], 1e-24),
                        reads=["aRL"], writes=["aRL"], tag=f"A.mx{nm}{xi}")
                    prog.add("scalar", lambda s, col=col: s.activation(
                        RL[:, col + 1:col + 2], RL[:, col + 1:col + 2], AF.Sqrt),
                        reads=["aRL"], writes=["aRL"], tag=f"A.sq{nm}{xi}")
                    prog.add("vector", lambda v, col=col: v.reciprocal(
                        RL[:, col + 2:col + 3], RL[:, col + 1:col + 2]),
                        reads=["aRL"], writes=["aRL"], tag=f"A.rc{nm}{xi}")
                    prog.add("vector", lambda v, ps=ps, col=col, gg=gg, dst=dst: v.scalar_tensor_tensor(
                        dst[:, :], ps[:, :], RL[:, col + 2:col + 3], gg[:, :], OP.mult, OP.mult),
                        reads=[psres, "aRL", ggres], writes=[dres], tag=f"A.sc{nm}{xi}")
                prog.add("scalar", lambda s: s.copy(VTK[:, :], pV[:, :]),
                         reads=["apV"], writes=["aVTK"], tag=f"A.ve{xi}")
                for srcn, sres, dstn, dres in [("aQTK", "aQTK", QF, "aQF"), ("aKTK", "aKTK", KF, "aKF")]:
                    srcb = ctx.sb[srcn]
                    for a in range(DP):
                        pa = a % 2
                        ph_ = pTRS[pa][:, 0:128]
                        prog.add("tensor", lambda t, a=a, srcb=srcb, ph_=ph_: t.transpose(
                            ph_, srcb[:, a * 128:(a + 1) * 128], ctx.sb["cIdentB"][:, :]),
                            reads=[sres, "cIdentB"], writes=[R("apTR", pa)], tag=f"A.tf{srcn}{xi}{a}")
                        prog.add("vector", lambda v, a=a, dstn=dstn, ph_=ph_: v.tensor_copy(
                            dstn[:, a * 128:(a + 1) * 128], ph_),
                            reads=[R("apTR", pa)], writes=[dres], tag=f"A.te{srcn}{xi}{a}")
                for h in range(NHEAD):
                    j = h // 2
                    ro = (h % 2) * 64
                    sim = pSIM[h % 2]
                    simres = "apQ" if h % 2 == 0 else "apK"
                    prog.add("tensor", lambda t, sim=sim, h=h, j=j, ro=ro: t.matmul(
                        sim[:, :], KF[ro:ro + 64, j * 128:(j + 1) * 128],
                        QF[ro:ro + 64, j * 128:(j + 1) * 128], start=True, stop=True),
                        reads=["aQF", "aKF"], writes=[simres], tag=f"A.sim{xi}.{h}")
                    sb_ = SIMB[h % 2]
                    prog.add("vector", lambda v, sim=sim, sb_=sb_: v.tensor_mul(
                        sb_[:, :], sim[:, :], CAUS[:, :]),
                        reads=[simres, "aCAUS"], writes=[f"aSIMB{h % 2}"], tag=f"A.sm{xi}.{h}")
                    def emY(t, h=h, j=j, ro=ro, sb_=sb_):
                        t.matmul(pYH[ro:ro + 64, :], VTK[:, h * 64:(h + 1) * 64], sb_[:, :],
                                 start=True, stop=False)
                        return t.matmul(pYH[ro:ro + 64, :],
                                        SST[ro:ro + 64, j * 64:(j + 1) * 64],
                                        QF[ro:ro + 64, j * 128:(j + 1) * 128],
                                        start=False, stop=True)
                    prog.add("tensor", emY, reads=["aVTK", f"aSIMB{h % 2}", "aSST", "aQF"],
                             writes=["apV"], tag=f"A.yh{xi}.{h}")
                    prog.add("tensor", lambda t, h=h, ro=ro: t.matmul(
                        pM[ro:ro + 64, :], KTK[:, h * 64:(h + 1) * 64],
                        VTK[:, h * 64:(h + 1) * 64], start=True, stop=True),
                        reads=["aKTK", "aVTK"], writes=["apV"], tag=f"A.m{xi}.{h}")
                    if h % 2 == 1:
                        prog.add("scalar", lambda s, j=j: s.copy(
                            YF[:, j * 128:(j + 1) * 128], pYH[:, :]),
                            reads=["apV"], writes=["aYF"], tag=f"A.ye{xi}.{j}")
                        prog.add("vector", lambda v, j=j: v.tensor_add(
                            SST[:, j * 64:(j + 1) * 64], SST[:, j * 64:(j + 1) * 64], pM[:, :]),
                            reads=["apV", "aSST"], writes=["aSST"], tag=f"A.su{xi}.{j}")
                        prog.add("vector", lambda v, j=j: v.tensor_scalar(
                            SST[:, j * 64:(j + 1) * 64], SST[:, j * 64:(j + 1) * 64],
                            GPC[:, j:j + 1], None, OP.mult),
                            reads=["aSST", "aGPC"], writes=["aSST"], tag=f"A.sg{xi}.{j}")
                def emO(t):
                    for half in range(2):
                        for a in range(DP):
                            t.matmul(pQ[:, half * 512:(half + 1) * 512],
                                     YF[:, a * 128:(a + 1) * 128],
                                     OWT()[:, a * 1024 + half * 512: a * 1024 + (half + 1) * 512],
                                     start=(a == 0), stop=False)
                        last = t.matmul(pQ[:, half * 512:(half + 1) * 512],
                                        ctx.sb["cOnesF"][0:1, 0:128],
                                        OBR[0:1, half * 512:(half + 1) * 512],
                                        start=False, stop=True)
                    return last
                prog.add("tensor", emO, reads=["aYF", "WBr3", "aOBR", "cOnesF"],
                         writes=["apQ"], tag=f"A.op{xi}")
                prog.add("vector", lambda v, i=i, xi=xi: v.tensor_add(
                    XMB[:, i * D:(i + 1) * D], pQ[:, :], XB[:, i * D:(i + 1) * D]),
                    reads=["apQ", R("XB", i)], writes=[R("aXMB", i)], tag=f"A.xm{xi}")
            load_w12()
            groupnorm_coefs(ctx, "aXMB", 0, G2, B2, AF2[:, :], BF2[:, :], "aAF2", "aBF2",
                            pQ, pK, pG, "apQ", "apK", GSUS, GSQS, GRS, GSM, GSQs,
                            ctx.sb["cOnesCB"], "A.gn2")
            for sl in range(TPB // 4):
                for jj in range(4):
                    i = sl * 4 + jj
                    norm_transpose_tile(ctx, "aXMB", i,
                                        lambda a, jj=jj: XN2F[:, a * 512 + jj * 128: a * 512 + (jj + 1) * 128],
                                        "aXN2F", AF2, "aAF2", BF2, "aBF2", pTRS, "apTR",
                                        f"A.x2{b}.{i}")
                for oc in range(16):
                    ph = pH1[oc % 2]
                    def emH(t, oc=oc, ph=ph):
                        for a in range(DP):
                            last = t.matmul(ph[:, :],
                                            W1T()[:, a * 2048 + oc * 128: a * 2048 + (oc + 1) * 128],
                                            XN2F[:, a * 512:(a + 1) * 512],
                                            start=(a == 0), stop=(a == DP - 1))
                        return last
                    prog.add("tensor", emH, reads=["aXN2F", "WBr0", "WBr1"],
                             writes=["apQ" if oc % 2 == 0 else "apK"],
                             tag=f"A.h1{b}.{sl}.{oc}")
                    prog.add("scalar", lambda s, oc=oc, ph=ph: s.activation(
                        H1F[:, oc * 512:(oc + 1) * 512], ph[:, :], AF.Silu,
                        bias=B1F[:, oc:oc + 1], scale=1.0),
                        reads=["apQ" if oc % 2 == 0 else "apK", "aB1F2"],
                        writes=[R("aH1F", oc)], tag=f"A.si{b}.{sl}.{oc}")
                for jj in range(4):
                    i = sl * 4 + jj
                    xi = nt0 + i
                    def emF(t, jj=jj):
                        for half in range(2):
                            for oc in range(16):
                                t.matmul(pV[:, half * 512:(half + 1) * 512],
                                         H1F[:, oc * 512 + jj * 128: oc * 512 + (jj + 1) * 128],
                                         W2T()[:, oc * 1024 + half * 512: oc * 1024 + (half + 1) * 512],
                                         start=(oc == 0), stop=False)
                            last = t.matmul(pV[:, half * 512:(half + 1) * 512],
                                            ctx.sb["cOnesF"][0:1, 0:128],
                                            B2R[0:1, half * 512:(half + 1) * 512],
                                            start=False, stop=True)
                        return last
                    prog.add("tensor", emF,
                             reads=[R("aH1F", oc) for oc in range(16)] + ["WBr2", "WBr3", "aB2R2", "cOnesF"],
                             writes=["apV"], tag=f"A.w2{b}.{sl}.{jj}")
                    prog.add("vector", lambda v, i=i, xi=xi: v.tensor_add(
                        ctx.sb["LNZ"][:, :], pV[:, :], XB[:, i * D:(i + 1) * D]),
                        reads=["apV", R("XB", i)], writes=["LNZ"], tag=f"A.z1{xi}")
                    prog.add("vector", lambda v, i=i: v.tensor_add(
                        ctx.sb["LNZ"][:, :], ctx.sb["LNZ"][:, :], XMB[:, i * D:(i + 1) * D]),
                        reads=["LNZ", R("aXMB", i)], writes=["LNZ"], tag=f"A.z2{xi}")
                    emit_ln_out(ctx, xi, is_last, f"A.ln{xi}")


    ctx.cur_es = None

def build_stage_B(ctx, is_last):
    nc, prog, cfg = ctx.nc, ctx.prog, ctx.cfg
    es = contextlib.ExitStack()
    TPB = ctx.TPB
    T = ctx.T
    K = T // 4
    NRS = K // 128
    RG = max(1, K // 256)
    RPG = K // RG
    NSG = RPG // 128
    NBIS = 36
    ctx.cur_es = es
    with es:
        GWTs = ctx.sbuf("bGWTs", [128, DP * 8], BF16)
        XF1 = ctx.sbuf("bXF1", [128, DP * 128], BF16)
        EXPT = ctx.sbuf("bEXPT", [128, 16], F32)
        RLB = ctx.sbuf("bRLB", [128, 8], F32)
        AFFT = ctx.sbuf("bAFFT", [128, ctx.B * TPB * 8], F32)
        AFFE = ctx.sbuf("bAFFE", [64, T], F32)
        THL = ctx.sbuf("bTHL", [64, 8], F32)
        THR = ctx.sbuf("bTHR", [1, 16], F32)
        THBS = ctx.sbuf("bTHBS", [128, 8], F32)
        MSKS = ctx.sbuf("bMSKS", [128, 8], F32)
        RSUM = ctx.sbuf("bRSUM", [1, 8], F32)
        RKT = ctx.sbuf("bRKT", [128, ctx.B * TPB * 8], F32)
        IDENTF = ctx.sbuf("bIDF", [128, 128], F32)
        LTRIS = ctx.sbuf("bLTRI", [128, 128], F32)
        IOTA512 = ctx.sbuf("bIOTA", [128, K], F32)
        PTS = ctx.sbuf("bPTS", [128, TPB * 256], BF16)
        PTB = ctx.sbuf("bPTB", [128, K], BF16)
        PSL = ctx.sbuf("bPSL", [128, NRS * 128], BF16)
        XST = ctx.sbuf("bXST", [128, DP * 512], BF16)
        SIL = ctx.sbuf("bSIL", [128, K], BF16)
        HT = ctx.sbuf("bHT", [128, 32 * 512], BF16)
        YB = ctx.sbuf("bYB", [128, 8 * NRS * 1024], BF16)
        WG = ctx.sbuf("bWG", [128, 2 * DP * 512], BF16)
        WU = ctx.sbuf("bWU", [128, 2 * DP * 512], BF16)
        WD = ctx.sbuf("bWD", [128, 2 * 512], BF16)
        pTR = [ctx.psum("bpTR0", [128, 128], BF16), ctx.psum("bpTR1", [128, 128], BF16)]
        bpA = ctx.psum("bpA", [128, 512])
        bpB = ctx.psum("bpB", [128, 512])
        Y4 = [ctx.psum(f"bY{c}", [128, 512]) for c in "abcd"]
        Y4r = [f"bY{c}" for c in "abcd"]

        XB = ctx.sb["XB"]
        gw_src = ctx.dram["bGwT"].ap().rearrange("(a p) n -> p a n", p=128)
        gw_dst = GWTs[:, :].rearrange("p (a n) -> p a n", a=DP)
        prog.add("gpsimd", lambda g, d=gw_dst, s=gw_src: g.dma_start(out=d, in_=s),
            writes=["bGWTs"], dma=True, tag="B.ldgw")
        ctx.dma_in("sync", IDENTF[:, :], ctx.dram["cIdent"].ap(), ["bIDF"], "B.idf")
        ctx.dma_in("sync", LTRIS[:, :], ctx.dram["cLtri"].ap(), ["bLTRI"], "B.ltri")
        prog.add("gpsimd", lambda g: g.iota(IOTA512[:, :], [[1, K]], base=1,
                                            channel_multiplier=0,
                                            allow_small_or_imprecise_dtypes=True),
                 writes=["bIOTA"], tag="B.iota")
        prog.add("vector", lambda v: v.memset(AFFE[:, :], 0.5), writes=["bAFFE"], tag="B.ae0")

        for b in range(ctx.B):
            load_xb(ctx, b, f"Br.{b}")
            for i in range(TPB):
                xi = b * TPB + i
                ci = (b * TPB + i) * 8
                for a in range(DP):
                    transpose_tile(ctx, XF1[:, a * 128:(a + 1) * 128],
                                   XB[:, i * D + a * 128: i * D + (a + 1) * 128],
                                   pTR, ctx.sb["cIdentB"][:, :],
                                   reads=[R("XB", i)], writes=["bXF1"], pres="bpTR",
                                   tag=f"B.xf{xi}.{a}", slot=a % 2)
                def emL(t):
                    for a in range(DP):
                        last = t.matmul(bpA[:, 0:8], XF1[:, a * 128:(a + 1) * 128],
                                        GWTs[:, a * 8:(a + 1) * 8],
                                        start=(a == 0), stop=(a == DP - 1))
                    return last
                prog.add("tensor", emL, reads=["bXF1", "bGWTs"], writes=["bbpA"],
                         tag=f"B.lg{xi}")
                pt = xi % 2
                ex = EXPT[:, pt * 8:(pt + 1) * 8]
                r0, r1, r2 = (RLB[:, pt * 4 + c:pt * 4 + c + 1] for c in (0, 1, 2))
                prog.add("vector", lambda v, ex=ex: v.tensor_copy(ex, bpA[:, 0:8]),
                         reads=["bbpA"], writes=[R("bEXPT", pt)], tag=f"B.sm0{xi}")
                prog.add("vector", lambda v, ex=ex, r0=r0: v.reduce_max(r0, ex,
                                                          axis=AX.X, negate=True),
                         reads=[R("bEXPT", pt)], writes=[R("bRLB", pt)], tag=f"B.sm1{xi}")
                prog.add("scalar", lambda s, ex=ex, r0=r0: s.activation(ex, ex, AF.Exp,
                                                          bias=r0),
                         reads=[R("bEXPT", pt), R("bRLB", pt)], writes=[R("bEXPT", pt)], tag=f"B.sm2{xi}")
                prog.add("vector", lambda v, ex=ex, r1=r1: v.reduce_sum(r1, ex, axis=AX.X),
                         reads=[R("bEXPT", pt)], writes=[R("bRLB", pt)], tag=f"B.sm3{xi}")
                prog.add("vector", lambda v, r1=r1, r2=r2: v.reciprocal(r2, r1),
                         reads=[R("bRLB", pt)], writes=[R("bRLB", pt)], tag=f"B.sm4{xi}")
                prog.add("vector", lambda v, ci=ci, ex=ex, r2=r2: v.tensor_scalar(
                    AFFT[:, ci:ci + 8], ex, r2, None, OP.mult),
                    reads=[R("bEXPT", pt), R("bRLB", pt)], writes=["bAFFT"], tag=f"B.sm5{xi}")
                prog.add("tensor", lambda t, ci=ci: t.transpose(
                    bpB[0:8, 0:128], AFFT[:, ci:ci + 8], IDENTF[:, :]),
                    reads=["bAFFT", "bIDF"], writes=["bbpB"], tag=f"B.tr{xi}")
                prog.add("vector", lambda v, b=b, i=i: v.tensor_copy(
                    AFFE[b * 32:b * 32 + 8, i * 128:(i + 1) * 128], bpB[0:8, 0:128]),
                    reads=["bbpB"], writes=["bAFFE"], tag=f"B.ae{xi}")
        prog.add("vector", lambda v: v.memset(THL[:, 0:1], 0.0), writes=["bTHL"], tag="B.th0")
        prog.add("vector", lambda v: v.memset(THL[:, 1:2], 1.0),
                 reads=["bTHL"], writes=["bTHL"], tag="B.th1")
        BSS = PTS[0:64, 0:T]
        for it in range(NBIS):
            prog.add("vector", lambda v: v.tensor_add(THL[:, 2:3], THL[:, 0:1], THL[:, 1:2]),
                     reads=["bTHL"], writes=["bTHL"], tag=f"B.b0.{it}")
            prog.add("vector", lambda v: v.tensor_scalar(THL[:, 2:3], THL[:, 2:3], 0.5, None, OP.mult),
                     reads=["bTHL"], writes=["bTHL"], tag=f"B.b1.{it}")
            prog.add("vector", lambda v: v.tensor_scalar(
                BSS, AFFE[:, :], THL[:, 2:3], None, OP.is_ge, OP.add, accum_out=THL[:, 3:4]),
                reads=["bAFFE", "bTHL"], writes=["bPTS", "bTHL"], tag=f"B.b2.{it}")
            prog.add("vector", lambda v: v.tensor_single_scalar(
                THL[:, 4:5], THL[:, 3:4], float(K), OP.is_ge),
                reads=["bTHL"], writes=["bTHL"], tag=f"B.b3.{it}")
            prog.add("vector", lambda v: v.tensor_sub(THL[:, 5:6], THL[:, 2:3], THL[:, 0:1]),
                     reads=["bTHL"], writes=["bTHL"], tag=f"B.b4.{it}")
            prog.add("vector", lambda v: v.scalar_tensor_tensor(
                THL[:, 0:1], THL[:, 5:6], THL[:, 4:5], THL[:, 0:1], OP.mult, OP.add),
                reads=["bTHL"], writes=["bTHL"], tag=f"B.b5.{it}")
            prog.add("vector", lambda v: v.tensor_sub(THL[:, 6:7], THL[:, 1:2], THL[:, 2:3]),
                     reads=["bTHL"], writes=["bTHL"], tag=f"B.b6.{it}")
            prog.add("vector", lambda v: v.scalar_tensor_tensor(
                THL[:, 1:2], THL[:, 6:7], THL[:, 4:5], THL[:, 2:3], OP.mult, OP.add),
                reads=["bTHL"], writes=["bTHL"], tag=f"B.b7.{it}")
        for b in range(ctx.B):
            prog.add("tensor", lambda t, b=b: t.transpose(
                bpB[0:1, 0:8], THL[b * 32:b * 32 + 8, 0:1], IDENTF[b * 32:b * 32 + 8, b * 32:b * 32 + 8]),
                reads=["bTHL", "bIDF"], writes=["bbpB"], tag=f"B.tt{b}")
            prog.add("vector", lambda v, b=b: v.tensor_copy(THR[0:1, b * 8:(b + 1) * 8],
                                                            bpB[0:1, 0:8]),
                     reads=["bbpB"], writes=["bTHR"], tag=f"B.te{b}")
        for b in range(ctx.B):
            prog.add("tensor", lambda t, b=b: t.matmul(
                bpA[:, 0:8], ctx.sb["cOnesF"][0:1, 0:128], THR[0:1, b * 8:(b + 1) * 8],
                start=True, stop=True),
                reads=["bTHR", "cOnesF"], writes=["bbpA"], tag=f"B.tb{b}")
            prog.add("vector", lambda v: v.tensor_copy(THBS[:, :], bpA[:, 0:8]),
                     reads=["bbpA"], writes=["bTHBS"], tag=f"B.tb2{b}")
            prog.add("vector", lambda v: v.memset(RSUM[:, :], 0.0),
                     reads=["bRSUM"], writes=["bRSUM"], tag=f"B.rs0{b}")
            for i in range(TPB):
                ci = (b * TPB + i) * 8
                prog.add("vector", lambda v, ci=ci: v.tensor_tensor(
                    MSKS[:, :], AFFT[:, ci:ci + 8], THBS[:, :], OP.is_ge),
                    reads=["bAFFT", "bTHBS"], writes=["bMSKS"], tag=f"B.mk{b}.{i}")
                def emRk(t):
                    t.matmul(bpA[:, 0:8], LTRIS[:, :], MSKS[:, :], start=True, stop=False)
                    return t.matmul(bpA[:, 0:8], ctx.sb["cOnesF"][0:1, 0:128],
                                    RSUM[0:1, 0:8], start=False, stop=True)
                prog.add("tensor", emRk, reads=["bMSKS", "bRSUM", "bLTRI", "cOnesF"],
                         writes=["bbpA"], tag=f"B.rk{b}.{i}")
                prog.add("vector", lambda v, ci=ci: v.tensor_mul(
                    RKT[:, ci:ci + 8], bpA[:, 0:8], MSKS[:, :]),
                    reads=["bbpA", "bMSKS"], writes=["bRKT"], tag=f"B.rm{b}.{i}")
                prog.add("tensor", lambda t: t.matmul(
                    bpB[0:1, 0:8], ctx.sb["cOnesF"][:, 0:1], MSKS[:, :], start=True, stop=True),
                    reads=["bMSKS", "cOnesF"], writes=["bbpB"], tag=f"B.cs{b}.{i}")
                prog.add("vector", lambda v: v.tensor_add(RSUM[0:1, 0:8], RSUM[0:1, 0:8],
                                                          bpB[0:1, 0:8]),
                         reads=["bbpB", "bRSUM"], writes=["bRSUM"], tag=f"B.ru{b}.{i}")
        for b in range(ctx.B):
            load_xb(ctx, b, f"Be.{b}")
            for e in range(8):
                for rg in range(RG):
                    for i in range(TPB):
                        ci = (b * TPB + i) * 8 + e
                        prog.add("vector", lambda v, i=i, ci=ci, rg=rg: v.tensor_scalar(
                            PTS[:, i * RPG:(i + 1) * RPG], IOTA512[:, rg * RPG:(rg + 1) * RPG],
                            RKT[:, ci:ci + 1], None, OP.is_equal),
                            reads=["bIOTA", "bRKT"], writes=[R("bPTS", i), "bPTS"],
                            tag=f"B.pt{b}.{e}.{rg}.{i}")
                    g4 = [bpA[:, 0:RPG], bpB[:, 0:RPG], Y4[0][:, 0:RPG], Y4[1][:, 0:RPG]]
                    g4res = ["bbpA", "bbpB", Y4r[0], Y4r[1]]
                    for pss in range(2):
                        for i in range(TPB):
                            def emG(t, i=i, pss=pss):
                                for dc in range(4):
                                    last = t.matmul(g4[dc],
                                                    XB[:, i * D + (pss * 4 + dc) * 128: i * D + (pss * 4 + dc + 1) * 128],
                                                    PTS[:, i * RPG:(i + 1) * RPG],
                                                    start=(i == 0), stop=(i == TPB - 1))
                                return last
                            prog.add("tensor", emG,
                                     reads=[R("XB", i), R("bPTS", i)],
                                     writes=list(set(g4res)), tag=f"B.g{pss}{b}.{e}.{rg}.{i}")
                        for dc in range(4):
                            prog.add("scalar", lambda s, dc=dc, pss=pss, rg=rg: s.copy(
                                XST[:, (pss * 4 + dc) * 512 + rg * 256: (pss * 4 + dc) * 512 + rg * 256 + RPG],
                                g4[dc]),
                                reads=[g4res[dc]], writes=["bXST"], tag=f"B.ge{pss}{b}.{e}.{rg}.{dc}")
                for ic in range(8):
                    slot = ic % 2
                    for wnm, wsb, wres in [("bWgT", WG, "bWG"), ("bWuT", WU, "bWU")]:
                        srcw = ctx.dram[wnm].ap()[e, :, ic * 512:(ic + 1) * 512].rearrange(
                            "(a p) n -> p a n", p=128)
                        dstw = wsb[:, slot * 4096:(slot + 1) * 4096].rearrange(
                            "p (a n) -> p a n", a=DP)
                        prog.add("gpsimd", lambda g, dst=dstw, src=srcw: g.dma_start(out=dst, in_=src),
                                 writes=[R(wres, slot)], dma=True, tag=f"B.lw{wnm}{b}.{e}.{ic}")
                    for isub in range(4):
                        def emGm(t, ic=ic, isub=isub, slot=slot):
                            for a in range(DP):
                                last = t.matmul(bpA[:, 0:K],
                                                WG[:, slot * 4096 + a * 512 + isub * 128: slot * 4096 + a * 512 + (isub + 1) * 128],
                                                XST[:, a * 512: a * 512 + K],
                                                start=(a == 0), stop=(a == DP - 1))
                            return last
                        prog.add("tensor", emGm, reads=[R("bWG", slot), "bXST"],
                                 writes=["bbpA"], tag=f"B.hg{b}.{e}.{ic}.{isub}")
                        prog.add("scalar", lambda s: s.activation(SIL[:, 0:K], bpA[:, 0:K], AF.Silu),
                                 reads=["bbpA"], writes=["bSIL"], tag=f"B.hs{b}.{e}.{ic}.{isub}")
                        def emUm(t, ic=ic, isub=isub, slot=slot):
                            for a in range(DP):
                                last = t.matmul(bpB[:, 0:K],
                                                WU[:, slot * 4096 + a * 512 + isub * 128: slot * 4096 + a * 512 + (isub + 1) * 128],
                                                XST[:, a * 512: a * 512 + K],
                                                start=(a == 0), stop=(a == DP - 1))
                            return last
                        prog.add("tensor", emUm, reads=[R("bWU", slot), "bXST"],
                                 writes=["bbpB"], tag=f"B.hu{b}.{e}.{ic}.{isub}")
                        hidx = ic * 4 + isub
                        prog.add("vector", lambda v, hidx=hidx: v.tensor_mul(
                            HT[:, hidx * 512: hidx * 512 + K],
                            SIL[:, 0:K], bpB[:, 0:K]),
                            reads=["bSIL", "bbpB"], writes=[R("bHT", hidx)],
                            tag=f"B.hm{b}.{e}.{ic}.{isub}")
                for dh in range(2):
                    for it in range(32):
                        slot = it % 2
                        srcw = ctx.dram["bWdT"].ap()[e, it * 128:(it + 1) * 128, dh * 512:(dh + 1) * 512]
                        prog.add("gpsimd", lambda g, slot=slot, src=srcw: g.dma_start(
                            out=WD[:, slot * 512:(slot + 1) * 512], in_=src),
                            writes=[R("bWD", slot)], dma=True, tag=f"B.lwd{b}.{e}.{dh}.{it}")
                        def emD(t, it=it, slot=slot):
                            for rsub in range(NRS):
                                last = t.matmul(Y4[rsub][:, 0:512],
                                                HT[:, it * 512 + rsub * 128: it * 512 + rsub * 128 + 128],
                                                WD[:, slot * 512:(slot + 1) * 512],
                                                start=(it == 0), stop=(it == 31))
                            return last
                        prog.add("tensor", emD,
                                 reads=[R("bHT", it), R("bWD", slot)],
                                 writes=[Y4r[rsub] for rsub in range(NRS)],
                                 tag=f"B.dn{b}.{e}.{dh}.{it}")
                    for rsub in range(NRS):
                        prog.add("vector", lambda v, rsub=rsub, dh=dh, e=e: v.tensor_copy(
                            YB[:, (e * NRS + rsub) * 1024 + dh * 512:(e * NRS + rsub) * 1024 + dh * 512 + 512],
                            Y4[rsub][:, 0:512]),
                            reads=[Y4r[rsub]],
                            writes=[R("bYB", e * NRS + rsub)], tag=f"B.ye{b}.{e}.{dh}.{rsub}")
            for i in range(TPB):
                xi = b * TPB + i
                for e in range(8):
                    ci = (b * TPB + i) * 8 + e
                    prog.add("vector", lambda v, ci=ci: v.tensor_scalar(
                        PTB[:, :], IOTA512[:, :], RKT[:, ci:ci + 1], AFFT[:, ci:ci + 1],
                        OP.is_equal, OP.mult),
                        reads=["bIOTA", "bRKT", "bAFFT"], writes=["bPTB"], tag=f"B.sp{b}.{i}.{e}")
                    for rsub in range(NRS):
                        pa = rsub % 2
                        ph_ = pTR[pa][:, 0:128]
                        prog.add("tensor", lambda t, rsub=rsub, ph_=ph_: t.transpose(
                            ph_, PTB[:, rsub * 128:(rsub + 1) * 128], ctx.sb["cIdentB"][:, :]),
                            reads=["bPTB", "cIdentB"], writes=[R("bpTR", pa)], tag=f"B.st{b}.{i}.{e}.{rsub}")
                        prog.add("vector", lambda v, rsub=rsub, ph_=ph_: v.tensor_copy(
                            PSL[:, rsub * 128:(rsub + 1) * 128], ph_),
                            reads=[R("bpTR", pa)], writes=["bPSL"], tag=f"B.se{b}.{i}.{e}.{rsub}")
                    def emSc(t, e=e):
                        for rsub in range(NRS):
                            for dh in range(2):
                                last = t.matmul(Y4[dh][:, 0:512],
                                                PSL[:, rsub * 128:(rsub + 1) * 128],
                                                YB[:, (e * NRS + rsub) * 1024 + dh * 512: (e * NRS + rsub) * 1024 + dh * 512 + 512],
                                                start=(e == 0 and rsub == 0), stop=(e == 7 and rsub == NRS - 1))
                        return last
                    prog.add("tensor", emSc,
                             reads=["bPSL"] + [R("bYB", e * NRS + rsub) for rsub in range(NRS)],
                             writes=[Y4r[0], Y4r[1]], tag=f"B.sc{b}.{i}.{e}")
                for dh in range(2):
                    prog.add("vector", lambda v, i=i, dh=dh: v.tensor_add(
                        ctx.sb["LNZ"][:, dh * 512:(dh + 1) * 512], Y4[dh][:, 0:512],
                        XB[:, i * D + dh * 512: i * D + dh * 512 + 512]),
                        reads=[Y4r[dh], R("XB", i)], writes=["LNZ"], tag=f"B.z{xi}.{dh}")
                emit_ln_out(ctx, xi, is_last, f"B.ln{xi}")

    ctx.cur_es = None

def build_stage_C(ctx, is_last):
    nc, prog, cfg = ctx.nc, ctx.prog, ctx.cfg
    es = contextlib.ExitStack()
    TPB = ctx.TPB
    NTOK = ctx.B * ctx.T
    NTOT = cfg["n_cores"] * NTOK if cfg["collective"] else NTOK
    NSL = TPB // 4
    ctx.cur_es = es
    with es:
        WC = ctx.sbuf("cWC", [128, 4 * 8192], BF16)
        XF = ctx.sbuf("cXF", [128, DP * 512], BF16)
        HB = ctx.sbuf("cHB", [128, 2 * 512], BF16)
        CSQ = ctx.sbuf("cCSQ", [128, 2 * 512], F32)
        SSTAT = ctx.sbuf("cSSTAT", [128, 48], F32)
        GSTAT = ctx.sbuf("cGSTAT", [128, 48], F32)
        RT = ctx.sbuf("cRT", [128, 4], F32)
        CF = ctx.sbuf("cCF", [128, 48], F32)
        A3 = ctx.sbuf("cA3", [128, 24], F32)
        B3 = ctx.sbuf("cB3", [128, 24], F32)
        VF = ctx.sbuf("cVF", [128, DP * 512], BF16)
        QBF = ctx.sbuf("cQBF", [128, DP * 512], BF16)
        KBF = ctx.sbuf("cKBF", [128, DP * 512], BF16)
        QKF = ctx.sbuf("cQKF", [128, DP * 512], BF16)
        OUTF = ctx.sbuf("cOUTF", [128, DP * 512], BF16)
        VTOK = ctx.sbuf("cVTOK", [128, TPB * D], BF16)
        VVS = ctx.sbuf("cVVS", [128, 8 * DH], BF16)
        BPR = ctx.sbuf("cBPR", [1, D], F32)
        pTR = [ctx.psum("cpTR0", [128, 128], BF16), ctx.psum("cpTR1", [128, 128], BF16)]
        pB0 = ctx.psum("cpB0", [128, 512])
        pB1 = ctx.psum("cpB1", [128, 512])
        pO = ctx.psum("cpO", [128, D])
        hq = nc.dram_tensor("c_hq", [D, NTOK], BF16)
        hk = nc.dram_tensor("c_hk", [D, NTOK], BF16)
        hv = nc.dram_tensor("c_hv", [D, NTOK], BF16)
        ccin = nc.dram_tensor("c_ccin", [128, 48], F32)
        ccout = nc.dram_tensor("c_ccout", [128, 48], F32)

        for r, nm in enumerate(["cWqT", "cWkT", "cWvT", "cWpT"]):
            src = ctx.dram[nm].ap().rearrange("(a p) n -> p a n", p=128)
            dst = WC[:, r * 8192:(r + 1) * 8192].rearrange("p (a n) -> p a n", a=DP)
            prog.add("gpsimd", lambda g, dst=dst, src=src: g.dma_start(out=dst, in_=src),
                     writes=[f"cWC{r}"], dma=True, tag=f"C.ld{nm}")
        PAR = {}
        for nm in ["cGq_g", "cGq_b", "cGk_g", "cGk_b", "cGv_g", "cGv_b"]:
            t = ctx.sbuf(nm + "s", [128, DP], F32)
            ctx.dma_in("sync", t[:, :], ctx.dram[nm].ap(), [nm + "s"], f"C.{nm}")
            PAR[nm] = t
        ctx.dma_in("sync", BPR[:, :], ctx.dram["cBpR"].ap(), ["cBPR"], "C.bp")
        prog.add("vector", lambda v: v.memset(SSTAT[:, :], 0.0), writes=["cSSTAT"], tag="C.st0")

        XB = ctx.sb["XB"]
        branches = [("q", hq, 0, 0), ("k", hk, 1, 1), ("v", hv, 2, 2)]
        for b in range(ctx.B):
            load_xb(ctx, b, f"C1.{b}")
            for s in range(NSL):
                for j in range(4):
                    i_loc = s * 4 + j
                    xi = b * TPB + i_loc
                    for a in range(DP):
                        transpose_tile(
                            ctx, XF[:, a * 512 + j * 128: a * 512 + (j + 1) * 128],
                            XB[:, i_loc * D + a * 128: i_loc * D + (a + 1) * 128],
                            pTR, ctx.sb["cIdentB"][:, :],
                            reads=[R("XB", i_loc)], writes=[R("cXF", a)], pres="cpTR",
                            tag=f"C.xf{xi}.{a}", slot=a % 2)
                for bi, (bn, hd, wr, sc0) in enumerate(branches):
                    for oc in range(DP):
                        pb = pB0 if oc % 2 == 0 else pB1
                        pres = "cpB0" if oc % 2 == 0 else "cpB1"
                        def emH(t, wr=wr, oc=oc, pb=pb):
                            for a in range(DP):
                                last = t.matmul(pb[:, :],
                                                WC[:, wr * 8192 + a * 1024 + oc * 128: wr * 8192 + a * 1024 + (oc + 1) * 128],
                                                XF[:, a * 512:(a + 1) * 512],
                                                start=(a == 0), stop=(a == DP - 1))
                            return last
                        prog.add("tensor", emH,
                                 reads=[R("cXF", a) for a in range(DP)] + [f"cWC{wr}"],
                                 writes=[pres], tag=f"C.h{bn}{b}.{s}.{oc}")
                        par = oc % 2
                        hb = HB[:, par * 512:(par + 1) * 512]
                        csq = CSQ[:, par * 512:(par + 1) * 512]
                        prog.add("scalar", lambda sc, pb=pb, hb=hb: sc.copy(hb, pb[:, :]),
                                 reads=[pres], writes=[R("cHB", par)], tag=f"C.he{bn}{b}.{s}.{oc}")
                        dst = hd.ap()[oc * 128:(oc + 1) * 128, b * ctx.T + s * 512: b * ctx.T + (s + 1) * 512]
                        ctx.dma_out("sync", dst, hb, reads=[R("cHB", par)],
                                    writes=[R(f"c_h{bn}", (b, s, oc))], tag=f"C.hs{bn}{b}.{s}.{oc}")
                        sumc = SSTAT[:, bi * 16 + oc: bi * 16 + oc + 1]
                        sqc = SSTAT[:, bi * 16 + 8 + oc: bi * 16 + 8 + oc + 1]
                        prog.add("scalar", lambda sc, pb=pb, csq=csq, par=par: sc.activation(
                                 csq, pb[:, :], AF.Identity, accum_out=RT[:, 2 * par:2 * par + 1]),
                                 reads=[pres], writes=[R("cCSQ", par), R("cRT", par)], tag=f"C.r1{bn}{b}.{s}.{oc}")
                        prog.add("vector", lambda v, sumc=sumc, par=par: v.tensor_add(
                                 sumc, sumc, RT[:, 2 * par:2 * par + 1]),
                                 reads=[R("cRT", par), "cSSTAT"], writes=["cSSTAT"], tag=f"C.r2{bn}{b}.{s}.{oc}")
                        prog.add("scalar", lambda sc, pb=pb, csq=csq, par=par: sc.activation(
                                 csq, pb[:, :], AF.Square, accum_out=RT[:, 2 * par + 1:2 * par + 2]),
                                 reads=[pres, R("cCSQ", par)], writes=[R("cCSQ", par), R("cRT", par)], tag=f"C.r3{bn}{b}.{s}.{oc}")
                        prog.add("vector", lambda v, sqc=sqc, par=par: v.tensor_add(
                                 sqc, sqc, RT[:, 2 * par + 1:2 * par + 2]),
                                 reads=[R("cRT", par), "cSSTAT"], writes=["cSSTAT"], tag=f"C.r4{bn}{b}.{s}.{oc}")
        if cfg["collective"]:
            ctx.dma_out("gpsimd", ccin.ap(), SSTAT[:, :], reads=["cSSTAT"], writes=["c_ccin"], tag="C.cci")
            prog.add("gpsimd", lambda g: g.collective_compute(
                "AllReduce", OP.add, replica_groups=[list(range(cfg["n_cores"]))],
                ins=[ccin.ap()], outs=[ccout.ap()]),
                reads=["c_ccin"], writes=["c_ccout"], dma="cc", tag="C.cc")
            ctx.dma_out("gpsimd", GSTAT[:, :], ccout.ap(), reads=["c_ccout"],
                        writes=["cGSTAT"], tag="C.cco")
        else:
            prog.add("vector", lambda v: v.tensor_copy(GSTAT[:, :], SSTAT[:, :]),
                     reads=["cSSTAT"], writes=["cGSTAT"], tag="C.cc")
        for bi, (bn, hd, wr, sc0) in enumerate(branches):
            g_sb = PAR[f"cG{bn}_g"]
            b_sb = PAR[f"cG{bn}_b"]
            mu = CF[:, bi * 16:bi * 16 + 8]
            e2 = CF[:, bi * 16 + 8:bi * 16 + 16]
            a3 = A3[:, bi * 8:(bi + 1) * 8]
            b3 = B3[:, bi * 8:(bi + 1) * 8]
            prog.add("vector", lambda v, bi=bi, mu=mu: v.tensor_scalar(
                mu, GSTAT[:, bi * 16:bi * 16 + 8], 1.0 / NTOT, None, OP.mult),
                reads=["cGSTAT"], writes=["cCF"], tag=f"C.c1{bn}")
            prog.add("vector", lambda v, bi=bi, e2=e2: v.tensor_scalar(
                e2, GSTAT[:, bi * 16 + 8:bi * 16 + 16], 1.0 / NTOT, None, OP.mult),
                reads=["cGSTAT"], writes=["cCF"], tag=f"C.c2{bn}")
            prog.add("vector", lambda v, mu=mu, e2=e2, a3=a3: v.tensor_mul(a3, mu, mu),
                     reads=["cCF"], writes=["cA3"], tag=f"C.c3{bn}")
            prog.add("vector", lambda v, e2=e2, a3=a3: v.tensor_sub(a3, e2, a3),
                     reads=["cCF", "cA3"], writes=["cA3"], tag=f"C.c4{bn}")
            prog.add("scalar", lambda s, a3=a3: s.activation(a3, a3, AF.Sqrt,
                                                             bias=ctx.sb["CONSTV"][:, 0:1]),
                     reads=["cA3", "CONSTV"], writes=["cA3"], tag=f"C.c5{bn}")
            prog.add("vector", lambda v, a3=a3: v.reciprocal(a3, a3),
                     reads=["cA3"], writes=["cA3"], tag=f"C.c6{bn}")
            prog.add("vector", lambda v, a3=a3, g_sb=g_sb: v.tensor_mul(a3, a3, g_sb[:, :]),
                     reads=["cA3", f"cG{bn}_gs"], writes=["cA3"], tag=f"C.c7{bn}")
            prog.add("vector", lambda v, b3=b3, mu=mu, a3=a3: v.tensor_mul(b3, mu, a3),
                     reads=["cCF", "cA3"], writes=["cB3"], tag=f"C.c8{bn}")
            prog.add("vector", lambda v, b3=b3, b_sb=b_sb: v.tensor_sub(b3, b_sb[:, :], b3),
                     reads=["cB3", f"cG{bn}_bs"], writes=["cB3"], tag=f"C.c9{bn}")
            prog.add("vector", lambda v, b3=b3: v.tensor_scalar(b3, b3, 4.0, -4.0, OP.mult, OP.add),
                     reads=["cB3"], writes=["cB3"], tag=f"C.cA{bn}")
            prog.add("vector", lambda v, a3=a3: v.tensor_scalar(a3, a3, 4.0, None, OP.mult),
                     reads=["cA3"], writes=["cA3"], tag=f"C.cB{bn}")

        def sig_stream(bn, bi, hd, b, s, dstF, dres):
            src = hd.ap()[:, b * ctx.T + s * 512: b * ctx.T + (s + 1) * 512].rearrange(
                "(a p) n -> p a n", p=128)
            dstH = dstF[:, :].rearrange("p (a n) -> p a n", a=DP)
            prog.add("sync", lambda g: g.dma_start(out=dstH, in_=src),
                     reads=[R(f"c_h{bn}", (b, s, oc)) for oc in range(DP)],
                     writes=[dres], dma=True, tag=f"C.l{bn}{b}.{s}")
            for oc in range(DP):
                prog.add("scalar", lambda sc, oc=oc, bi=bi, dstF=dstF: sc.activation(
                    dstF[:, oc * 512:(oc + 1) * 512], dstF[:, oc * 512:(oc + 1) * 512],
                    AF.Sigmoid, bias=B3[:, bi * 8 + oc:bi * 8 + oc + 1],
                    scale=A3[:, bi * 8 + oc:bi * 8 + oc + 1]),
                    reads=[dres, "cA3", "cB3"], writes=[dres], tag=f"C.sg{bn}{b}.{s}.{oc}")

        for b in range(ctx.B):
            load_xb(ctx, b, f"C2.{b}")
            for s in range(NSL):
                sig_stream("v", 2, hv, b, s, VF, "cVF")
                for j in range(4):
                    for a in range(DP):
                        transpose_tile(
                            ctx, VTOK[:, (s * 4 + j) * D + a * 128: (s * 4 + j) * D + (a + 1) * 128],
                            VF[:, a * 512 + j * 128: a * 512 + (j + 1) * 128],
                            pTR, ctx.sb["cIdentB"][:, :],
                            reads=["cVF"], writes=[R("cVTOK", s * 4 + j)], pres="cpTR",
                            tag=f"C.vt{b}.{s * 4 + j}.{a}", slot=a % 2)
            for j in range(8):
                def emVV(t, j=j):
                    for hh in range(2):
                        h = 2 * j + hh
                        ro = hh * 64
                        for it in range(TPB):
                            last = t.matmul(pB0[ro:ro + 64, 0:64],
                                            VTOK[:, it * D + h * 64: it * D + h * 64 + 64],
                                            VTOK[:, it * D + h * 64: it * D + h * 64 + 64],
                                            start=(it == 0), stop=(it == TPB - 1))
                    return last
                prog.add("tensor", emVV, reads=[R("cVTOK", it) for it in range(TPB)],
                         writes=["cpB0"], tag=f"C.vv{b}.{j}")
                prog.add("vector", lambda v, j=j: v.tensor_scalar(
                    VVS[:, j * 64:(j + 1) * 64], pB0[:, 0:64], 0.125, None, OP.mult),
                    reads=["cpB0"], writes=["cVVS"], tag=f"C.ve{b}.{j}")
            for s in range(NSL):
                sig_stream("q", 0, hq, b, s, QBF, "cQBF")
                sig_stream("k", 1, hk, b, s, KBF, "cKBF")
                for a in range(DP):
                    prog.add("vector", lambda v, a=a: v.tensor_mul(
                        QKF[:, a * 512:(a + 1) * 512], QBF[:, a * 512:(a + 1) * 512],
                        KBF[:, a * 512:(a + 1) * 512]),
                        reads=["cQBF", "cKBF"], writes=["cQKF"], tag=f"C.qk{b}.{s}.{a}")
                for j in range(8):
                    def emO2(t, j=j):
                        for hh in range(2):
                            ro = hh * 64
                            last = t.matmul(pB1[ro:ro + 64, :],
                                            VVS[ro:ro + 64, j * 64:(j + 1) * 64],
                                            QKF[ro:ro + 64, j * 512:(j + 1) * 512],
                                            start=True, stop=True)
                        return last
                    prog.add("tensor", emO2, reads=["cVVS", "cQKF"], writes=["cpB1"],
                             tag=f"C.o{b}.{s}.{j}")
                    prog.add("scalar", lambda sc, j=j: sc.copy(OUTF[:, j * 512:(j + 1) * 512], pB1[:, :]),
                             reads=["cpB1"], writes=["cOUTF"], tag=f"C.oe{b}.{s}.{j}")
                for j in range(4):
                    i_loc = s * 4 + j
                    xi = b * TPB + i_loc
                    def emP(t, j=j):
                        for half in range(2):
                            for a in range(DP):
                                t.matmul(pO[:, half * 512:(half + 1) * 512],
                                         OUTF[:, a * 512 + j * 128: a * 512 + (j + 1) * 128],
                                         WC[:, 3 * 8192 + a * 1024 + half * 512: 3 * 8192 + a * 1024 + (half + 1) * 512],
                                         start=(a == 0), stop=False)
                            last = t.matmul(pO[:, half * 512:(half + 1) * 512],
                                            ctx.sb["cOnesF"][0:1, 0:128],
                                            BPR[0:1, half * 512:(half + 1) * 512],
                                            start=False, stop=True)
                        return last
                    prog.add("tensor", emP, reads=["cOUTF", "cWC3", "cBPR", "cOnesF"],
                             writes=["cpO"], tag=f"C.p{xi}")
                    prog.add("vector", lambda v, xi=xi, i_loc=i_loc: v.tensor_add(
                        ctx.sb["LNZ"][:, :], pO[:, :], XB[:, i_loc * D:(i_loc + 1) * D]),
                        reads=["cpO", R("XB", i_loc)], writes=["LNZ"], tag=f"C.z{xi}")
                    emit_ln_out(ctx, xi, is_last, f"C.ln{xi}")


    ctx.cur_es = None


def kernel(**inputs):
    cfg = dict(CFG)
    n_cores = cfg["n_cores"]
    T, Bpc = cfg["T"], cfg["B_per_core"]
    stage_groups = [cfg["stages"]] if cfg["fused"] else [(s,) for s in cfg["stages"]]

    x = np.asarray(inputs["L0_attn_x"], dtype=np.float32)
    B_total = x.shape[0]
    cur = inputs
    out_parts = None
    for gi, grp in enumerate(stage_groups):
        gcfg = dict(cfg)
        gcfg["stages"] = grp
        in_maps = stage_inputs(cur, grp, T, Bpc, n_cores)
        if gi > 0:
            for c in range(n_cores):
                in_maps[c]["x"] = out_parts[c]
        nc = build_program(gcfg)
        res = bass_utils.run_bass_kernel_spmd(nc, in_maps, core_ids=list(range(n_cores)))
        out_parts = [np.asarray(r["xout"]) for r in res.results]
    out = np.concatenate([p.reshape(Bpc, T, D) for p in out_parts], axis=0)
    return out.astype(np.float32)
